# version 09
# baseline (speedup 1.0000x reference)
import os
import numpy as np
import ml_dtypes
import concourse.bass as bass
import concourse.mybir as mybir
from concourse.bass_utils import run_bass_kernel_spmd

F32 = mybir.dt.float32
BF16 = mybir.dt.bfloat16
AF = mybir.ActivationFunctionType
ALU = mybir.AluOpType
AX = mybir.AxisListType

H, HD, T, C, R = 16, 64, 2048, 1024, 8
NS = 2
NC8 = 8
NT = 16
NQ = 4
SCALE = 0.125
EPS = 1e-5


class Plan:

    ENGINES = ("t", "v", "s", "y", "g")
    NLANES = {"y": 8, "g": 4}

    def __init__(self):
        self.script = {e: [] for e in self.ENGINES}
        self.count = {e: 0 for e in ("t", "v", "s")}
        self.lane_count = {}
        self.lane_rr = {"y": 0, "g": 0}
        self.events = {}
        self.hazard = set()

    def lane_names(self):
        return [f"{e}{i}" for e in ("y", "g") for i in range(self.NLANES[e])]

    def step(self, eng, emit, waits=(), sig=None, hazard=False):
        w = []
        for name in waits:
            if name is None:
                continue
            ev = self.events[name]
            if ev[0] != eng or ev[0] not in ("t", "v", "s") or name in self.hazard:
                w.append(ev)
        if hazard and sig is not None:
            self.hazard.add(sig)
        lane = None
        if eng in ("y", "g"):
            lane = f"{eng}{self.lane_rr[eng] % self.NLANES[eng]}"
            self.lane_rr[eng] += 1
            prev = self.lane_count.get(lane, 0)
            if prev:
                w.append((lane, prev))
            self.lane_count[lane] = prev + 16
            if sig is not None:
                assert sig not in self.events, sig
                self.events[sig] = (lane, prev + 16)
        elif sig is not None:
            self.count[eng] += 1
            assert sig not in self.events, sig
            self.events[sig] = (eng, self.count[eng])
        self.script[eng].append((w, emit, sig is not None or lane is not None, lane))

    def alias(self, new, old):
        self.events[new] = self.events[old]

    def materialize(self, block, sems):
        def mk(ename):
            hiwater = {}

            def body(eng):
                for waits, emit, has_sig, lane in self.script[ename]:
                    for (pe, val) in waits:
                        if hiwater.get(pe, 0) >= val:
                            continue
                        hiwater[pe] = val
                        eng.wait_ge(sems[pe], val)
                    inst = emit(eng)
                    if lane is not None:
                        inst.then_inc(sems[lane], 16)
                    elif has_sig:
                        inst.then_inc(sems[ename], 1)
            return body

        for ename in self.ENGINES:
            getattr(block, {"t": "tensor", "v": "vector", "s": "scalar",
                            "y": "sync", "g": "gpsimd"}[ename])(mk(ename))


def build_program(dbg=(), stop="full"):
    serial = stop.endswith("S")
    stop = stop.rstrip("S")
    lvl = {"pro0": -3, "pro1": -2, "pro2": -1, "pro": 0, "a": 1, "norm": 2,
           "b1": 3, "b2": 35, "b": 4, "cd1": 5, "full": 9}[stop]
    nc = bass.Bass("TRN2", target_bir_lowering=False,
                   detect_race_conditions=False)

    def din(name, shape, dt=F32):
        return nc.dram_tensor(name, shape, dt, kind="ExternalInput").ap()

    xT_d = din("xT", [NS, C, T])
    x_d = din("x_tok", [NS, T, C])
    wq_d, wk_d, wv_d = din("wq_cat", [C, C]), din("wk_cat", [C, C]), din("wv_cat", [C, C])
    aq_d, ak_d, av_d = din("aq_t", [R, H * C]), din("ak_t", [R, H * C]), din("av_t", [R, H * C])
    bq_d, bk_d, bv_d = din("bq_r", [R, C]), din("bk_r", [R, C]), din("bv_r", [R, C])
    wp_d = din("wp", [C, C])
    ap_d = din("ap_t", [R, C])
    bpm_d = din("bp_m", [R, C])
    bpb_d = din("bp_b", [1, C])
    fr_d, fk_d, fv_d = din("fr_t", [C, C]), din("fk_t", [C, C]), din("fv_t", [C, C])
    mur_d, muk_d = din("mu_r_c", [128, 8]), din("mu_k_c", [128, 8])
    cidf_d = din("c_idf", [128, 128])
    cidb_d = din("c_idb", [128, 128], BF16)
    cmask_d = din("c_mask", [128, 128], BF16)
    cones_d = din("c_ones", [128, 128], BF16)
    cones2_d = din("c_ones2", [128, 33], BF16)

    y_d = nc.dram_tensor("y", [NS, T, C], F32, kind="ExternalOutput").ap()

    dbg_d = {}
    for name, shape in (("dbg_qt", [128, NC8 * T]), ("dbg_kt", [128, NC8 * T]),
                        ("dbg_vt", [128, 16 * 1040]), ("dbg_ot", [128, NC8 * T]),
                        ("dbg_x1", [T, C]), ("dbg_bias", [128, 16]),
                        ("dbg_s1", [128, 4096]), ("dbg_stat", [128, 64]),
                        ("dbg_qn", [128, 16]), ("dbg_kn", [128, 16]),
                        ("dbg_x1t", [128, NC8 * 513])):
        if name in dbg:
            dbg_d[name] = nc.dram_tensor(name, shape, F32, kind="ExternalOutput").ap()

    s_weff = nc.dram_tensor("s_wef", [4, C, C], BF16, kind="Internal").ap()
    s_ffn = nc.dram_tensor("s_ffn", [3, C, C], BF16, kind="Internal").ap()

    base, bend = nc.bump_sbuf(211136)
    assert bend <= 229376, (base, bend)

    def at(name, shape, dt, off, live):
        off = off - 16384 + base
        nbytes = int(np.prod(shape)) // shape[0] * (2 if dt == BF16 else 4)
        assert off + nbytes <= bend, (name, off + nbytes, bend)
        return nc.alloc_sbuf_tensor_at(name, shape, dt, offset=off)

    XT = at("XT", [128, NC8 * 1024], BF16, 16384, "A")
    QT = at("QT", [128, NC8 * T], BF16, 32768, "AB")
    KT = at("KT", [128, NC8 * T], BF16, 65536, "AB")
    VT = at("VT", [128, 16 * 1040], BF16, 98304, "AB")
    WV = at("WV", [128, 8192], BF16, 98304, "CD")
    XKQ = at("XKQ", [128, 4096], BF16, 114688, "CD")
    OT = at("OT", [128, NC8 * T], BF16, 131584, "BC")
    PT = at("PT", [128, T], BF16, 164352, "B")
    STG = at("STG", [128, T], BF16, 168448, "B")
    RECF = at("RECF", [128, 512], F32, 168448, "B")
    LNR = at("LNR", [128, 2048], F32, 16384, "B")
    W0 = at("W0", [128, 8192], BF16, 172544, "W")
    W1 = at("W1", [128, 8192], BF16, 188928, "W")
    W2 = at("W2", [128, 8192], BF16, 205312, "W")
    LNIN = at("LNIN", [128, NQ * 1024], F32, 16384, "CD")
    X1Q = at("X1Q", [128, NQ * 1024], F32, 32768, "CD")
    XRQ = at("XRQ", [128, NC8 * 512], BF16, 49152, "CD")
    X1T = at("X1T", [128, NC8 * 513], BF16, 57344, "CD")
    KAQ = at("KAQ", [128, NC8 * 512], BF16, 65568, "CD")
    XIN = at("XIN", [128, 2048], F32, 73760, "CD")
    OUTQ = at("OUTQ", [128, 2048], F32, 81952, "CD")
    THQ = at("THQ", [128, NQ * 1024], BF16, 164352, "CD")
    AT_BF = at("AT_BF", [R, H * C], BF16, 32768, "P")
    BT_BF = at("BT_BF", [R, C], BF16, 65536, "P")
    WRAW = at("WRAW", [128, 8192], BF16, 131584, "P")
    WSTG = at("WSTG", [128, 8192], BF16, 147968, "P")
    o = 221696
    def small(name, shape, dt):
        nonlocal o
        h = at(name, shape, dt, o, "X")
        o += int(np.prod(shape)) // shape[0] * (2 if dt == BF16 else 4)
        o = (o + 31) & ~31
        return h
    CIDF = small("CIDF", [128, 128], F32)
    CIDB = small("CIDB", [128, 128], BF16)
    CMSK = small("CMSK", [128, 128], BF16)
    CONE = small("CONE", [128, 128], BF16)
    CON2 = small("CON2", [128, 33], BF16)
    BPB = small("BPB", [1, 1024], BF16)
    MUR = small("MUR", [128, 8], F32)
    MUK = small("MUK", [128, 8], F32)
    OMR = small("OMR", [128, 8], F32)
    OMK = small("OMK", [128, 8], F32)
    QN2 = small("QN2", [128, 16], BF16)
    KN2 = small("KN2", [128, 16], BF16)
    BIASV = small("BIASV", [128, 16], F32)
    MIXT = small("MIXT", [128, 512], BF16)
    STAT = small("STAT", [128, 64], F32)
    BNDRY = small("BNDRY", [128, 8], BF16)
    EPSV = small("EPSV", [128, 1], F32)
    assert o <= 227520, o

    W = [W0, W1, W2]
    PSA = nc.alloc_psum_tensor("PSA", [128, 2048], F32)
    PSB = nc.alloc_psum_tensor("PSB", [128, 2048], F32)

    VT4 = VT[:, :].rearrange("p (j h e) -> p j h e", j=16, h=16)
    X1T3 = X1T[:, :].rearrange("p (k x) -> p k x", k=8)

    p = Plan()

    def dma(eng_key, dst, src, waits=(), sig=None):
        p.step(eng_key, lambda e, d=dst, s=src: e.dma_start(d, s), waits, sig)

    def rows_ap(d2):
        return d2.rearrange("(k x) c -> x k c", k=NC8)

    dma("y", CIDF[:, :], cidf_d, sig="pr.cid")
    dma("y", CIDB[:, :], cidb_d, sig="pr.cidb")
    dma("y", CMSK[:, :], cmask_d, sig="pr.cmsk")
    dma("y", CONE[:, :], cones_d, sig="pr.cone")
    dma("y", CON2[:, :], cones2_d, sig="pr.con2")
    dma("y", MUR[:, :], mur_d, sig="pr.mur")
    dma("y", MUK[:, :], muk_d, sig="pr.muk")
    dma("g", BPB[:, :], bpb_d, sig="pr.bpb")
    p.step("s", lambda e: e.activation(OMR[:, :], MUR[:, :], AF.Copy, bias=1.0, scale=-1.0),
           waits=("pr.mur",), sig="pr.omr")
    p.step("s", lambda e: e.activation(OMK[:, :], MUK[:, :], AF.Copy, bias=1.0, scale=-1.0),
           waits=("pr.muk",), sig="pr.omk")
    p.step("v", lambda e: e.memset(VT4[:, :, :, 64:65], 1.0), waits=(), sig="pr.vones")
    p.step("v", lambda e: e.memset(EPSV[:, :], EPS), waits=(), sig="pr.eps")
    for k in range(NC8):
        dma("g", XT[:, k * 1024:(k + 1) * 1024], xT_d[0, k * 128:(k + 1) * 128, 0:1024],
            waits=(), sig=f"s0.lx0.{k}")

    def build_weff(pi, a_d, b_d, w_d, tag, prev_tag, slot=None):
        aw = (f"{prev_tag}.mm7",) if prev_tag else ()
        ww = (f"{prev_tag}.ev7",) if prev_tag else ()
        dma("g", AT_BF[:, :] if tag != "wp" else AT_BF[0:R, 0:C], a_d, waits=aw, sig=f"{tag}.a")
        dma("g", BT_BF[:, :], b_d, waits=aw, sig=f"{tag}.b")
        dma("g", WRAW[:, :], rows_ap(w_d), waits=ww, sig=f"{tag}.w")
        for k in range(NC8):
            we = PSA[:, (k % 2) * 1024:(k % 2) * 1024 + 1024]

            def emit_mm(e, k=k, we=we):
                inst = None
                if tag != "wp":
                    for h in range(H):
                        inst = e.matmul(we[:, h * 64:h * 64 + 64],
                                        AT_BF[0:R, h * C + k * 128:h * C + k * 128 + 128],
                                        BT_BF[0:R, h * 64:h * 64 + 64],
                                        start=True, stop=True)
                else:
                    for nn in range(2):
                        inst = e.matmul(we[:, nn * 512:(nn + 1) * 512],
                                        AT_BF[0:R, k * 128:k * 128 + 128],
                                        BT_BF[0:R, nn * 512:(nn + 1) * 512],
                                        start=True, stop=True)
                return inst
            p.step("t", emit_mm,
                   waits=(f"{tag}.a", f"{tag}.b",
                          f"{tag}.ev{k-2}" if k >= 2 else
                          (f"{prev_tag}.ev{6+k}" if prev_tag else None)),
                   sig=f"{tag}.mm{k}")
            dst = WSTG if slot is None else slot
            p.step("v", lambda e, k=k, we=we, dst=dst: e.scalar_tensor_tensor(
                dst[:, k * 1024:(k + 1) * 1024], we, 2.0,
                WRAW[:, k * 1024:(k + 1) * 1024], ALU.mult, ALU.add),
                waits=(f"{tag}.mm{k}", f"{tag}.w",
                       (f"{prev_tag}.st{k}" if prev_tag and slot is None else None)),
                sig=f"{tag}.ev{k}")
            dma("y", s_weff[pi, k * 128:(k + 1) * 128, :], dst[:, k * 1024:(k + 1) * 1024],
                waits=(f"{tag}.ev{k}",), sig=f"{tag}.st{k}")

    if lvl >= -2:
        build_weff(0, aq_d, bq_d, wq_d, "weq", None, slot=W[0])
    if lvl >= -1:
        build_weff(1, ak_d, bk_d, wk_d, "wek", "weq", slot=W[1])
        build_weff(2, av_d, bv_d, wv_d, "wev", "wek", slot=W[2])
        build_weff(3, ap_d, bpm_d, wp_d, "wp", "wev")

    prev = "wp"
    for pi, (fd, tag) in enumerate(((fr_d, "fr"), (fk_d, "fk"), (fv_d, "fv")) if lvl >= 0 else ()):
        if prev == "wp":
            ww = (f"{prev}.ev7",) + tuple(f"wp.st{k}" for k in range(NC8))
        else:
            ww = tuple(f"{prev}.fst{k}" for k in range(NC8))
        dma("g", WRAW[:, :], rows_ap(fd), waits=ww, sig=f"{tag}.c")
        for k in range(NC8):
            dma("y", s_ffn[pi, k * 128:(k + 1) * 128, :], WRAW[:, k * 1024:(k + 1) * 1024],
                waits=(f"{tag}.c",), sig=f"{tag}.fst{k}")
        prev = tag

    for s in range(NS):
        if lvl <= 0 or (s > 0 and lvl < 9):
            break
        tg = f"s{s}"
        pg = f"s{s-1}"

        if s == 0:
            p.alias(f"{tg}.vones", "pr.vones")
        else:
            p.step("v", lambda e: e.memset(VT4[:, :, :, 64:65], 1.0),
                   waits=(f"{pg}.d.q3.vo3", f"{pg}.b.h15.pv_last"),
                   sig=f"{tg}.vones", hazard=True)
        if s == 0:
            p.alias(f"{tg}.lw.q", "weq.ev7")
            p.alias(f"{tg}.lw.k", "wek.ev7")
            p.alias(f"{tg}.lw.v", "wev.ev7")
        else:
            dma("y", W[0][:, :], rows_ap(s_weff[0]),
                waits=(f"{pg}.d.q3.rmm3",) + tuple(f"weq.st{k}" for k in range(NC8)),
                sig=f"{tg}.lw.q")
            dma("y", W[1][:, :], rows_ap(s_weff[1]),
                waits=(f"{pg}.d.q3.kmm7",) + tuple(f"wek.st{k}" for k in range(NC8)),
                sig=f"{tg}.lw.k")
            dma("y", W[2][:, :], rows_ap(s_weff[2]),
                waits=(f"{pg}.c.q3.pj3",) + tuple(f"wev.st{k}" for k in range(NC8)),
                sig=f"{tg}.lw.v")


        for th in range(2):
            t0 = th * 1024
            for k in range(NC8):
                if s == 0 and th == 0:
                    continue
                elif th == 0:
                    ww = (f"{pg}.d.q3.ln2aff.i3",) if k == 0 else ()
                else:
                    ww = (f"{tg}.a.v.mm7",) if k == 0 else ()
                dma("g", XT[:, k * 1024:(k + 1) * 1024],
                    xT_d[s, k * 128:(k + 1) * 128, t0:t0 + 1024],
                    waits=ww, sig=f"{tg}.lx{th}.{k}")

            for (buf, wsl, lw, pn) in ((QT, W[0], f"{tg}.lw.q", "q"), (KT, W[1], f"{tg}.lw.k", "k")):
                for m in range(NC8):
                    r = (m + (0 if pn == "q" else 8)) % 4
                    PS = (PSA, PSB)[r // 2][:, (r % 2) * 1024:(r % 2) * 1024 + 1024]
                    pidx = m - 4
                    if pidx >= 0:
                        pev = f"{tg}.a.{pn}.ev{th}.{pidx}"
                    elif pn == "k":
                        pev = f"{tg}.a.q.ev{th}.{pidx+8}"
                    elif th > 0:
                        pev = f"{tg}.a.v.ev{(th-1)*8 + 4 + m}"
                    elif s == 0:
                        pev = "wp.ev7"
                    else:
                        pev = f"{pg}.d.q3.g.i3"
                    extra = []
                    if m == 0 and th == 0 and s > 0:
                        if pn == "q":
                            extra += [f"{pg}.d.q3.s2.i3", f"{pg}.d.q3.sq2.i3"]
                        else:
                            extra += [f"{pg}.d.q3.out3"]

                    def emit(e, m=m, PS=PS, wsl=wsl, t0=t0):
                        inst = None
                        for k in range(NC8):
                            for nn in range(2):
                                inst = e.matmul(
                                    PS[:, nn * 512:(nn + 1) * 512],
                                    wsl[:, k * 1024 + m * 128:k * 1024 + m * 128 + 128],
                                    XT[:, k * 1024 + nn * 512:k * 1024 + (nn + 1) * 512],
                                    start=(k == 0), stop=(k == NC8 - 1))
                        return inst
                    p.step("t", emit,
                           waits=(lw, pev) + tuple(extra) +
                                 tuple(f"{tg}.lx{th}.{k}" for k in range(NC8)),
                           sig=f"{tg}.a.{pn}.mm{th}.{m}")
                    p.step("s", lambda e, m=m, PS=PS, buf=buf, t0=t0: e.copy(
                        buf[:, m * T + t0:m * T + t0 + 1024], PS),
                        waits=(f"{tg}.a.{pn}.mm{th}.{m}",), sig=f"{tg}.a.{pn}.ev{th}.{m}")

            for jj in range(8):
                j = th * 8 + jj
                r = jj % 4
                PS = (PSA, PSB)[r // 2][:, (r % 2) * 1024:(r % 2) * 1024 + 1024]
                pidx = jj - 4
                pev = (f"{tg}.a.v.ev{j-4}" if pidx >= 0 else f"{tg}.a.k.ev{th}.{pidx+8}")

                def emitv(e, j=j, jj=jj, PS=PS):
                    inst = None
                    for k in range(NC8):
                        for nn in range(2):
                            inst = e.matmul(
                                PS[:, nn * 512:(nn + 1) * 512],
                                XT[:, k * 1024 + jj * 128:k * 1024 + jj * 128 + 128],
                                W[2][:, k * 1024 + nn * 512:k * 1024 + (nn + 1) * 512],
                                start=(k == 0), stop=(k == NC8 - 1))
                    return inst
                p.step("t", emitv, waits=(f"{tg}.lw.v", pev), sig=f"{tg}.a.v.mm{j}")
                p.step("v", lambda e, j=j, PS=PS: e.tensor_copy(
                    VT4[:, j, :, 0:64], PS.rearrange("x (h d) -> x h d", h=16)),
                    waits=(f"{tg}.a.v.mm{j}", f"{tg}.vones",
                           f"{pg}.b.h15.pv_last" if s > 0 and j == 0 else None),
                    sig=f"{tg}.a.v.ev{j}")

        if lvl == 1:
            break
        for (srcbuf, dstn, nt, evp) in ((QT, QN2, "nq", "q"), (KT, KN2, "nk", "k")):
            for m in range(NC8):
                extra = None
                if m == 0 and nt == "nq":
                    extra = f"{pg}.b.h15.rbe1" if s > 0 else None
                p.step("v", lambda e, m=m, srcbuf=srcbuf: e.tensor_mul(
                    PT[:, :], srcbuf[:, m * T:(m + 1) * T], srcbuf[:, m * T:(m + 1) * T]),
                    waits=(f"{tg}.a.{evp}.ev1.{m}", extra,
                           f"{tg}.{nt}.mm{m-1}" if m >= 1 else
                           (f"{tg}.nq.mm7" if nt == "nk" else None)),
                    sig=f"{tg}.{nt}.sq{m}")

                def emitn(e, m=m):
                    inst = None
                    for nn in range(4):
                        inst = e.matmul(PSB[0:33, nn * 512:(nn + 1) * 512],
                                        CON2[:, :], PT[:, nn * 512:(nn + 1) * 512],
                                        start=True, stop=True)
                    return inst
                p.step("t", emitn,
                       waits=(f"{tg}.{nt}.sq{m}", "pr.con2",
                              f"{tg}.a.v.ev{NT-1}" if (m == 0 and nt == "nq") else None,
                              f"{tg}.a.v.ev{NT-2}" if (m == 0 and nt == "nq") else None),
                       sig=f"{tg}.{nt}.mm{m}")
                p.step("v", lambda e, m=m, dstn=dstn: e.reduce_max(
                    dstn[0:33, m:m + 1], PSB[0:33, :], axis=AX.X),
                    waits=(f"{tg}.{nt}.mm{m}",), sig=f"{tg}.{nt}.red{m}", hazard=True)
        p.step("v", lambda e: e.tensor_add(QN2[0:33, 0:8], QN2[0:33, 0:8], KN2[0:33, 0:8]),
               waits=tuple(f"{tg}.nq.red{m}" for m in range(NC8)) +
                     tuple(f"{tg}.nk.red{m}" for m in range(NC8)), sig=f"{tg}.nb.add")

        def emitbb(e):
            inst = None
            for h in range(H):
                row = (h % 2) * 32
                inst = e.matmul(PSA[:, h:h + 1], CONE[row:row + 1, 0:128],
                                QN2[row:row + 1, h // 2:h // 2 + 1], start=True, stop=True)
            return inst
        p.step("t", emitbb, waits=(f"{tg}.nb.add", f"{tg}.a.v.ev{NT-1}",
                                   f"{tg}.a.v.ev{NT-2}", "pr.cone"),
               sig=f"{tg}.nb.mm")
        p.step("v", lambda e: e.tensor_scalar(BIASV[:, :], PSA[:, 0:16], -0.0625,
                                              None, ALU.mult),
               waits=(f"{tg}.nb.mm",), sig=f"{tg}.nb.bias")

        dma("y", W[0][:, :], rows_ap(s_ffn[0]),
            waits=tuple(f"fr.fst{k}" for k in range(NC8)) +
                  tuple(f"weq.st{k}" for k in range(NC8)) +
                  (f"{tg}.a.q.mm1.7", f"{tg}.nb.mm"), sig=f"{tg}.lw.fr")
        dma("y", W[1][:, :], rows_ap(s_ffn[1]),
            waits=tuple(f"fk.fst{k}" for k in range(NC8)) +
                  tuple(f"wek.st{k}" for k in range(NC8)) +
                  (f"{tg}.a.k.mm1.7",), sig=f"{tg}.lw.fk")
        dma("y", W[2][:, :], rows_ap(s_weff[3]),
            waits=tuple(f"wp.st{k}" for k in range(NC8)) +
                  tuple(f"wev.st{k}" for k in range(NC8)) + (f"{tg}.a.v.mm15",),
            sig=f"{tg}.lw.p0")

        if lvl == 2:
            break
        for h in range(1 if lvl == 3 else (2 if lvl == 35 else H)):
            hof = (h % 2) * 64
            ch = h // 2
            ht = f"{tg}.b.h{h}"
            jq = []
            for j in range(NT):
                qlen = T - j * 128
                subs_j = []
                off = 0
                while off < qlen:
                    n = min(512, qlen - off)
                    subs_j.append((j, j * 128 + off, n))
                    off += n
                jq.extend(subs_j[1:] + subs_j[:1])
            def segments(q0, n):
                a = q0
                out = []
                while a < q0 + n:
                    b = min(q0 + n, (a // 512 + 1) * 512)
                    out.append((a, b))
                    a = b
                return out
            pvf, pvl = {}, {}
            for (j, q0, n) in jq:
                for (a, b) in segments(q0, n):
                    g = a // 512
                    pvf.setdefault(g, (j, q0, a))
                    pvl[g] = (j, q0, a)
            prev_exp = {0: None, 1: None, 2: None, 3: None}
            subs = list(jq)
            NSUB = len(subs)
            mask_of = {}

            def emit_st_step(si):
                j, q0, n = subs[si - 1]
                pp = (si - 1) % 4
                stp = PSA[:, pp * 512:pp * 512 + 512]
                stw = [prev_exp[pp]]
                if serial and h > 0 and si == 1:
                    stw += [f"{tg}.b.h{h-1}.ev0", f"{tg}.b.h{h-1}.ev1",
                            f"{tg}.b.h{h-1}.done"]
                if h == 0 and si <= 4:
                    stw.append(f"{tg}.nb.bias")
                if h > 0 and si <= 4:
                    half = 0 if pp < 2 else 1
                    if (h - 1) % 2 == 1:
                        stw.append(f"{tg}.b.h{h-1}.sme{half}")
                    else:
                        stw.append(f"{tg}.b.h{h-1}.rbe{half}")

                def emit_st(e, j=j, q0=q0, n=n, stp=stp, hof=hof, ch=ch):
                    return e.matmul(
                        stp[:, 0:n],
                        KT[hof:hof + 64, ch * T + j * 128:ch * T + j * 128 + 128],
                        QT[hof:hof + 64, ch * T + q0:ch * T + q0 + n],
                        start=True, stop=True)
                p.step("t", emit_st, waits=tuple(stw), sig=f"{ht}.st.{j}.{q0}")

            def emit_exp_step(si):
                j, q0, n = subs[si - 1]
                pp = (si - 1) % 4
                stp = PSA[:, pp * 512:pp * 512 + 512]
                expw = [f"{ht}.st.{j}.{q0}"]
                if h == 0 and si == 1:
                    expw.append(f"{tg}.nb.bias")
                if h > 0 and si == 1:
                    expw.append(f"{tg}.b.h{h-1}.ev1")
                for back in (1, 2, 3):
                    if si > back:
                        jp, qp, np_ = subs[si - 1 - back]
                        if q0 < qp + np_ and qp < q0 + n:
                            expw.append(f"{ht}.pv.{jp}.{qp}")
                p.step("s", lambda e, q0=q0, n=n, stp=stp, h=h: e.activation(
                    PT[:, q0:q0 + n], stp[:, 0:n], AF.Exp,
                    bias=BIASV[:, h:h + 1], scale=SCALE),
                    waits=tuple(expw), sig=f"{ht}.exp.{j}.{q0}")
                prev_exp[pp] = f"{ht}.exp.{j}.{q0}"
                if q0 == j * 128:
                    p.step("v", lambda e, j=j: e.tensor_mul(
                        PT[:, j * 128:j * 128 + 128], PT[:, j * 128:j * 128 + 128],
                        CMSK[:, :]),
                        waits=(f"{ht}.exp.{j}.{q0}", "pr.cmsk"), sig=f"{ht}.msk.{j}")
                    mask_of[si] = f"{ht}.msk.{j}"

            def emit_pv_step(si):
                j, q0, n = subs[si - 1]

                def emit_pv(e, j=j, q0=q0, n=n, h=h):
                    inst = None
                    for (a, b) in segments(q0, n):
                        g = a // 512
                        inst = e.matmul(
                            PSB[0:65, a:b],
                            VT[:, j * 1040 + h * 65:j * 1040 + h * 65 + 65],
                            PT[:, a:b],
                            start=(pvf[g] == (j, q0, a)), stop=(pvl[g] == (j, q0, a)))
                    return inst
                pvw = [f"{ht}.exp.{j}.{q0}", mask_of.get(si)]
                if si == 1:
                    pvw.append(f"{tg}.a.v.ev{NT-1}")
                    if h > 0:
                        pvw.append(f"{tg}.b.h{h-1}.ev1")
                    else:
                        pvw.append(f"{tg}.nk.red7")
                p.step("t", emit_pv, waits=tuple(pvw), sig=f"{ht}.pv.{j}.{q0}")
                for (a, b) in segments(q0, n):
                    g = a // 512
                    if pvl[g] == (j, q0, a):
                        p.step("v", lambda e, g=g: e.reciprocal(
                            RECF[64:65, 0:512], PSB[64:65, g * 512:(g + 1) * 512]),
                            waits=(f"{ht}.pv.{j}.{q0}",), sig=f"{ht}.rc{g}",
                            hazard=True)
                        p.step("v", lambda e, g=g: e.tensor_copy(
                            PT[64:65, g * 512:(g + 1) * 512], RECF[64:65, 0:512]),
                            waits=(), sig=f"{ht}.rcb{g}", hazard=True)

            for si in range(1, NSUB + 1):
                emit_st_step(si)
                if si >= 4:
                    emit_pv_step(si - 3)
                if si >= 3:
                    emit_exp_step(si - 2)
            emit_pv_step(NSUB - 2)
            emit_exp_step(NSUB - 1)
            emit_pv_step(NSUB - 1)
            emit_exp_step(NSUB)
            emit_pv_step(NSUB)
            jN, qN, _ = subs[NSUB - 1]
            last_pv = f"{ht}.pv.{jN}.{qN}"
            p.alias(f"{ht}.pv_last", last_pv)

            for hh in range(2):
                def emit_rb(e, hh=hh):
                    inst = None
                    for nn in range(2):
                        c0 = hh * 1024 + nn * 512
                        inst = e.matmul(PSA[0:64, c0:c0 + 512],
                                        CONE[64:65, 0:64], PT[64:65, c0:c0 + 512],
                                        start=True, stop=True)
                    return inst
                p.step("t", emit_rb,
                       waits=(f"{ht}.rcb{2*hh}", f"{ht}.rcb{2*hh+1}",
                              prev_exp[2 * hh], prev_exp[2 * hh + 1]),
                       sig=f"{ht}.rb{hh}")
                p.step("v", lambda e, hh=hh: e.tensor_copy(
                    PT[0:64, hh * 1024:(hh + 1) * 1024],
                    PSA[0:64, hh * 1024:(hh + 1) * 1024]),
                    waits=(f"{ht}.rb{hh}",), sig=f"{ht}.rbe{hh}", hazard=True)
                dst = OT if hof == 0 else STG
                dc0 = (ch * T if hof == 0 else 0) + hh * 1024
                evw = [f"{ht}.rbe{hh}"]
                if s == 0 and h == 0 and hh == 0:
                    evw += [f"fv.fst{k}" for k in range(NC8)]
                    evw += [f"wp.st{k}" for k in range(NC8)]
                if hof == 64 and h > 1:
                    evw.append(f"{tg}.b.h{h-2}.shm{hh}")
                p.step("v", lambda e, hh=hh, dst=dst, dc0=dc0: e.tensor_mul(
                    dst[0:64, dc0:dc0 + 1024],
                    PSB[0:64, hh * 1024:(hh + 1) * 1024],
                    PT[0:64, hh * 1024:(hh + 1) * 1024]),
                    waits=tuple(evw), sig=f"{ht}.ev{hh}")
            p.alias(f"{ht}.rbe", f"{ht}.rbe1")
            p.alias(f"{ht}.ev", f"{ht}.ev1")
            if hof == 0:
                p.alias(f"{ht}.done", f"{ht}.ev1")
            else:
                for hh in range(2):
                    def emit_sh(e, hh=hh):
                        inst = None
                        for nn in range(2):
                            c0 = hh * 1024 + nn * 512
                            inst = e.matmul(PSA[64:128, c0:c0 + 512],
                                            CIDB[0:64, 0:64], STG[0:64, c0:c0 + 512],
                                            start=True, stop=True)
                        return inst
                    p.step("t", emit_sh,
                           waits=(f"{ht}.ev{hh}", "pr.cidb", f"{ht}.rbe{hh}"),
                           sig=f"{ht}.shm{hh}")
                    p.step("v", lambda e, hh=hh, ch=ch: e.tensor_copy(
                        OT[64:128, ch * T + hh * 1024:ch * T + (hh + 1) * 1024],
                        PSA[64:128, hh * 1024:(hh + 1) * 1024]),
                        waits=(f"{ht}.shm{hh}",), sig=f"{ht}.sme{hh}")
                p.alias(f"{ht}.shm", f"{ht}.shm1")
                p.alias(f"{ht}.sme", f"{ht}.sme1")
                p.alias(f"{ht}.done", f"{ht}.sme1")

        if lvl in (3, 35, 4):
            break
        for q in range(NQ if lvl != 5 else 1):
            qt = f"{tg}.c.q{q}"
            dt_ = f"{tg}.d.q{q}"
            if q == 0:
                dma("y", WV[:, :], rows_ap(s_ffn[2]),
                    waits=tuple(f"fv.fst{k}" for k in range(NC8)) +
                          (f"{tg}.b.h15.pv_last",), sig=f"{tg}.lw.fv")
            for i4 in range(4):
                i = q * 4 + i4
                pp = i4 % 2
                xin_prev = (f"{qt}.s1.i{i4-2}" if i4 >= 2 else
                            (f"{tg}.c.q{q-1}.s1.i{i4+2}" if q >= 1 else
                             (f"{pg}.c.q3.s1.i{i4+2}" if s > 0 else None)))
                kt_free = f"{tg}.b.h15.pv_last" if (q == 0 and i4 < 2) else None
                dma("y", XIN[:, pp * 1024:(pp + 1) * 1024], x_d[s, i * 128:(i + 1) * 128, :],
                    waits=(xin_prev, kt_free), sig=f"{qt}.xin{i4}")
                PJ = PSA[:, (i4 % 2) * 1024:(i4 % 2) * 1024 + 1024]
                prev_use = (f"{qt}.s1.i{i4-2}" if i4 >= 2 else
                            (f"{tg}.c.q{q-1}.s1.i{i4+2}" if q >= 1 else None))

                def emit_pj(e, i=i, PJ=PJ):
                    for k in range(NC8):
                        for nn in range(2):
                            e.matmul(PJ[:, nn * 512:(nn + 1) * 512],
                                     OT[:, k * T + i * 128:k * T + i * 128 + 128],
                                     W[2][:, k * 1024 + nn * 512:k * 1024 + (nn + 1) * 512],
                                     start=(k == 0), stop=False)
                    inst = None
                    for nn in range(2):
                        inst = e.matmul(PJ[:, nn * 512:(nn + 1) * 512],
                                        CONE[0:1, 0:128], BPB[0:1, nn * 512:(nn + 1) * 512],
                                        start=False, stop=True)
                    return inst
                pjw = [f"{tg}.lw.p0", "pr.bpb", prev_use]
                if q == 0 and i4 == 0:
                    pjw += [f"{tg}.b.h15.sme1", f"{tg}.b.h14.ev1"]
                p.step("t", emit_pj, waits=tuple(pjw), sig=f"{qt}.pj{i4}")
                p.step("v", lambda e, i4=i4, PJ=PJ: e.scalar_tensor_tensor(
                    LNIN[:, i4 * 1024:(i4 + 1) * 1024], PJ, 1.0, XIN[:, (i4 % 2) * 1024:(i4 % 2 + 1) * 1024],
                    ALU.mult, ALU.add, accum_out=STAT[:, i4:i4 + 1]),
                    waits=(f"{qt}.pj{i4}", f"{qt}.xin{i4}",
                           f"{tg}.d.q{q-1}.sq2.i{i4}" if q >= 1 else
                           (f"{pg}.d.q3.sq2.i{i4}" if s > 0 else None),
                           f"{tg}.lx1.7" if q == 0 and i4 == 0 else None),
                    sig=f"{qt}.s1.i{i4}", hazard=True)
                p.step("s", lambda e, i4=i4: e.activation(
                    XRQ[:, 0:1024], LNIN[:, i4 * 1024:(i4 + 1) * 1024], AF.Square,
                    accum_out=STAT[:, 4 + i4:5 + i4]),
                    waits=(f"{qt}.s1.i{i4}",
                           f"{tg}.d.q{q-1}.rmm3" if q >= 1 and i4 == 0 else None),
                    sig=f"{qt}.sq.i{i4}")
            p.step("v", lambda e: e.tensor_scalar(STAT[:, 16:20], STAT[:, 0:4],
                                                  1.0 / C, None, ALU.mult),
                   waits=(f"{qt}.s1.i3",), sig=f"{qt}.mean", hazard=True)
            p.step("v", lambda e: e.tensor_mul(STAT[:, 20:24], STAT[:, 16:20],
                                               STAT[:, 16:20]), waits=(f"{qt}.mean",), sig=f"{qt}.m2", hazard=True)
            p.step("v", lambda e: e.scalar_tensor_tensor(
                STAT[:, 24:28], STAT[:, 4:8], 1.0 / C, STAT[:, 20:24],
                ALU.mult, ALU.subtract),
                waits=(f"{qt}.sq.i3", f"{qt}.m2"), sig=f"{qt}.var", hazard=True)
            p.step("s", lambda e: e.activation(STAT[:, 28:32], STAT[:, 24:28],
                                               AF.Sqrt, bias=EPSV[:, 0:1]),
                   waits=(f"{qt}.var", "pr.eps"), sig=f"{qt}.lnv")
            p.step("v", lambda e: e.reciprocal(STAT[:, 8:12], STAT[:, 28:32]),
                   waits=(f"{qt}.lnv",), sig=f"{qt}.rstd", hazard=True)
            p.step("v", lambda e: e.tensor_mul(STAT[:, 12:16], STAT[:, 16:20],
                                               STAT[:, 8:12]),
                   waits=(f"{qt}.rstd",), sig=f"{qt}.mr", hazard=True)
            for i4 in range(4):
                p.step("v", lambda e, i4=i4: e.tensor_scalar(
                    X1Q[:, i4 * 1024:(i4 + 1) * 1024], LNIN[:, i4 * 1024:(i4 + 1) * 1024],
                    STAT[:, 8 + i4:9 + i4], STAT[:, 12 + i4:13 + i4],
                    ALU.mult, ALU.subtract),
                    waits=(f"{qt}.mr",
                           f"{tg}.d.q{q-1}.s2.i{i4}" if q >= 1 else
                           (f"{pg}.d.q3.s2.i{i4}" if s > 0 else None)),
                    sig=f"{qt}.x1.i{i4}", hazard=True)
            if q == 0:
                p.step("v", lambda e: e.memset(X1T3[:, :, 0:1], 0.0),
                       waits=(), sig=f"{qt}.pad", hazard=True)
            else:
                p.step("v", lambda e: e.tensor_copy(X1T3[:, :, 0:1], BNDRY[:, 0:8]),
                       waits=(f"{tg}.c.q{q-1}.bnd",), sig=f"{qt}.pad", hazard=True)
            for k in range(NC8):
                ppk = k % 4

                def emit_tp(e, k=k, ppk=ppk):
                    inst = None
                    for i4 in range(4):
                        inst = e.transpose(
                            PSB[:, ppk * 512 + i4 * 128:ppk * 512 + (i4 + 1) * 128],
                            X1Q[:, i4 * 1024 + k * 128:i4 * 1024 + k * 128 + 128],
                            CIDF[:, :])
                    return inst
                tpw = ["pr.cid", f"{qt}.x1.i3",
                       f"{qt}.tpe.k{k-4}" if k >= 4 else None]
                if k <= 3:
                    if q == 0 and s == 0:
                        tpw.append(None)
                    else:
                        tpw.append(f"{tg}.c.q{q-1}.tpe.k{4+k}" if q >= 1 else
                                   f"{pg}.c.q3.tpe.k{4+k}")
                    if q == 0:
                        tpw.append(f"{tg}.b.h15.ev1")
                p.step("t", emit_tp, waits=tuple(w for w in tpw if w), sig=f"{qt}.tp.k{k}")
                p.step("s", lambda e, k=k, ppk=ppk: e.copy(
                    X1T[:, k * 513 + 1:k * 513 + 513],
                    PSB[:, ppk * 512:(ppk + 1) * 512]),
                    waits=(f"{qt}.tp.k{k}", f"{qt}.pad",
                           f"{tg}.d.q{q-1}.mixk{k}" if q >= 1 else None),
                    sig=f"{qt}.tpe.k{k}")
            for k in range(NC8):
                p.step("v", lambda e, k=k: e.tensor_sub(
                    MIXT[:, :], X1T[:, k * 513 + 1:k * 513 + 513],
                    X1T[:, k * 513:k * 513 + 512]),
                    waits=(f"{qt}.tpe.k{k}",), sig=f"{dt_}.mt{k}", hazard=True)
                p.step("v", lambda e, k=k: e.scalar_tensor_tensor(
                    XRQ[:, k * 512:(k + 1) * 512], MIXT[:, :],
                    MUR[:, k:k + 1], X1T[:, k * 513:k * 513 + 512],
                    ALU.mult, ALU.add),
                    waits=("pr.mur", f"{dt_}.mt{k}",
                           f"{qt}.sq.i3" if k == 0 else None,
                           f"{tg}.d.q{q-1}.rmm3" if q >= 1 and k == 0 else
                           (f"{pg}.d.q3.rmm3" if s > 0 and q == 0 and k == 0 else None)),
                    sig=f"{dt_}.mixr{k}")
                p.step("v", lambda e, k=k: e.scalar_tensor_tensor(
                    XKQ[:, k * 512:(k + 1) * 512], MIXT[:, :],
                    MUK[:, k:k + 1], X1T[:, k * 513:k * 513 + 512],
                    ALU.mult, ALU.add),
                    waits=("pr.muk",
                           f"{tg}.d.q{q-1}.kmm7" if q >= 1 and k == 0 else
                           (f"{pg}.d.q3.kmm7" if s > 0 and q == 0 and k == 0 else None)),
                    sig=f"{dt_}.mixk{k}")
            p.step("v", lambda e: e.tensor_copy(BNDRY[:, 0:8], X1T3[:, :, 512:513]),
                   waits=(f"{qt}.tpe.k7",), sig=f"{qt}.bnd", hazard=True)
            for i4 in range(4):
                RP = PSA[:, (i4 % 2) * 1024:(i4 % 2) * 1024 + 1024]
                prev_use = f"{dt_}.sig{i4-2}" if i4 >= 2 else f"{qt}.s1.i{i4+2}"

                def emit_r(e, i4=i4, RP=RP, k0=0, k1=NC8):
                    inst = None
                    for k in range(k0, k1):
                        for nn in range(2):
                            inst = e.matmul(
                                RP[:, nn * 512:(nn + 1) * 512],
                                XRQ[:, k * 512 + i4 * 128:k * 512 + i4 * 128 + 128],
                                W[0][:, k * 1024 + nn * 512:k * 1024 + (nn + 1) * 512],
                                start=(k == 0), stop=(k == NC8 - 1))
                    return inst
                if i4 == 0:
                    from functools import partial
                    p.step("t", partial(emit_r, k0=0, k1=4),
                           waits=(f"{tg}.lw.fr", prev_use) +
                           tuple(f"{dt_}.mixr{k}" for k in range(4)),
                           sig=f"{dt_}.rmmA")
                    p.step("t", partial(emit_r, k0=4, k1=NC8),
                           waits=tuple(f"{dt_}.mixr{k}" for k in range(4, NC8)),
                           sig=f"{dt_}.rmm{i4}")
                else:
                    p.step("t", emit_r, waits=(f"{tg}.lw.fr", prev_use) +
                           tuple(f"{dt_}.mixr{k}" for k in range(NC8)),
                           sig=f"{dt_}.rmm{i4}")
                p.step("s", lambda e, i4=i4, RP=RP: e.activation(
                    THQ[:, i4 * 1024:(i4 + 1) * 1024], RP, AF.Sigmoid),
                    waits=(f"{dt_}.rmm{i4}",
                           f"{tg}.b.h15.sme1" if q == 0 and i4 == 0 else None,
                           f"{tg}.d.q{q-1}.g.i{i4}" if q >= 1 else
                           (f"{pg}.d.q3.g.i{i4}" if s > 0 else None)),
                    sig=f"{dt_}.sig{i4}")
            for m in range(NC8):
                KP = PSB[:, (m % 2) * 512:(m % 2) * 512 + 512]
                prev_use = (f"{dt_}.krelu{m-2}" if m >= 2 else f"{qt}.tpe.k{4+m}")

                def emit_k(e, m=m, KP=KP):
                    inst = None
                    for k in range(NC8):
                        inst = e.matmul(
                            KP, W[1][:, k * 1024 + m * 128:k * 1024 + m * 128 + 128],
                            XKQ[:, k * 512:(k + 1) * 512],
                            start=(k == 0), stop=(k == NC8 - 1))
                    return inst
                p.step("t", emit_k, waits=(f"{tg}.lw.fk", prev_use) +
                       tuple(f"{dt_}.mixk{k}" for k in range(NC8)),
                       sig=f"{dt_}.kmm{m}")
                p.step("s", lambda e, m=m, KP=KP: e.activation(
                    KAQ[:, m * 512:(m + 1) * 512], KP, AF.Relu),
                    waits=(f"{dt_}.kmm{m}",
                           f"{tg}.d.q{q-1}.vo3" if q >= 1 and m == 0 else
                           (f"{pg}.d.q3.vo3" if s > 0 and q == 0 and m == 0 else None)),
                    sig=f"{dt_}.krelu{m}")
                p.step("s", lambda e, m=m: e.activation(
                    KAQ[:, m * 512:(m + 1) * 512], KAQ[:, m * 512:(m + 1) * 512],
                    AF.Square),
                    waits=(f"{dt_}.krelu{m}",), sig=f"{dt_}.ksq{m}")
            for i4 in range(4):
                VP = PSA[:, (i4 % 2) * 1024:(i4 % 2) * 1024 + 1024]
                prev_use = f"{dt_}.g.i{i4-2}" if i4 >= 2 else f"{dt_}.sig{i4+2}"

                def emit_vo(e, i4=i4, VP=VP):
                    inst = None
                    for k in range(NC8):
                        for nn in range(2):
                            inst = e.matmul(
                                VP[:, nn * 512:(nn + 1) * 512],
                                KAQ[:, k * 512 + i4 * 128:k * 512 + i4 * 128 + 128],
                                WV[:, k * 1024 + nn * 512:k * 1024 + (nn + 1) * 512],
                                start=(k == 0), stop=(k == NC8 - 1))
                    return inst
                p.step("t", emit_vo, waits=(f"{tg}.lw.fv", prev_use) +
                       tuple(f"{dt_}.ksq{k}" for k in range(NC8)),
                       sig=f"{dt_}.vo{i4}")
                p.step("v", lambda e, i4=i4, VP=VP: e.tensor_mul(
                    THQ[:, i4 * 1024:(i4 + 1) * 1024], THQ[:, i4 * 1024:(i4 + 1) * 1024],
                    VP), waits=(f"{dt_}.vo{i4}",), sig=f"{dt_}.g.i{i4}", hazard=True)
                p.step("v", lambda e, i4=i4: e.scalar_tensor_tensor(
                    LNIN[:, i4 * 1024:(i4 + 1) * 1024], THQ[:, i4 * 1024:(i4 + 1) * 1024],
                    1.0, X1Q[:, i4 * 1024:(i4 + 1) * 1024], ALU.mult, ALU.add,
                    accum_out=STAT[:, 32 + i4:33 + i4]),
                    waits=(f"{dt_}.g.i{i4}",), sig=f"{dt_}.s2.i{i4}", hazard=True)
                p.step("s", lambda e, i4=i4: e.activation(
                    XRQ[:, 0:1024], LNIN[:, i4 * 1024:(i4 + 1) * 1024], AF.Square,
                    accum_out=STAT[:, 36 + i4:37 + i4]),
                    waits=(f"{dt_}.s2.i{i4}", f"{dt_}.rmm3" if i4 == 0 else None),
                    sig=f"{dt_}.sq2.i{i4}")
            p.step("v", lambda e: e.tensor_scalar(STAT[:, 40:44], STAT[:, 32:36],
                                                  1.0 / C, None, ALU.mult),
                   waits=(f"{dt_}.s2.i3",), sig=f"{dt_}.mean2", hazard=True)
            p.step("v", lambda e: e.tensor_mul(STAT[:, 44:48], STAT[:, 40:44],
                                               STAT[:, 40:44]), waits=(f"{dt_}.mean2",), sig=f"{dt_}.m22", hazard=True)
            p.step("v", lambda e: e.scalar_tensor_tensor(
                STAT[:, 48:52], STAT[:, 36:40], 1.0 / C, STAT[:, 44:48],
                ALU.mult, ALU.subtract),
                waits=(f"{dt_}.sq2.i3", f"{dt_}.m22"), sig=f"{dt_}.var2", hazard=True)
            p.step("s", lambda e: e.activation(STAT[:, 52:56], STAT[:, 48:52],
                                               AF.Sqrt, bias=EPSV[:, 0:1]),
                   waits=(f"{dt_}.var2", "pr.eps"), sig=f"{dt_}.lnv2")
            p.step("v", lambda e: e.reciprocal(STAT[:, 56:60], STAT[:, 52:56]),
                   waits=(f"{dt_}.lnv2",), sig=f"{dt_}.rstd2", hazard=True)
            p.step("v", lambda e: e.tensor_mul(STAT[:, 60:64], STAT[:, 40:44],
                                               STAT[:, 56:60]),
                   waits=(f"{dt_}.rstd2",), sig=f"{dt_}.mr2", hazard=True)
            for i4 in range(4):
                i = q * 4 + i4
                pp = i4 % 2
                prev_dma = (f"{dt_}.out{i4-2}" if i4 >= 2 else
                            (f"{tg}.d.q{q-1}.out{i4+2}" if q >= 1 else
                             (f"{pg}.d.q3.out{i4+2}" if s > 0 else None)))
                p.step("v", lambda e, i4=i4, pp=pp: e.tensor_scalar(
                    OUTQ[:, pp * 1024:(pp + 1) * 1024], LNIN[:, i4 * 1024:(i4 + 1) * 1024],
                    STAT[:, 56 + i4:57 + i4], STAT[:, 60 + i4:61 + i4],
                    ALU.mult, ALU.subtract),
                    waits=(f"{dt_}.mr2", prev_dma), sig=f"{dt_}.ln2aff.i{i4}")
                dma("y", y_d[s, i * 128:(i + 1) * 128, :], OUTQ[:, pp * 1024:(pp + 1) * 1024],
                    waits=(f"{dt_}.ln2aff.i{i4}",), sig=f"{dt_}.out{i4}")

    DSQ = int(os.environ.get('MK_DBG_SEQ', '0'))
    if "dbg_qt" in dbg_d and f"s{DSQ}.a.q.ev1.7" in p.events:
        dma("g", dbg_d["dbg_qt"], QT[:, :], waits=(f"s{DSQ}.a.q.ev1.7",), sig="dbg.qt")
    if "dbg_kt" in dbg_d and f"s{DSQ}.a.k.ev1.7" in p.events:
        dma("g", dbg_d["dbg_kt"], KT[:, :], waits=(f"s{DSQ}.a.k.ev1.7",), sig="dbg.kt")
    if "dbg_vt" in dbg_d and f"s{DSQ}.a.v.ev{NT-1}" in p.events:
        dma("g", dbg_d["dbg_vt"], VT[:, :], waits=(f"s{DSQ}.a.v.ev{NT-1}",), sig="dbg.vt")
    if "dbg_ot" in dbg_d:
        lastd = [f"s{DSQ}.b.h{hh_}.done" for hh_ in range(H) if f"s{DSQ}.b.h{hh_}.done" in p.events]
        lastv = [f"s{DSQ}.b.h{hh_}.ev1" for hh_ in range(H) if f"s{DSQ}.b.h{hh_}.ev1" in p.events]
        if lastd:
            dma("g", dbg_d["dbg_ot"], OT[:, :],
                waits=(lastd[-1], lastv[-1]), sig="dbg.ot")
    if "dbg_x1" in dbg_d:
        for q in range(NQ):
            for i4 in range(4):
                i = q * 4 + i4
                if f"s{DSQ}.c.q{q}.x1.i{i4}" not in p.events:
                    continue
                dma("g", dbg_d["dbg_x1"][i * 128:(i + 1) * 128, :],
                    X1Q[:, i4 * 1024:(i4 + 1) * 1024],
                    waits=(f"s{DSQ}.c.q{q}.x1.i{i4}",), sig=f"dbg.x1.{i}")
    if "dbg_s1" in dbg_d and f"s{DSQ}.c.q0.sq.i3" in p.events:
        dma("g", dbg_d["dbg_s1"], LNIN[:, :], waits=(f"s{DSQ}.c.q0.sq.i3",), sig="dbg.s1")
    if "dbg_stat" in dbg_d and f"s{DSQ}.c.q0.mr" in p.events:
        dma("g", dbg_d["dbg_stat"], STAT[:, :], waits=(f"s{DSQ}.c.q0.mr",), sig="dbg.stat")
    if "dbg_x1t" in dbg_d and f"s{DSQ}.c.q0.bnd" in p.events:
        dma("g", dbg_d["dbg_x1t"], X1T[:, :], waits=(f"s{DSQ}.c.q0.bnd",), sig="dbg.x1t")
    if "dbg_qn" in dbg_d and f"s{DSQ}.nb.add" in p.events:
        dma("g", dbg_d["dbg_qn"], QN2[:, :], waits=(f"s{DSQ}.nb.add",), sig="dbg.qn")
        dma("g", dbg_d["dbg_kn"], KN2[:, :], waits=(f"s{DSQ}.nb.add",), sig="dbg.kn")
    if "dbg_bias" in dbg_d and f"s{DSQ}.nb.bias" in p.events:
        dma("y", dbg_d["dbg_bias"], BIASV[:, :], waits=(f"s{DSQ}.nb.bias",), sig="dbg.bias")

    import contextlib
    stack = contextlib.ExitStack()
    sems = {name: stack.enter_context(nc.semaphore(f"sem_{name}"))
            for name in ("t", "v", "s")}
    for ln in p.lane_names():
        sems[ln] = stack.enter_context(nc.semaphore(f"sem_{ln}"))
    finals = dict(p.lane_count)

    def final_wait(e):
        for ln, v in finals.items():
            if v:
                e.wait_ge(sems[ln], v)
        return None
    p.script["y"].append(([], final_wait, False, None))
    with nc.Block() as block:
        p.materialize(block, sems)
    stack.close()
    return nc, sorted(dbg_d.keys())


def _prep_core(inp, core):
    b = [2 * core, 2 * core + 1]
    x = np.asarray(inp["L0_attn_x"])
    f32 = np.float32
    bf = ml_dtypes.bfloat16
    d = {}
    d["xT"] = np.ascontiguousarray(x[b].transpose(0, 2, 1)).astype(f32)
    d["x_tok"] = np.ascontiguousarray(x[b]).astype(f32)
    for nm, key in (("wq_cat", "L0_attn_Wq"), ("wk_cat", "L0_attn_Wk"),
                    ("wv_cat", "L0_attn_Wv")):
        Wm = np.asarray(inp[key])
        d[nm] = np.ascontiguousarray(Wm.transpose(1, 0, 2).reshape(C, C)).astype(f32)
    for nm, key in (("aq_t", "L0_attn_Aq"), ("ak_t", "L0_attn_Ak"), ("av_t", "L0_attn_Av")):
        A = np.asarray(inp[key])
        d[nm] = np.ascontiguousarray(A.transpose(2, 0, 1).reshape(R, H * C)).astype(f32)
    for nm, key in (("bq_r", "L0_attn_Bq"), ("bk_r", "L0_attn_Bk"), ("bv_r", "L0_attn_Bv")):
        B = np.asarray(inp[key])
        d[nm] = np.ascontiguousarray(B.transpose(1, 0, 2).reshape(R, H * HD)).astype(f32)
    d["wp"] = np.ascontiguousarray(np.asarray(inp["L0_attn_Wp"])).astype(f32)
    d["ap_t"] = np.ascontiguousarray(np.asarray(inp["L0_attn_Ap"]).T).astype(f32)
    d["bp_m"] = np.ascontiguousarray(np.asarray(inp["L0_attn_Bp"])).astype(f32)
    d["bp_b"] = np.ascontiguousarray(np.asarray(inp["L0_attn_bp"]).reshape(1, C)).astype(f32)
    d["fr_t"] = np.ascontiguousarray(np.asarray(inp["L0_ffn_W_r"]).T).astype(f32)
    d["fk_t"] = np.ascontiguousarray(np.asarray(inp["L0_ffn_W_k"]).T).astype(f32)
    d["fv_t"] = np.ascontiguousarray(np.asarray(inp["L0_ffn_W_v"]).T).astype(f32)
    d["mu_r_c"] = np.ascontiguousarray(np.asarray(inp["L0_ffn_mu_r"]).reshape(8, 128).T).astype(f32)
    d["mu_k_c"] = np.ascontiguousarray(np.asarray(inp["L0_ffn_mu_k"]).reshape(8, 128).T).astype(f32)
    d["c_idf"] = np.eye(128, dtype=f32)
    d["c_idb"] = np.eye(128, dtype=bf)
    sg, qg = np.meshgrid(np.arange(128), np.arange(128), indexing="ij")
    d["c_mask"] = (qg >= sg).astype(bf)
    d["c_ones"] = np.ones((128, 128), dtype=bf)
    c2 = np.zeros((128, 33), dtype=bf)
    c2[0:64, 0] = 1
    c2[64:128, 32] = 1
    d["c_ones2"] = c2
    return d


_CACHE = {}


def kernel(**inputs):
    dbg = inputs.pop("_dbg", ())
    stop = inputs.pop("_stop", "full")
    key = (tuple(sorted(dbg)), stop)
    if key not in _CACHE:
        _CACHE[key] = build_program(dbg, stop)
    nc, dbg_names = _CACHE[key]
    in_maps = [_prep_core(inputs, c) for c in range(8)]
    res = run_bass_kernel_spmd(nc, in_maps, core_ids=list(range(8)))
    out = np.empty((16, T, C), dtype=np.float32)
    for c in range(8):
        out[2 * c:2 * c + 2] = res.results[c]["y"]
    if dbg:
        return out, {n: res.results[0][n] for n in dbg_names}
    return out


# version 10
# speedup vs baseline: 1.0123x; 1.0123x over previous
import os
import numpy as np
import ml_dtypes
import concourse.bass as bass
import concourse.mybir as mybir
from concourse.bass_utils import run_bass_kernel_spmd

F32 = mybir.dt.float32
BF16 = mybir.dt.bfloat16
AF = mybir.ActivationFunctionType
ALU = mybir.AluOpType
AX = mybir.AxisListType

H, HD, T, C, R = 16, 64, 2048, 1024, 8
NS = 2
NC8 = 8
NT = 16
NQ = 4
SCALE = 0.125
EPS = 1e-5


class Plan:

    ENGINES = ("t", "v", "s", "y", "g")
    NLANES = {"y": 8, "g": 4}

    def __init__(self):
        self.script = {e: [] for e in self.ENGINES}
        self.count = {e: 0 for e in ("t", "v", "s")}
        self.lane_count = {}
        self.lane_rr = {"y": 0, "g": 0}
        self.events = {}
        self.hazard = set()

    def lane_names(self):
        return [f"{e}{i}" for e in ("y", "g") for i in range(self.NLANES[e])]

    def step(self, eng, emit, waits=(), sig=None, hazard=False):
        w = []
        for name in waits:
            if name is None:
                continue
            ev = self.events[name]
            if ev[0] != eng or ev[0] not in ("t", "v", "s") or name in self.hazard:
                w.append(ev)
        if hazard and sig is not None:
            self.hazard.add(sig)
        lane = None
        if eng in ("y", "g"):
            lane = f"{eng}{self.lane_rr[eng] % self.NLANES[eng]}"
            self.lane_rr[eng] += 1
            prev = self.lane_count.get(lane, 0)
            if prev:
                w.append((lane, prev))
            self.lane_count[lane] = prev + 16
            if sig is not None:
                assert sig not in self.events, sig
                self.events[sig] = (lane, prev + 16)
        elif sig is not None:
            self.count[eng] += 1
            assert sig not in self.events, sig
            self.events[sig] = (eng, self.count[eng])
        self.script[eng].append((w, emit, sig is not None or lane is not None, lane))

    def alias(self, new, old):
        self.events[new] = self.events[old]

    def materialize(self, block, sems):
        def mk(ename):
            hiwater = {}

            def body(eng):
                for waits, emit, has_sig, lane in self.script[ename]:
                    for (pe, val) in waits:
                        if hiwater.get(pe, 0) >= val:
                            continue
                        hiwater[pe] = val
                        eng.wait_ge(sems[pe], val)
                    inst = emit(eng)
                    if lane is not None:
                        inst.then_inc(sems[lane], 16)
                    elif has_sig:
                        inst.then_inc(sems[ename], 1)
            return body

        for ename in self.ENGINES:
            getattr(block, {"t": "tensor", "v": "vector", "s": "scalar",
                            "y": "sync", "g": "gpsimd"}[ename])(mk(ename))


def build_program(dbg=(), stop="full"):
    serial = stop.endswith("S")
    stop = stop.rstrip("S")
    lvl = {"pro0": -3, "pro1": -2, "pro2": -1, "pro": 0, "a": 1, "norm": 2,
           "b1": 3, "b2": 35, "b": 4, "cd1": 5, "full": 9}[stop]
    nc = bass.Bass("TRN2", target_bir_lowering=False,
                   detect_race_conditions=False)

    def din(name, shape, dt=F32):
        return nc.dram_tensor(name, shape, dt, kind="ExternalInput").ap()

    xT_d = din("xT", [NS, C, T])
    x_d = din("x_tok", [NS, T, C])
    wq_d, wk_d, wv_d = din("wq_cat", [C, C]), din("wk_cat", [C, C]), din("wv_cat", [C, C])
    aq_d, ak_d, av_d = din("aq_t", [R, H * C]), din("ak_t", [R, H * C]), din("av_t", [R, H * C])
    bq_d, bk_d, bv_d = din("bq_r", [R, C]), din("bk_r", [R, C]), din("bv_r", [R, C])
    wp_d = din("wp", [C, C])
    ap_d = din("ap_t", [R, C])
    bpm_d = din("bp_m", [R, C])
    bpb_d = din("bp_b", [1, C])
    fr_d, fk_d, fv_d = din("fr_t", [C, C]), din("fk_t", [C, C]), din("fv_t", [C, C])
    mur_d, muk_d = din("mu_r_c", [128, 8]), din("mu_k_c", [128, 8])
    cidf_d = din("c_idf", [128, 128])
    cidb_d = din("c_idb", [128, 128], BF16)
    cmask_d = din("c_mask", [128, 128], BF16)
    cones_d = din("c_ones", [128, 128], BF16)
    cones2_d = din("c_ones2", [128, 33], BF16)

    y_d = nc.dram_tensor("y", [NS, T, C], F32, kind="ExternalOutput").ap()

    dbg_d = {}
    for name, shape in (("dbg_qt", [128, NC8 * T]), ("dbg_kt", [128, NC8 * T]),
                        ("dbg_vt", [128, 16 * 1040]), ("dbg_ot", [128, NC8 * T]),
                        ("dbg_x1", [T, C]), ("dbg_bias", [128, 16]),
                        ("dbg_s1", [128, 4096]), ("dbg_stat", [128, 64]),
                        ("dbg_qn", [128, 16]), ("dbg_kn", [128, 16]),
                        ("dbg_x1t", [128, NC8 * 513])):
        if name in dbg:
            dbg_d[name] = nc.dram_tensor(name, shape, F32, kind="ExternalOutput").ap()

    s_weff = nc.dram_tensor("s_wef", [4, C, C], BF16, kind="Internal").ap()
    s_ffn = nc.dram_tensor("s_ffn", [3, C, C], BF16, kind="Internal").ap()

    base, bend = nc.bump_sbuf(211136)
    assert bend <= 229376, (base, bend)

    def at(name, shape, dt, off, live):
        off = off - 16384 + base
        nbytes = int(np.prod(shape)) // shape[0] * (2 if dt == BF16 else 4)
        assert off + nbytes <= bend, (name, off + nbytes, bend)
        return nc.alloc_sbuf_tensor_at(name, shape, dt, offset=off)

    XT = at("XT", [128, NC8 * 1024], BF16, 16384, "A")
    QT = at("QT", [128, NC8 * T], BF16, 32768, "AB")
    KT = at("KT", [128, NC8 * T], BF16, 65536, "AB")
    VT = at("VT", [128, 16 * 1040], BF16, 98304, "AB")
    WV = at("WV", [128, 8192], BF16, 98304, "CD")
    XKQ = at("XKQ", [128, 4096], BF16, 114688, "CD")
    OT = at("OT", [128, NC8 * T], BF16, 131584, "BC")
    PT = at("PT", [128, T], BF16, 164352, "B")
    STG = at("STG", [128, T], BF16, 168448, "B")
    RECF = at("RECF", [128, 512], F32, 168448, "B")
    LNR = at("LNR", [128, 2048], F32, 16384, "B")
    W0 = at("W0", [128, 8192], BF16, 172544, "W")
    W1 = at("W1", [128, 8192], BF16, 188928, "W")
    W2 = at("W2", [128, 8192], BF16, 205312, "W")
    LNIN = at("LNIN", [128, NQ * 1024], F32, 16384, "CD")
    X1Q = at("X1Q", [128, NQ * 1024], F32, 32768, "CD")
    XRQ = at("XRQ", [128, NC8 * 512], BF16, 49152, "CD")
    X1T = at("X1T", [128, NC8 * 513], BF16, 57344, "CD")
    KAQ = at("KAQ", [128, NC8 * 512], BF16, 65568, "CD")
    XIN = at("XIN", [128, 2048], F32, 73760, "CD")
    OUTQ = at("OUTQ", [128, 2048], F32, 81952, "CD")
    THQ = at("THQ", [128, NQ * 1024], BF16, 164352, "CD")
    AT_BF = at("AT_BF", [R, H * C], BF16, 32768, "P")
    BT_BF = at("BT_BF", [R, C], BF16, 65536, "P")
    WRAW = at("WRAW", [128, 8192], BF16, 131584, "P")
    WSTG = at("WSTG", [128, 8192], BF16, 147968, "P")
    o = 221696
    def small(name, shape, dt):
        nonlocal o
        h = at(name, shape, dt, o, "X")
        o += int(np.prod(shape)) // shape[0] * (2 if dt == BF16 else 4)
        o = (o + 31) & ~31
        return h
    CIDF = small("CIDF", [128, 128], F32)
    CIDB = small("CIDB", [128, 128], BF16)
    CMSK = small("CMSK", [128, 128], BF16)
    CONE = small("CONE", [128, 128], BF16)
    CON2 = small("CON2", [128, 33], BF16)
    BPB = small("BPB", [1, 1024], BF16)
    MUR = small("MUR", [128, 8], F32)
    MUK = small("MUK", [128, 8], F32)
    OMR = small("OMR", [128, 8], F32)
    OMK = small("OMK", [128, 8], F32)
    QN2 = small("QN2", [128, 16], BF16)
    KN2 = small("KN2", [128, 16], BF16)
    BIASV = small("BIASV", [128, 16], F32)
    MIXT = small("MIXT", [128, 512], BF16)
    STAT = small("STAT", [128, 64], F32)
    BNDRY = small("BNDRY", [128, 8], BF16)
    EPSV = small("EPSV", [128, 1], F32)
    assert o <= 227520, o

    W = [W0, W1, W2]
    PSA = nc.alloc_psum_tensor("PSA", [128, 2048], F32)
    PSB = nc.alloc_psum_tensor("PSB", [128, 2048], F32)

    VT4 = VT[:, :].rearrange("p (j h e) -> p j h e", j=16, h=16)
    X1T3 = X1T[:, :].rearrange("p (k x) -> p k x", k=8)

    p = Plan()

    def dma(eng_key, dst, src, waits=(), sig=None):
        p.step(eng_key, lambda e, d=dst, s=src: e.dma_start(d, s), waits, sig)

    def rows_ap(d2):
        return d2.rearrange("(k x) c -> x k c", k=NC8)

    dma("y", CIDF[:, :], cidf_d, sig="pr.cid")
    dma("y", CIDB[:, :], cidb_d, sig="pr.cidb")
    dma("y", CMSK[:, :], cmask_d, sig="pr.cmsk")
    dma("y", CONE[:, :], cones_d, sig="pr.cone")
    dma("y", CON2[:, :], cones2_d, sig="pr.con2")
    dma("y", MUR[:, :], mur_d, sig="pr.mur")
    dma("y", MUK[:, :], muk_d, sig="pr.muk")
    dma("g", BPB[:, :], bpb_d, sig="pr.bpb")
    p.step("s", lambda e: e.activation(OMR[:, :], MUR[:, :], AF.Copy, bias=1.0, scale=-1.0),
           waits=("pr.mur",), sig="pr.omr")
    p.step("s", lambda e: e.activation(OMK[:, :], MUK[:, :], AF.Copy, bias=1.0, scale=-1.0),
           waits=("pr.muk",), sig="pr.omk")
    p.step("v", lambda e: e.memset(VT4[:, :, :, 64:65], 1.0), waits=(), sig="pr.vones")
    p.step("v", lambda e: e.memset(EPSV[:, :], EPS), waits=(), sig="pr.eps")
    for k in range(NC8):
        dma("g", XT[:, k * 1024:(k + 1) * 1024], xT_d[0, k * 128:(k + 1) * 128, 0:1024],
            waits=(), sig=f"s0.lx0.{k}")

    def build_weff(pi, a_d, b_d, w_d, tag, prev_tag, slot=None):
        aw = (f"{prev_tag}.mm7",) if prev_tag else ()
        ww = (f"{prev_tag}.ev7",) if prev_tag else ()
        dma("g", AT_BF[:, :] if tag != "wp" else AT_BF[0:R, 0:C], a_d, waits=aw, sig=f"{tag}.a")
        dma("g", BT_BF[:, :], b_d, waits=aw, sig=f"{tag}.b")
        dma("g", WRAW[:, :], rows_ap(w_d), waits=ww, sig=f"{tag}.w")
        for k in range(NC8):
            we = PSA[:, (k % 2) * 1024:(k % 2) * 1024 + 1024]

            def emit_mm(e, k=k, we=we):
                inst = None
                if tag != "wp":
                    for h in range(H):
                        inst = e.matmul(we[:, h * 64:h * 64 + 64],
                                        AT_BF[0:R, h * C + k * 128:h * C + k * 128 + 128],
                                        BT_BF[0:R, h * 64:h * 64 + 64],
                                        start=True, stop=True)
                else:
                    for nn in range(2):
                        inst = e.matmul(we[:, nn * 512:(nn + 1) * 512],
                                        AT_BF[0:R, k * 128:k * 128 + 128],
                                        BT_BF[0:R, nn * 512:(nn + 1) * 512],
                                        start=True, stop=True)
                return inst
            p.step("t", emit_mm,
                   waits=(f"{tag}.a", f"{tag}.b",
                          f"{tag}.ev{k-2}" if k >= 2 else
                          (f"{prev_tag}.ev{6+k}" if prev_tag else None)),
                   sig=f"{tag}.mm{k}")
            dst = WSTG if slot is None else slot
            p.step("v", lambda e, k=k, we=we, dst=dst: e.scalar_tensor_tensor(
                dst[:, k * 1024:(k + 1) * 1024], we, 2.0,
                WRAW[:, k * 1024:(k + 1) * 1024], ALU.mult, ALU.add),
                waits=(f"{tag}.mm{k}", f"{tag}.w",
                       (f"{prev_tag}.st{k}" if prev_tag and slot is None else None)),
                sig=f"{tag}.ev{k}")
            dma("y", s_weff[pi, k * 128:(k + 1) * 128, :], dst[:, k * 1024:(k + 1) * 1024],
                waits=(f"{tag}.ev{k}",), sig=f"{tag}.st{k}")

    if lvl >= -2:
        build_weff(0, aq_d, bq_d, wq_d, "weq", None, slot=W[0])
    if lvl >= -1:
        build_weff(1, ak_d, bk_d, wk_d, "wek", "weq", slot=W[1])
        build_weff(2, av_d, bv_d, wv_d, "wev", "wek", slot=W[2])
        build_weff(3, ap_d, bpm_d, wp_d, "wp", "wev")

    prev = "wp"
    for pi, (fd, tag) in enumerate(((fr_d, "fr"), (fk_d, "fk"), (fv_d, "fv")) if lvl >= 0 else ()):
        if prev == "wp":
            ww = (f"{prev}.ev7",) + tuple(f"wp.st{k}" for k in range(NC8))
        else:
            ww = tuple(f"{prev}.fst{k}" for k in range(NC8))
        dma("g", WRAW[:, :], rows_ap(fd), waits=ww, sig=f"{tag}.c")
        for k in range(NC8):
            dma("y", s_ffn[pi, k * 128:(k + 1) * 128, :], WRAW[:, k * 1024:(k + 1) * 1024],
                waits=(f"{tag}.c",), sig=f"{tag}.fst{k}")
        prev = tag

    for s in range(NS):
        if lvl <= 0 or (s > 0 and lvl < 9):
            break
        tg = f"s{s}"
        pg = f"s{s-1}"

        if s == 0:
            p.alias(f"{tg}.vones", "pr.vones")
        else:
            p.step("v", lambda e: e.memset(VT4[:, :, :, 64:65], 1.0),
                   waits=(f"{pg}.d.q3.vo3", f"{pg}.b.h15.pv_last"),
                   sig=f"{tg}.vones", hazard=True)
        if s == 0:
            p.alias(f"{tg}.lw.q", "weq.ev7")
            p.alias(f"{tg}.lw.k", "wek.ev7")
            p.alias(f"{tg}.lw.v", "wev.ev7")
        else:
            dma("y", W[0][:, :], rows_ap(s_weff[0]),
                waits=(f"{pg}.d.q3.rmm3",) + tuple(f"weq.st{k}" for k in range(NC8)),
                sig=f"{tg}.lw.q")
            dma("y", W[1][:, :], rows_ap(s_weff[1]),
                waits=(f"{pg}.d.q3.kmm7",) + tuple(f"wek.st{k}" for k in range(NC8)),
                sig=f"{tg}.lw.k")
            dma("y", W[2][:, :], rows_ap(s_weff[2]),
                waits=(f"{pg}.c.q3.pj3",) + tuple(f"wev.st{k}" for k in range(NC8)),
                sig=f"{tg}.lw.v")


        for th in range(2):
            t0 = th * 1024
            for k in range(NC8):
                if s == 0 and th == 0:
                    continue
                elif th == 0:
                    ww = (f"{pg}.d.q3.ln2aff.i3",) if k == 0 else ()
                else:
                    ww = (f"{tg}.a.v.mm7",) if k == 0 else ()
                dma("g", XT[:, k * 1024:(k + 1) * 1024],
                    xT_d[s, k * 128:(k + 1) * 128, t0:t0 + 1024],
                    waits=ww, sig=f"{tg}.lx{th}.{k}")

            for (buf, wsl, lw, pn) in ((QT, W[0], f"{tg}.lw.q", "q"), (KT, W[1], f"{tg}.lw.k", "k")):
                for m in range(NC8):
                    r = (m + (0 if pn == "q" else 8)) % 4
                    PS = (PSA, PSB)[r // 2][:, (r % 2) * 1024:(r % 2) * 1024 + 1024]
                    pidx = m - 4
                    if pidx >= 0:
                        pev = f"{tg}.a.{pn}.ev{th}.{pidx}"
                    elif pn == "k":
                        pev = f"{tg}.a.q.ev{th}.{pidx+8}"
                    elif th > 0:
                        pev = f"{tg}.a.v.ev{(th-1)*8 + 4 + m}"
                    elif s == 0:
                        pev = "wp.ev7"
                    else:
                        pev = f"{pg}.d.q3.g.i3"
                    extra = []
                    if m == 0 and th == 0 and s > 0:
                        if pn == "q":
                            extra += [f"{pg}.d.q3.s2.i3", f"{pg}.d.q3.sq2.i3"]
                        else:
                            extra += [f"{pg}.d.q3.out3"]

                    def emit(e, m=m, PS=PS, wsl=wsl, t0=t0):
                        inst = None
                        for k in range(NC8):
                            for nn in range(2):
                                inst = e.matmul(
                                    PS[:, nn * 512:(nn + 1) * 512],
                                    wsl[:, k * 1024 + m * 128:k * 1024 + m * 128 + 128],
                                    XT[:, k * 1024 + nn * 512:k * 1024 + (nn + 1) * 512],
                                    start=(k == 0), stop=(k == NC8 - 1))
                        return inst
                    p.step("t", emit,
                           waits=(lw, pev) + tuple(extra) +
                                 tuple(f"{tg}.lx{th}.{k}" for k in range(NC8)),
                           sig=f"{tg}.a.{pn}.mm{th}.{m}")
                    p.step("s", lambda e, m=m, PS=PS, buf=buf, t0=t0: e.copy(
                        buf[:, m * T + t0:m * T + t0 + 1024], PS),
                        waits=(f"{tg}.a.{pn}.mm{th}.{m}",), sig=f"{tg}.a.{pn}.ev{th}.{m}")

            for jj in range(8):
                j = th * 8 + jj
                r = jj % 4
                PS = (PSA, PSB)[r // 2][:, (r % 2) * 1024:(r % 2) * 1024 + 1024]
                pidx = jj - 4
                pev = (f"{tg}.a.v.ev{j-4}" if pidx >= 0 else f"{tg}.a.k.ev{th}.{pidx+8}")

                def emitv(e, j=j, jj=jj, PS=PS):
                    inst = None
                    for k in range(NC8):
                        for nn in range(2):
                            inst = e.matmul(
                                PS[:, nn * 512:(nn + 1) * 512],
                                XT[:, k * 1024 + jj * 128:k * 1024 + jj * 128 + 128],
                                W[2][:, k * 1024 + nn * 512:k * 1024 + (nn + 1) * 512],
                                start=(k == 0), stop=(k == NC8 - 1))
                    return inst
                p.step("t", emitv, waits=(f"{tg}.lw.v", pev), sig=f"{tg}.a.v.mm{j}")
                p.step("v", lambda e, j=j, PS=PS: e.tensor_copy(
                    VT4[:, j, :, 0:64], PS.rearrange("x (h d) -> x h d", h=16)),
                    waits=(f"{tg}.a.v.mm{j}", f"{tg}.vones",
                           f"{pg}.b.h15.pv_last" if s > 0 and j == 0 else None),
                    sig=f"{tg}.a.v.ev{j}")

        if lvl == 1:
            break
        sq_prev = {0: None, 1: None}
        prev_red = None
        sq_i = 0
        for (srcbuf, dstn, nt, evp) in ((QT, QN2, "nq", "q"), (KT, KN2, "nk", "k")):
            for m in range(NC8):
                par = sq_i % 2
                sq_i += 1
                sqb = PT if par == 0 else STG
                extra = None
                if sq_i == 1:
                    extra = f"{pg}.b.h15.rbe1" if s > 0 else None
                elif sq_i == 2:
                    extra = f"{pg}.b.h15.shm1" if s > 0 else None
                p.step("s", lambda e, m=m, srcbuf=srcbuf, sqb=sqb: e.activation(
                    sqb[:, 0:T], srcbuf[:, m * T:(m + 1) * T], AF.Square),
                    waits=(f"{tg}.a.{evp}.ev1.{m}", extra, sq_prev[par]),
                    sig=f"{tg}.{nt}.sq{m}")

                def emitn(e, m=m, sqb=sqb):
                    inst = None
                    for nn in range(4):
                        inst = e.matmul(PSB[0:33, nn * 512:(nn + 1) * 512],
                                        CON2[:, :], sqb[:, nn * 512:(nn + 1) * 512],
                                        start=True, stop=True)
                    return inst
                p.step("t", emitn,
                       waits=(f"{tg}.{nt}.sq{m}", "pr.con2", prev_red,
                              f"{tg}.a.v.ev{NT-1}" if (m == 0 and nt == "nq") else None,
                              f"{tg}.a.v.ev{NT-2}" if (m == 0 and nt == "nq") else None),
                       sig=f"{tg}.{nt}.mm{m}")
                sq_prev[par] = f"{tg}.{nt}.mm{m}"
                p.step("v", lambda e, m=m, dstn=dstn: e.reduce_max(
                    dstn[0:33, m:m + 1], PSB[0:33, :], axis=AX.X),
                    waits=(f"{tg}.{nt}.mm{m}",), sig=f"{tg}.{nt}.red{m}", hazard=True)
                prev_red = f"{tg}.{nt}.red{m}"
        p.step("v", lambda e: e.tensor_add(QN2[0:33, 0:8], QN2[0:33, 0:8], KN2[0:33, 0:8]),
               waits=tuple(f"{tg}.nq.red{m}" for m in range(NC8)) +
                     tuple(f"{tg}.nk.red{m}" for m in range(NC8)), sig=f"{tg}.nb.add")

        def emitbb(e):
            inst = None
            for h in range(H):
                row = (h % 2) * 32
                inst = e.matmul(PSA[:, h:h + 1], CONE[row:row + 1, 0:128],
                                QN2[row:row + 1, h // 2:h // 2 + 1], start=True, stop=True)
            return inst
        p.step("t", emitbb, waits=(f"{tg}.nb.add", f"{tg}.a.v.ev{NT-1}",
                                   f"{tg}.a.v.ev{NT-2}", "pr.cone"),
               sig=f"{tg}.nb.mm")
        p.step("v", lambda e: e.tensor_scalar(BIASV[:, :], PSA[:, 0:16], -0.0625,
                                              None, ALU.mult),
               waits=(f"{tg}.nb.mm",), sig=f"{tg}.nb.bias")

        dma("y", W[0][:, :], rows_ap(s_ffn[0]),
            waits=tuple(f"fr.fst{k}" for k in range(NC8)) +
                  tuple(f"weq.st{k}" for k in range(NC8)) +
                  (f"{tg}.a.q.mm1.7", f"{tg}.nb.mm"), sig=f"{tg}.lw.fr")
        dma("y", W[1][:, :], rows_ap(s_ffn[1]),
            waits=tuple(f"fk.fst{k}" for k in range(NC8)) +
                  tuple(f"wek.st{k}" for k in range(NC8)) +
                  (f"{tg}.a.k.mm1.7",), sig=f"{tg}.lw.fk")
        dma("y", W[2][:, :], rows_ap(s_weff[3]),
            waits=tuple(f"wp.st{k}" for k in range(NC8)) +
                  tuple(f"wev.st{k}" for k in range(NC8)) + (f"{tg}.a.v.mm15",),
            sig=f"{tg}.lw.p0")

        if lvl == 2:
            break
        for h in range(1 if lvl == 3 else (2 if lvl == 35 else H)):
            hof = (h % 2) * 64
            ch = h // 2
            ht = f"{tg}.b.h{h}"
            jq = []
            for j in range(NT):
                qlen = T - j * 128
                subs_j = []
                off = 0
                while off < qlen:
                    n = min(512, qlen - off)
                    subs_j.append((j, j * 128 + off, n))
                    off += n
                jq.extend(subs_j[1:] + subs_j[:1])
            def segments(q0, n):
                a = q0
                out = []
                while a < q0 + n:
                    b = min(q0 + n, (a // 512 + 1) * 512)
                    out.append((a, b))
                    a = b
                return out
            pvf, pvl = {}, {}
            for (j, q0, n) in jq:
                for (a, b) in segments(q0, n):
                    g = a // 512
                    pvf.setdefault(g, (j, q0, a))
                    pvl[g] = (j, q0, a)
            prev_exp = {0: None, 1: None, 2: None, 3: None}
            subs = list(jq)
            NSUB = len(subs)
            mask_of = {}

            def emit_st_step(si):
                j, q0, n = subs[si - 1]
                pp = (si - 1) % 4
                stp = PSA[:, pp * 512:pp * 512 + 512]
                stw = [prev_exp[pp]]
                if serial and h > 0 and si == 1:
                    stw += [f"{tg}.b.h{h-1}.ev0", f"{tg}.b.h{h-1}.ev1",
                            f"{tg}.b.h{h-1}.done"]
                if h == 0 and si <= 4:
                    stw.append(f"{tg}.nb.bias")
                if h > 0 and si <= 4:
                    half = 0 if pp < 2 else 1
                    if (h - 1) % 2 == 1:
                        stw.append(f"{tg}.b.h{h-1}.sme{half}")
                    else:
                        stw.append(f"{tg}.b.h{h-1}.rbe{half}")

                def emit_st(e, j=j, q0=q0, n=n, stp=stp, hof=hof, ch=ch):
                    return e.matmul(
                        stp[:, 0:n],
                        KT[hof:hof + 64, ch * T + j * 128:ch * T + j * 128 + 128],
                        QT[hof:hof + 64, ch * T + q0:ch * T + q0 + n],
                        start=True, stop=True)
                p.step("t", emit_st, waits=tuple(stw), sig=f"{ht}.st.{j}.{q0}")

            def emit_exp_step(si):
                j, q0, n = subs[si - 1]
                pp = (si - 1) % 4
                stp = PSA[:, pp * 512:pp * 512 + 512]
                expw = [f"{ht}.st.{j}.{q0}"]
                if h == 0 and si == 1:
                    expw.append(f"{tg}.nb.bias")
                if h > 0 and si == 1:
                    expw.append(f"{tg}.b.h{h-1}.ev1")
                for back in (1, 2, 3):
                    if si > back:
                        jp, qp, np_ = subs[si - 1 - back]
                        if q0 < qp + np_ and qp < q0 + n:
                            expw.append(f"{ht}.pv.{jp}.{qp}")
                p.step("s", lambda e, q0=q0, n=n, stp=stp, h=h: e.activation(
                    PT[:, q0:q0 + n], stp[:, 0:n], AF.Exp,
                    bias=BIASV[:, h:h + 1], scale=SCALE),
                    waits=tuple(expw), sig=f"{ht}.exp.{j}.{q0}")
                prev_exp[pp] = f"{ht}.exp.{j}.{q0}"
                if q0 == j * 128:
                    p.step("v", lambda e, j=j: e.tensor_mul(
                        PT[:, j * 128:j * 128 + 128], PT[:, j * 128:j * 128 + 128],
                        CMSK[:, :]),
                        waits=(f"{ht}.exp.{j}.{q0}", "pr.cmsk"), sig=f"{ht}.msk.{j}")
                    mask_of[si] = f"{ht}.msk.{j}"

            def emit_pv_step(si):
                j, q0, n = subs[si - 1]

                def emit_pv(e, j=j, q0=q0, n=n, h=h):
                    inst = None
                    for (a, b) in segments(q0, n):
                        g = a // 512
                        inst = e.matmul(
                            PSB[0:65, a:b],
                            VT[:, j * 1040 + h * 65:j * 1040 + h * 65 + 65],
                            PT[:, a:b],
                            start=(pvf[g] == (j, q0, a)), stop=(pvl[g] == (j, q0, a)))
                    return inst
                pvw = [f"{ht}.exp.{j}.{q0}", mask_of.get(si)]
                if si == 1:
                    pvw.append(f"{tg}.a.v.ev{NT-1}")
                    if h > 0:
                        pvw.append(f"{tg}.b.h{h-1}.ev1")
                    else:
                        pvw.append(f"{tg}.nk.red7")
                p.step("t", emit_pv, waits=tuple(pvw), sig=f"{ht}.pv.{j}.{q0}")
                for (a, b) in segments(q0, n):
                    g = a // 512
                    if pvl[g] == (j, q0, a):
                        p.step("v", lambda e, g=g: e.reciprocal(
                            RECF[64:65, 0:512], PSB[64:65, g * 512:(g + 1) * 512]),
                            waits=(f"{ht}.pv.{j}.{q0}",), sig=f"{ht}.rc{g}",
                            hazard=True)
                        p.step("v", lambda e, g=g: e.tensor_copy(
                            PT[64:65, g * 512:(g + 1) * 512], RECF[64:65, 0:512]),
                            waits=(), sig=f"{ht}.rcb{g}", hazard=True)

            for si in range(1, NSUB + 1):
                emit_st_step(si)
                if si >= 4:
                    emit_pv_step(si - 3)
                if si >= 3:
                    emit_exp_step(si - 2)
            emit_pv_step(NSUB - 2)
            emit_exp_step(NSUB - 1)
            emit_pv_step(NSUB - 1)
            emit_exp_step(NSUB)
            emit_pv_step(NSUB)
            jN, qN, _ = subs[NSUB - 1]
            last_pv = f"{ht}.pv.{jN}.{qN}"
            p.alias(f"{ht}.pv_last", last_pv)

            for hh in range(2):
                def emit_rb(e, hh=hh):
                    inst = None
                    for nn in range(2):
                        c0 = hh * 1024 + nn * 512
                        inst = e.matmul(PSA[0:64, c0:c0 + 512],
                                        CONE[64:65, 0:64], PT[64:65, c0:c0 + 512],
                                        start=True, stop=True)
                    return inst
                p.step("t", emit_rb,
                       waits=(f"{ht}.rcb{2*hh}", f"{ht}.rcb{2*hh+1}",
                              prev_exp[2 * hh], prev_exp[2 * hh + 1]),
                       sig=f"{ht}.rb{hh}")
                p.step("v", lambda e, hh=hh: e.tensor_copy(
                    PT[0:64, hh * 1024:(hh + 1) * 1024],
                    PSA[0:64, hh * 1024:(hh + 1) * 1024]),
                    waits=(f"{ht}.rb{hh}",), sig=f"{ht}.rbe{hh}", hazard=True)
                dst = OT if hof == 0 else STG
                dc0 = (ch * T if hof == 0 else 0) + hh * 1024
                evw = [f"{ht}.rbe{hh}"]
                if s == 0 and h == 0 and hh == 0:
                    evw += [f"fv.fst{k}" for k in range(NC8)]
                    evw += [f"wp.st{k}" for k in range(NC8)]
                if hof == 64 and h > 1:
                    evw.append(f"{tg}.b.h{h-2}.shm{hh}")
                p.step("v", lambda e, hh=hh, dst=dst, dc0=dc0: e.tensor_mul(
                    dst[0:64, dc0:dc0 + 1024],
                    PSB[0:64, hh * 1024:(hh + 1) * 1024],
                    PT[0:64, hh * 1024:(hh + 1) * 1024]),
                    waits=tuple(evw), sig=f"{ht}.ev{hh}")
            p.alias(f"{ht}.rbe", f"{ht}.rbe1")
            p.alias(f"{ht}.ev", f"{ht}.ev1")
            if hof == 0:
                p.alias(f"{ht}.done", f"{ht}.ev1")
            else:
                for hh in range(2):
                    def emit_sh(e, hh=hh):
                        inst = None
                        for nn in range(2):
                            c0 = hh * 1024 + nn * 512
                            inst = e.matmul(PSA[64:128, c0:c0 + 512],
                                            CIDB[0:64, 0:64], STG[0:64, c0:c0 + 512],
                                            start=True, stop=True)
                        return inst
                    p.step("t", emit_sh,
                           waits=(f"{ht}.ev{hh}", "pr.cidb", f"{ht}.rbe{hh}"),
                           sig=f"{ht}.shm{hh}")
                    p.step("v", lambda e, hh=hh, ch=ch: e.tensor_copy(
                        OT[64:128, ch * T + hh * 1024:ch * T + (hh + 1) * 1024],
                        PSA[64:128, hh * 1024:(hh + 1) * 1024]),
                        waits=(f"{ht}.shm{hh}",), sig=f"{ht}.sme{hh}")
                p.alias(f"{ht}.shm", f"{ht}.shm1")
                p.alias(f"{ht}.sme", f"{ht}.sme1")
                p.alias(f"{ht}.done", f"{ht}.sme1")

        if lvl in (3, 35, 4):
            break
        for q in range(NQ if lvl != 5 else 1):
            qt = f"{tg}.c.q{q}"
            dt_ = f"{tg}.d.q{q}"
            if q == 0:
                dma("y", WV[:, :], rows_ap(s_ffn[2]),
                    waits=tuple(f"fv.fst{k}" for k in range(NC8)) +
                          (f"{tg}.b.h15.pv_last",), sig=f"{tg}.lw.fv")
            for i4 in range(4):
                i = q * 4 + i4
                pp = i4 % 2
                xin_prev = (f"{qt}.s1.i{i4-2}" if i4 >= 2 else
                            (f"{tg}.c.q{q-1}.s1.i{i4+2}" if q >= 1 else
                             (f"{pg}.c.q3.s1.i{i4+2}" if s > 0 else None)))
                kt_free = f"{tg}.b.h15.pv_last" if (q == 0 and i4 < 2) else None
                dma("y", XIN[:, pp * 1024:(pp + 1) * 1024], x_d[s, i * 128:(i + 1) * 128, :],
                    waits=(xin_prev, kt_free), sig=f"{qt}.xin{i4}")
                PJ = PSA[:, (i4 % 2) * 1024:(i4 % 2) * 1024 + 1024]
                prev_use = (f"{qt}.s1.i{i4-2}" if i4 >= 2 else
                            (f"{tg}.c.q{q-1}.s1.i{i4+2}" if q >= 1 else None))

                def emit_pj(e, i=i, PJ=PJ):
                    for k in range(NC8):
                        for nn in range(2):
                            e.matmul(PJ[:, nn * 512:(nn + 1) * 512],
                                     OT[:, k * T + i * 128:k * T + i * 128 + 128],
                                     W[2][:, k * 1024 + nn * 512:k * 1024 + (nn + 1) * 512],
                                     start=(k == 0), stop=False)
                    inst = None
                    for nn in range(2):
                        inst = e.matmul(PJ[:, nn * 512:(nn + 1) * 512],
                                        CONE[0:1, 0:128], BPB[0:1, nn * 512:(nn + 1) * 512],
                                        start=False, stop=True)
                    return inst
                pjw = [f"{tg}.lw.p0", "pr.bpb", prev_use]
                if q == 0 and i4 == 0:
                    pjw += [f"{tg}.b.h15.sme1", f"{tg}.b.h14.ev1"]
                p.step("t", emit_pj, waits=tuple(pjw), sig=f"{qt}.pj{i4}")
                p.step("v", lambda e, i4=i4, PJ=PJ: e.scalar_tensor_tensor(
                    LNIN[:, i4 * 1024:(i4 + 1) * 1024], PJ, 1.0, XIN[:, (i4 % 2) * 1024:(i4 % 2 + 1) * 1024],
                    ALU.mult, ALU.add, accum_out=STAT[:, i4:i4 + 1]),
                    waits=(f"{qt}.pj{i4}", f"{qt}.xin{i4}",
                           f"{tg}.d.q{q-1}.sq2.i{i4}" if q >= 1 else
                           (f"{pg}.d.q3.sq2.i{i4}" if s > 0 else None),
                           f"{tg}.lx1.7" if q == 0 and i4 == 0 else None),
                    sig=f"{qt}.s1.i{i4}", hazard=True)
                p.step("s", lambda e, i4=i4: e.activation(
                    XRQ[:, 0:1024], LNIN[:, i4 * 1024:(i4 + 1) * 1024], AF.Square,
                    accum_out=STAT[:, 4 + i4:5 + i4]),
                    waits=(f"{qt}.s1.i{i4}",
                           f"{tg}.d.q{q-1}.rmm3" if q >= 1 and i4 == 0 else None),
                    sig=f"{qt}.sq.i{i4}")
            p.step("v", lambda e: e.tensor_scalar(STAT[:, 16:20], STAT[:, 0:4],
                                                  1.0 / C, None, ALU.mult),
                   waits=(f"{qt}.s1.i3",), sig=f"{qt}.mean", hazard=True)
            p.step("v", lambda e: e.tensor_mul(STAT[:, 20:24], STAT[:, 16:20],
                                               STAT[:, 16:20]), waits=(f"{qt}.mean",), sig=f"{qt}.m2", hazard=True)
            p.step("v", lambda e: e.scalar_tensor_tensor(
                STAT[:, 24:28], STAT[:, 4:8], 1.0 / C, STAT[:, 20:24],
                ALU.mult, ALU.subtract),
                waits=(f"{qt}.sq.i3", f"{qt}.m2"), sig=f"{qt}.var", hazard=True)
            p.step("s", lambda e: e.activation(STAT[:, 28:32], STAT[:, 24:28],
                                               AF.Sqrt, bias=EPSV[:, 0:1]),
                   waits=(f"{qt}.var", "pr.eps"), sig=f"{qt}.lnv")
            p.step("v", lambda e: e.reciprocal(STAT[:, 8:12], STAT[:, 28:32]),
                   waits=(f"{qt}.lnv",), sig=f"{qt}.rstd", hazard=True)
            p.step("v", lambda e: e.tensor_mul(STAT[:, 12:16], STAT[:, 16:20],
                                               STAT[:, 8:12]),
                   waits=(f"{qt}.rstd",), sig=f"{qt}.mr", hazard=True)
            for i4 in range(4):
                p.step("v", lambda e, i4=i4: e.tensor_scalar(
                    X1Q[:, i4 * 1024:(i4 + 1) * 1024], LNIN[:, i4 * 1024:(i4 + 1) * 1024],
                    STAT[:, 8 + i4:9 + i4], STAT[:, 12 + i4:13 + i4],
                    ALU.mult, ALU.subtract),
                    waits=(f"{qt}.mr",
                           f"{tg}.d.q{q-1}.s2.i{i4}" if q >= 1 else
                           (f"{pg}.d.q3.s2.i{i4}" if s > 0 else None)),
                    sig=f"{qt}.x1.i{i4}", hazard=True)
            if q == 0:
                p.step("v", lambda e: e.memset(X1T3[:, :, 0:1], 0.0),
                       waits=(), sig=f"{qt}.pad", hazard=True)
            else:
                p.step("v", lambda e: e.tensor_copy(X1T3[:, :, 0:1], BNDRY[:, 0:8]),
                       waits=(f"{tg}.c.q{q-1}.bnd",), sig=f"{qt}.pad", hazard=True)
            for k in range(NC8):
                ppk = k % 4

                def emit_tp(e, k=k, ppk=ppk, i0=0, i1=4):
                    inst = None
                    for i4 in range(i0, i1):
                        inst = e.transpose(
                            PSB[:, ppk * 512 + i4 * 128:ppk * 512 + (i4 + 1) * 128],
                            X1Q[:, i4 * 1024 + k * 128:i4 * 1024 + k * 128 + 128],
                            CIDF[:, :])
                    return inst
                tpw = ["pr.cid", f"{qt}.x1.i1",
                       f"{qt}.tpe.k{k-4}" if k >= 4 else None]
                if k <= 3:
                    if q == 0 and s == 0:
                        tpw.append(None)
                    else:
                        tpw.append(f"{tg}.c.q{q-1}.tpe.k{4+k}" if q >= 1 else
                                   f"{pg}.c.q3.tpe.k{4+k}")
                    if q == 0:
                        tpw.append(f"{tg}.b.h15.ev1")
                from functools import partial
                p.step("t", partial(emit_tp, i0=0, i1=2),
                       waits=tuple(w for w in tpw if w), sig=f"{qt}.tpa.k{k}")
                p.step("t", partial(emit_tp, i0=2, i1=4),
                       waits=(f"{qt}.x1.i3",), sig=f"{qt}.tp.k{k}")
                p.step("s", lambda e, k=k, ppk=ppk: e.copy(
                    X1T[:, k * 513 + 1:k * 513 + 513],
                    PSB[:, ppk * 512:(ppk + 1) * 512]),
                    waits=(f"{qt}.tp.k{k}", f"{qt}.pad",
                           f"{tg}.d.q{q-1}.mixk{k}" if q >= 1 else None),
                    sig=f"{qt}.tpe.k{k}")
            for k in range(NC8):
                p.step("v", lambda e, k=k: e.tensor_sub(
                    MIXT[:, :], X1T[:, k * 513 + 1:k * 513 + 513],
                    X1T[:, k * 513:k * 513 + 512]),
                    waits=(f"{qt}.tpe.k{k}",), sig=f"{dt_}.mt{k}", hazard=True)
                p.step("v", lambda e, k=k: e.scalar_tensor_tensor(
                    XRQ[:, k * 512:(k + 1) * 512], MIXT[:, :],
                    MUR[:, k:k + 1], X1T[:, k * 513:k * 513 + 512],
                    ALU.mult, ALU.add),
                    waits=("pr.mur", f"{dt_}.mt{k}",
                           f"{qt}.sq.i3" if k == 0 else None,
                           f"{tg}.d.q{q-1}.rmm3" if q >= 1 and k == 0 else
                           (f"{pg}.d.q3.rmm3" if s > 0 and q == 0 and k == 0 else None)),
                    sig=f"{dt_}.mixr{k}")
                p.step("v", lambda e, k=k: e.scalar_tensor_tensor(
                    XKQ[:, k * 512:(k + 1) * 512], MIXT[:, :],
                    MUK[:, k:k + 1], X1T[:, k * 513:k * 513 + 512],
                    ALU.mult, ALU.add),
                    waits=("pr.muk",
                           f"{tg}.d.q{q-1}.kmm7" if q >= 1 and k == 0 else
                           (f"{pg}.d.q3.kmm7" if s > 0 and q == 0 and k == 0 else None)),
                    sig=f"{dt_}.mixk{k}")
            p.step("v", lambda e: e.tensor_copy(BNDRY[:, 0:8], X1T3[:, :, 512:513]),
                   waits=(f"{qt}.tpe.k7",), sig=f"{qt}.bnd", hazard=True)
            for i4 in range(4):
                RP = PSA[:, (i4 % 2) * 1024:(i4 % 2) * 1024 + 1024]
                prev_use = f"{dt_}.sig{i4-2}" if i4 >= 2 else f"{qt}.s1.i{i4+2}"

                def emit_r(e, i4=i4, RP=RP, k0=0, k1=NC8):
                    inst = None
                    for k in range(k0, k1):
                        for nn in range(2):
                            inst = e.matmul(
                                RP[:, nn * 512:(nn + 1) * 512],
                                XRQ[:, k * 512 + i4 * 128:k * 512 + i4 * 128 + 128],
                                W[0][:, k * 1024 + nn * 512:k * 1024 + (nn + 1) * 512],
                                start=(k == 0), stop=(k == NC8 - 1))
                    return inst
                if i4 == 0:
                    from functools import partial
                    p.step("t", partial(emit_r, k0=0, k1=4),
                           waits=(f"{tg}.lw.fr", prev_use) +
                           tuple(f"{dt_}.mixr{k}" for k in range(4)),
                           sig=f"{dt_}.rmmA")
                    p.step("t", partial(emit_r, k0=4, k1=NC8),
                           waits=tuple(f"{dt_}.mixr{k}" for k in range(4, NC8)),
                           sig=f"{dt_}.rmm{i4}")
                else:
                    p.step("t", emit_r, waits=(f"{tg}.lw.fr", prev_use) +
                           tuple(f"{dt_}.mixr{k}" for k in range(NC8)),
                           sig=f"{dt_}.rmm{i4}")
                p.step("s", lambda e, i4=i4, RP=RP: e.activation(
                    THQ[:, i4 * 1024:(i4 + 1) * 1024], RP, AF.Sigmoid),
                    waits=(f"{dt_}.rmm{i4}",
                           f"{tg}.b.h15.sme1" if q == 0 and i4 == 0 else None,
                           f"{tg}.d.q{q-1}.g.i{i4}" if q >= 1 else
                           (f"{pg}.d.q3.g.i{i4}" if s > 0 else None)),
                    sig=f"{dt_}.sig{i4}")
            for m in range(NC8):
                KP = PSB[:, (m % 2) * 512:(m % 2) * 512 + 512]
                prev_use = (f"{dt_}.krelu{m-2}" if m >= 2 else f"{qt}.tpe.k{4+m}")

                def emit_k(e, m=m, KP=KP):
                    inst = None
                    for k in range(NC8):
                        inst = e.matmul(
                            KP, W[1][:, k * 1024 + m * 128:k * 1024 + m * 128 + 128],
                            XKQ[:, k * 512:(k + 1) * 512],
                            start=(k == 0), stop=(k == NC8 - 1))
                    return inst
                p.step("t", emit_k, waits=(f"{tg}.lw.fk", prev_use) +
                       tuple(f"{dt_}.mixk{k}" for k in range(NC8)),
                       sig=f"{dt_}.kmm{m}")
                p.step("s", lambda e, m=m, KP=KP: e.activation(
                    KAQ[:, m * 512:(m + 1) * 512], KP, AF.Relu),
                    waits=(f"{dt_}.kmm{m}",
                           f"{tg}.d.q{q-1}.vo3" if q >= 1 and m == 0 else
                           (f"{pg}.d.q3.vo3" if s > 0 and q == 0 and m == 0 else None)),
                    sig=f"{dt_}.krelu{m}")
                p.step("s", lambda e, m=m: e.activation(
                    KAQ[:, m * 512:(m + 1) * 512], KAQ[:, m * 512:(m + 1) * 512],
                    AF.Square),
                    waits=(f"{dt_}.krelu{m}",), sig=f"{dt_}.ksq{m}")
            for i4 in range(4):
                VP = PSA[:, (i4 % 2) * 1024:(i4 % 2) * 1024 + 1024]
                prev_use = f"{dt_}.g.i{i4-2}" if i4 >= 2 else f"{dt_}.sig{i4+2}"

                def emit_vo(e, i4=i4, VP=VP):
                    inst = None
                    for k in range(NC8):
                        for nn in range(2):
                            inst = e.matmul(
                                VP[:, nn * 512:(nn + 1) * 512],
                                KAQ[:, k * 512 + i4 * 128:k * 512 + i4 * 128 + 128],
                                WV[:, k * 1024 + nn * 512:k * 1024 + (nn + 1) * 512],
                                start=(k == 0), stop=(k == NC8 - 1))
                    return inst
                p.step("t", emit_vo, waits=(f"{tg}.lw.fv", prev_use) +
                       tuple(f"{dt_}.ksq{k}" for k in range(NC8)),
                       sig=f"{dt_}.vo{i4}")
                p.step("v", lambda e, i4=i4, VP=VP: e.tensor_mul(
                    THQ[:, i4 * 1024:(i4 + 1) * 1024], THQ[:, i4 * 1024:(i4 + 1) * 1024],
                    VP), waits=(f"{dt_}.vo{i4}",), sig=f"{dt_}.g.i{i4}", hazard=True)
                p.step("v", lambda e, i4=i4: e.scalar_tensor_tensor(
                    LNIN[:, i4 * 1024:(i4 + 1) * 1024], THQ[:, i4 * 1024:(i4 + 1) * 1024],
                    1.0, X1Q[:, i4 * 1024:(i4 + 1) * 1024], ALU.mult, ALU.add,
                    accum_out=STAT[:, 32 + i4:33 + i4]),
                    waits=(f"{dt_}.g.i{i4}",), sig=f"{dt_}.s2.i{i4}", hazard=True)
                p.step("s", lambda e, i4=i4: e.activation(
                    XRQ[:, 0:1024], LNIN[:, i4 * 1024:(i4 + 1) * 1024], AF.Square,
                    accum_out=STAT[:, 36 + i4:37 + i4]),
                    waits=(f"{dt_}.s2.i{i4}", f"{dt_}.rmm3" if i4 == 0 else None),
                    sig=f"{dt_}.sq2.i{i4}")
            p.step("v", lambda e: e.tensor_scalar(STAT[:, 40:44], STAT[:, 32:36],
                                                  1.0 / C, None, ALU.mult),
                   waits=(f"{dt_}.s2.i3",), sig=f"{dt_}.mean2", hazard=True)
            p.step("v", lambda e: e.tensor_mul(STAT[:, 44:48], STAT[:, 40:44],
                                               STAT[:, 40:44]), waits=(f"{dt_}.mean2",), sig=f"{dt_}.m22", hazard=True)
            p.step("v", lambda e: e.scalar_tensor_tensor(
                STAT[:, 48:52], STAT[:, 36:40], 1.0 / C, STAT[:, 44:48],
                ALU.mult, ALU.subtract),
                waits=(f"{dt_}.sq2.i3", f"{dt_}.m22"), sig=f"{dt_}.var2", hazard=True)
            p.step("s", lambda e: e.activation(STAT[:, 52:56], STAT[:, 48:52],
                                               AF.Sqrt, bias=EPSV[:, 0:1]),
                   waits=(f"{dt_}.var2", "pr.eps"), sig=f"{dt_}.lnv2")
            p.step("v", lambda e: e.reciprocal(STAT[:, 56:60], STAT[:, 52:56]),
                   waits=(f"{dt_}.lnv2",), sig=f"{dt_}.rstd2", hazard=True)
            p.step("v", lambda e: e.tensor_mul(STAT[:, 60:64], STAT[:, 40:44],
                                               STAT[:, 56:60]),
                   waits=(f"{dt_}.rstd2",), sig=f"{dt_}.mr2", hazard=True)
            for i4 in range(4):
                i = q * 4 + i4
                pp = i4 % 2
                prev_dma = (f"{dt_}.out{i4-2}" if i4 >= 2 else
                            (f"{tg}.d.q{q-1}.out{i4+2}" if q >= 1 else
                             (f"{pg}.d.q3.out{i4+2}" if s > 0 else None)))
                p.step("v", lambda e, i4=i4, pp=pp: e.tensor_scalar(
                    OUTQ[:, pp * 1024:(pp + 1) * 1024], LNIN[:, i4 * 1024:(i4 + 1) * 1024],
                    STAT[:, 56 + i4:57 + i4], STAT[:, 60 + i4:61 + i4],
                    ALU.mult, ALU.subtract),
                    waits=(f"{dt_}.mr2", prev_dma), sig=f"{dt_}.ln2aff.i{i4}")
                dma("y", y_d[s, i * 128:(i + 1) * 128, :], OUTQ[:, pp * 1024:(pp + 1) * 1024],
                    waits=(f"{dt_}.ln2aff.i{i4}",), sig=f"{dt_}.out{i4}")

    DSQ = int(os.environ.get('MK_DBG_SEQ', '0'))
    if "dbg_qt" in dbg_d and f"s{DSQ}.a.q.ev1.7" in p.events:
        dma("g", dbg_d["dbg_qt"], QT[:, :], waits=(f"s{DSQ}.a.q.ev1.7",), sig="dbg.qt")
    if "dbg_kt" in dbg_d and f"s{DSQ}.a.k.ev1.7" in p.events:
        dma("g", dbg_d["dbg_kt"], KT[:, :], waits=(f"s{DSQ}.a.k.ev1.7",), sig="dbg.kt")
    if "dbg_vt" in dbg_d and f"s{DSQ}.a.v.ev{NT-1}" in p.events:
        dma("g", dbg_d["dbg_vt"], VT[:, :], waits=(f"s{DSQ}.a.v.ev{NT-1}",), sig="dbg.vt")
    if "dbg_ot" in dbg_d:
        lastd = [f"s{DSQ}.b.h{hh_}.done" for hh_ in range(H) if f"s{DSQ}.b.h{hh_}.done" in p.events]
        lastv = [f"s{DSQ}.b.h{hh_}.ev1" for hh_ in range(H) if f"s{DSQ}.b.h{hh_}.ev1" in p.events]
        if lastd:
            dma("g", dbg_d["dbg_ot"], OT[:, :],
                waits=(lastd[-1], lastv[-1]), sig="dbg.ot")
    if "dbg_x1" in dbg_d:
        for q in range(NQ):
            for i4 in range(4):
                i = q * 4 + i4
                if f"s{DSQ}.c.q{q}.x1.i{i4}" not in p.events:
                    continue
                dma("g", dbg_d["dbg_x1"][i * 128:(i + 1) * 128, :],
                    X1Q[:, i4 * 1024:(i4 + 1) * 1024],
                    waits=(f"s{DSQ}.c.q{q}.x1.i{i4}",), sig=f"dbg.x1.{i}")
    if "dbg_s1" in dbg_d and f"s{DSQ}.c.q0.sq.i3" in p.events:
        dma("g", dbg_d["dbg_s1"], LNIN[:, :], waits=(f"s{DSQ}.c.q0.sq.i3",), sig="dbg.s1")
    if "dbg_stat" in dbg_d and f"s{DSQ}.c.q0.mr" in p.events:
        dma("g", dbg_d["dbg_stat"], STAT[:, :], waits=(f"s{DSQ}.c.q0.mr",), sig="dbg.stat")
    if "dbg_x1t" in dbg_d and f"s{DSQ}.c.q0.bnd" in p.events:
        dma("g", dbg_d["dbg_x1t"], X1T[:, :], waits=(f"s{DSQ}.c.q0.bnd",), sig="dbg.x1t")
    if "dbg_qn" in dbg_d and f"s{DSQ}.nb.add" in p.events:
        dma("g", dbg_d["dbg_qn"], QN2[:, :], waits=(f"s{DSQ}.nb.add",), sig="dbg.qn")
        dma("g", dbg_d["dbg_kn"], KN2[:, :], waits=(f"s{DSQ}.nb.add",), sig="dbg.kn")
    if "dbg_bias" in dbg_d and f"s{DSQ}.nb.bias" in p.events:
        dma("y", dbg_d["dbg_bias"], BIASV[:, :], waits=(f"s{DSQ}.nb.bias",), sig="dbg.bias")

    import contextlib
    stack = contextlib.ExitStack()
    sems = {name: stack.enter_context(nc.semaphore(f"sem_{name}"))
            for name in ("t", "v", "s")}
    for ln in p.lane_names():
        sems[ln] = stack.enter_context(nc.semaphore(f"sem_{ln}"))
    finals = dict(p.lane_count)

    def final_wait(e):
        for ln, v in finals.items():
            if v:
                e.wait_ge(sems[ln], v)
        return None
    p.script["y"].append(([], final_wait, False, None))
    with nc.Block() as block:
        p.materialize(block, sems)
    stack.close()
    return nc, sorted(dbg_d.keys())


def _prep_core(inp, core):
    b = [2 * core, 2 * core + 1]
    x = np.asarray(inp["L0_attn_x"])
    f32 = np.float32
    bf = ml_dtypes.bfloat16
    d = {}
    d["xT"] = np.ascontiguousarray(x[b].transpose(0, 2, 1)).astype(f32)
    d["x_tok"] = np.ascontiguousarray(x[b]).astype(f32)
    for nm, key in (("wq_cat", "L0_attn_Wq"), ("wk_cat", "L0_attn_Wk"),
                    ("wv_cat", "L0_attn_Wv")):
        Wm = np.asarray(inp[key])
        d[nm] = np.ascontiguousarray(Wm.transpose(1, 0, 2).reshape(C, C)).astype(f32)
    for nm, key in (("aq_t", "L0_attn_Aq"), ("ak_t", "L0_attn_Ak"), ("av_t", "L0_attn_Av")):
        A = np.asarray(inp[key])
        d[nm] = np.ascontiguousarray(A.transpose(2, 0, 1).reshape(R, H * C)).astype(f32)
    for nm, key in (("bq_r", "L0_attn_Bq"), ("bk_r", "L0_attn_Bk"), ("bv_r", "L0_attn_Bv")):
        B = np.asarray(inp[key])
        d[nm] = np.ascontiguousarray(B.transpose(1, 0, 2).reshape(R, H * HD)).astype(f32)
    d["wp"] = np.ascontiguousarray(np.asarray(inp["L0_attn_Wp"])).astype(f32)
    d["ap_t"] = np.ascontiguousarray(np.asarray(inp["L0_attn_Ap"]).T).astype(f32)
    d["bp_m"] = np.ascontiguousarray(np.asarray(inp["L0_attn_Bp"])).astype(f32)
    d["bp_b"] = np.ascontiguousarray(np.asarray(inp["L0_attn_bp"]).reshape(1, C)).astype(f32)
    d["fr_t"] = np.ascontiguousarray(np.asarray(inp["L0_ffn_W_r"]).T).astype(f32)
    d["fk_t"] = np.ascontiguousarray(np.asarray(inp["L0_ffn_W_k"]).T).astype(f32)
    d["fv_t"] = np.ascontiguousarray(np.asarray(inp["L0_ffn_W_v"]).T).astype(f32)
    d["mu_r_c"] = np.ascontiguousarray(np.asarray(inp["L0_ffn_mu_r"]).reshape(8, 128).T).astype(f32)
    d["mu_k_c"] = np.ascontiguousarray(np.asarray(inp["L0_ffn_mu_k"]).reshape(8, 128).T).astype(f32)
    d["c_idf"] = np.eye(128, dtype=f32)
    d["c_idb"] = np.eye(128, dtype=bf)
    sg, qg = np.meshgrid(np.arange(128), np.arange(128), indexing="ij")
    d["c_mask"] = (qg >= sg).astype(bf)
    d["c_ones"] = np.ones((128, 128), dtype=bf)
    c2 = np.zeros((128, 33), dtype=bf)
    c2[0:64, 0] = 1
    c2[64:128, 32] = 1
    d["c_ones2"] = c2
    return d


_CACHE = {}


def kernel(**inputs):
    dbg = inputs.pop("_dbg", ())
    stop = inputs.pop("_stop", "full")
    key = (tuple(sorted(dbg)), stop)
    if key not in _CACHE:
        _CACHE[key] = build_program(dbg, stop)
    nc, dbg_names = _CACHE[key]
    in_maps = [_prep_core(inputs, c) for c in range(8)]
    res = run_bass_kernel_spmd(nc, in_maps, core_ids=list(range(8)))
    out = np.empty((16, T, C), dtype=np.float32)
    for c in range(8):
        out[2 * c:2 * c + 2] = res.results[c]["y"]
    if dbg:
        return out, {n: res.results[0][n] for n in dbg_names}
    return out


# version 11
# speedup vs baseline: 1.0149x; 1.0026x over previous
import os
import numpy as np
import ml_dtypes
import concourse.bass as bass
import concourse.mybir as mybir
from concourse.bass_utils import run_bass_kernel_spmd

F32 = mybir.dt.float32
BF16 = mybir.dt.bfloat16
AF = mybir.ActivationFunctionType
ALU = mybir.AluOpType
AX = mybir.AxisListType

H, HD, T, C, R = 16, 64, 2048, 1024, 8
NS = 2
NC8 = 8
NT = 16
NQ = 4
SCALE = 0.125
EPS = 1e-5


class Plan:

    ENGINES = ("t", "v", "s", "y", "g")
    NLANES = {"y": 8, "g": 4}

    def __init__(self):
        self.script = {e: [] for e in self.ENGINES}
        self.count = {e: 0 for e in ("t", "v", "s")}
        self.lane_count = {}
        self.lane_rr = {"y": 0, "g": 0}
        self.events = {}
        self.hazard = set()

    def lane_names(self):
        return [f"{e}{i}" for e in ("y", "g") for i in range(self.NLANES[e])]

    def step(self, eng, emit, waits=(), sig=None, hazard=False):
        w = []
        for name in waits:
            if name is None:
                continue
            ev = self.events[name]
            if ev[0] != eng or ev[0] not in ("t", "v", "s") or name in self.hazard:
                w.append(ev)
        if hazard and sig is not None:
            self.hazard.add(sig)
        lane = None
        if eng in ("y", "g"):
            lane = f"{eng}{self.lane_rr[eng] % self.NLANES[eng]}"
            self.lane_rr[eng] += 1
            prev = self.lane_count.get(lane, 0)
            if prev:
                w.append((lane, prev))
            self.lane_count[lane] = prev + 16
            if sig is not None:
                assert sig not in self.events, sig
                self.events[sig] = (lane, prev + 16)
        elif sig is not None:
            self.count[eng] += 1
            assert sig not in self.events, sig
            self.events[sig] = (eng, self.count[eng])
        self.script[eng].append((w, emit, sig is not None or lane is not None, lane))

    def alias(self, new, old):
        self.events[new] = self.events[old]

    def materialize(self, block, sems):
        def mk(ename):
            hiwater = {}

            def body(eng):
                for waits, emit, has_sig, lane in self.script[ename]:
                    for (pe, val) in waits:
                        if hiwater.get(pe, 0) >= val:
                            continue
                        hiwater[pe] = val
                        eng.wait_ge(sems[pe], val)
                    inst = emit(eng)
                    if lane is not None:
                        inst.then_inc(sems[lane], 16)
                    elif has_sig:
                        inst.then_inc(sems[ename], 1)
            return body

        for ename in self.ENGINES:
            getattr(block, {"t": "tensor", "v": "vector", "s": "scalar",
                            "y": "sync", "g": "gpsimd"}[ename])(mk(ename))


def build_program(dbg=(), stop="full"):
    serial = stop.endswith("S")
    stop = stop.rstrip("S")
    lvl = {"pro0": -3, "pro1": -2, "pro2": -1, "pro": 0, "a": 1, "norm": 2,
           "b1": 3, "b2": 35, "b": 4, "cd1": 5, "full": 9}[stop]
    nc = bass.Bass("TRN2", target_bir_lowering=False,
                   detect_race_conditions=False)

    def din(name, shape, dt=F32):
        return nc.dram_tensor(name, shape, dt, kind="ExternalInput").ap()

    xT_d = din("xT", [NS, C, T])
    x_d = din("x_tok", [NS, T, C])
    wq_d, wk_d, wv_d = din("wq_cat", [C, C]), din("wk_cat", [C, C]), din("wv_cat", [C, C])
    aq_d, ak_d, av_d = din("aq_t", [R, H * C]), din("ak_t", [R, H * C]), din("av_t", [R, H * C])
    bq_d, bk_d, bv_d = din("bq_r", [R, C]), din("bk_r", [R, C]), din("bv_r", [R, C])
    wp_d = din("wp", [C, C])
    ap_d = din("ap_t", [R, C])
    bpm_d = din("bp_m", [R, C])
    bpb_d = din("bp_b", [1, C])
    fr_d, fk_d, fv_d = din("fr_t", [C, C]), din("fk_t", [C, C]), din("fv_t", [C, C])
    mur_d, muk_d = din("mu_r_c", [128, 8]), din("mu_k_c", [128, 8])
    cidf_d = din("c_idf", [128, 128])
    cidb_d = din("c_idb", [128, 128], BF16)
    cmask_d = din("c_mask", [128, 128], BF16)
    cones_d = din("c_ones", [128, 128], BF16)
    cones2_d = din("c_ones2", [128, 33], BF16)

    y_d = nc.dram_tensor("y", [NS, T, C], F32, kind="ExternalOutput").ap()

    dbg_d = {}
    for name, shape in (("dbg_qt", [128, NC8 * T]), ("dbg_kt", [128, NC8 * T]),
                        ("dbg_vt", [128, 16 * 1040]), ("dbg_ot", [128, NC8 * T]),
                        ("dbg_x1", [T, C]), ("dbg_bias", [128, 16]),
                        ("dbg_s1", [128, 4096]), ("dbg_stat", [128, 64]),
                        ("dbg_qn", [128, 16]), ("dbg_kn", [128, 16]),
                        ("dbg_x1t", [128, NC8 * 513])):
        if name in dbg:
            dbg_d[name] = nc.dram_tensor(name, shape, F32, kind="ExternalOutput").ap()

    s_weff = nc.dram_tensor("s_wef", [4, C, C], BF16, kind="Internal").ap()
    s_ffn = nc.dram_tensor("s_ffn", [3, C, C], BF16, kind="Internal").ap()

    base, bend = nc.bump_sbuf(211136)
    assert bend <= 229376, (base, bend)

    def at(name, shape, dt, off, live):
        off = off - 16384 + base
        nbytes = int(np.prod(shape)) // shape[0] * (2 if dt == BF16 else 4)
        assert off + nbytes <= bend, (name, off + nbytes, bend)
        return nc.alloc_sbuf_tensor_at(name, shape, dt, offset=off)

    XT = at("XT", [128, NC8 * 1024], BF16, 16384, "A")
    QT = at("QT", [128, NC8 * T], BF16, 32768, "AB")
    KT = at("KT", [128, NC8 * T], BF16, 65536, "AB")
    VT = at("VT", [128, 16 * 1040], BF16, 98304, "AB")
    WV = at("WV", [128, 8192], BF16, 98304, "CD")
    XKQ = at("XKQ", [128, 4096], BF16, 114688, "CD")
    OT = at("OT", [128, NC8 * T], BF16, 131584, "BC")
    PT = at("PT", [128, T], BF16, 164352, "B")
    STG = at("STG", [128, T], BF16, 168448, "B")
    RECF = at("RECF", [128, 512], F32, 168448, "B")
    LNR = at("LNR", [128, 2048], F32, 16384, "B")
    W0 = at("W0", [128, 8192], BF16, 172544, "W")
    W1 = at("W1", [128, 8192], BF16, 188928, "W")
    W2 = at("W2", [128, 8192], BF16, 205312, "W")
    LNIN = at("LNIN", [128, NQ * 1024], F32, 16384, "CD")
    X1Q = at("X1Q", [128, NQ * 1024], F32, 32768, "CD")
    XRQ = at("XRQ", [128, NC8 * 512], BF16, 49152, "CD")
    X1T = at("X1T", [128, NC8 * 513], BF16, 57344, "CD")
    KAQ = at("KAQ", [128, NC8 * 512], BF16, 65568, "CD")
    XIN = at("XIN", [128, 2048], F32, 73760, "CD")
    OUTQ = at("OUTQ", [128, 2048], F32, 81952, "CD")
    THQ = at("THQ", [128, NQ * 1024], BF16, 164352, "CD")
    AT_BF = at("AT_BF", [R, H * C], BF16, 32768, "P")
    BT_BF = at("BT_BF", [R, C], BF16, 65536, "P")
    WRAW = at("WRAW", [128, 8192], BF16, 131584, "P")
    WSTG = at("WSTG", [128, 8192], BF16, 147968, "P")
    o = 221696
    def small(name, shape, dt):
        nonlocal o
        h = at(name, shape, dt, o, "X")
        o += int(np.prod(shape)) // shape[0] * (2 if dt == BF16 else 4)
        o = (o + 31) & ~31
        return h
    CIDF = small("CIDF", [128, 128], F32)
    CIDB = small("CIDB", [128, 128], BF16)
    CMSK = small("CMSK", [128, 128], BF16)
    CONE = small("CONE", [128, 128], BF16)
    CON2 = small("CON2", [128, 33], BF16)
    BPB = small("BPB", [1, 1024], BF16)
    MUR = small("MUR", [128, 8], F32)
    MUK = small("MUK", [128, 8], F32)
    OMR = small("OMR", [128, 8], F32)
    OMK = small("OMK", [128, 8], F32)
    QN2 = small("QN2", [128, 16], BF16)
    KN2 = small("KN2", [128, 16], BF16)
    BIASV = small("BIASV", [128, 16], F32)
    MIXT = small("MIXT", [128, 512], BF16)
    STAT = small("STAT", [128, 64], F32)
    BNDRY = small("BNDRY", [128, 8], BF16)
    EPSV = small("EPSV", [128, 1], F32)
    assert o <= 227520, o

    W = [W0, W1, W2]
    PSA = nc.alloc_psum_tensor("PSA", [128, 2048], F32)
    PSB = nc.alloc_psum_tensor("PSB", [128, 2048], F32)

    VT4 = VT[:, :].rearrange("p (j h e) -> p j h e", j=16, h=16)
    X1T3 = X1T[:, :].rearrange("p (k x) -> p k x", k=8)

    p = Plan()

    def dma(eng_key, dst, src, waits=(), sig=None):
        p.step(eng_key, lambda e, d=dst, s=src: e.dma_start(d, s), waits, sig)

    def rows_ap(d2):
        return d2.rearrange("(k x) c -> x k c", k=NC8)

    dma("y", CIDF[:, :], cidf_d, sig="pr.cid")
    dma("y", CIDB[:, :], cidb_d, sig="pr.cidb")
    dma("y", CMSK[:, :], cmask_d, sig="pr.cmsk")
    dma("y", CONE[:, :], cones_d, sig="pr.cone")
    dma("y", CON2[:, :], cones2_d, sig="pr.con2")
    dma("y", MUR[:, :], mur_d, sig="pr.mur")
    dma("y", MUK[:, :], muk_d, sig="pr.muk")
    dma("g", BPB[:, :], bpb_d, sig="pr.bpb")
    p.step("s", lambda e: e.activation(OMR[:, :], MUR[:, :], AF.Copy, bias=1.0, scale=-1.0),
           waits=("pr.mur",), sig="pr.omr")
    p.step("s", lambda e: e.activation(OMK[:, :], MUK[:, :], AF.Copy, bias=1.0, scale=-1.0),
           waits=("pr.muk",), sig="pr.omk")
    p.step("v", lambda e: e.memset(VT4[:, :, :, 64:65], 1.0), waits=(), sig="pr.vones")
    p.step("v", lambda e: e.memset(EPSV[:, :], EPS), waits=(), sig="pr.eps")
    for k in range(NC8):
        dma("g", XT[:, k * 1024:(k + 1) * 1024], xT_d[0, k * 128:(k + 1) * 128, 0:1024],
            waits=(), sig=f"s0.lx0.{k}")

    def build_weff(pi, a_d, b_d, w_d, tag, prev_tag, slot=None):
        aw = (f"{prev_tag}.mm7",) if prev_tag else ()
        ww = (f"{prev_tag}.ev7",) if prev_tag else ()
        dma("g", AT_BF[:, :] if tag != "wp" else AT_BF[0:R, 0:C], a_d, waits=aw, sig=f"{tag}.a")
        dma("g", BT_BF[:, :], b_d, waits=aw, sig=f"{tag}.b")
        dma("g", WRAW[:, :], rows_ap(w_d), waits=ww, sig=f"{tag}.w")
        for k in range(NC8):
            we = PSA[:, (k % 2) * 1024:(k % 2) * 1024 + 1024]

            def emit_mm(e, k=k, we=we):
                inst = None
                if tag != "wp":
                    for h in range(H):
                        inst = e.matmul(we[:, h * 64:h * 64 + 64],
                                        AT_BF[0:R, h * C + k * 128:h * C + k * 128 + 128],
                                        BT_BF[0:R, h * 64:h * 64 + 64],
                                        start=True, stop=True)
                else:
                    for nn in range(2):
                        inst = e.matmul(we[:, nn * 512:(nn + 1) * 512],
                                        AT_BF[0:R, k * 128:k * 128 + 128],
                                        BT_BF[0:R, nn * 512:(nn + 1) * 512],
                                        start=True, stop=True)
                return inst
            p.step("t", emit_mm,
                   waits=(f"{tag}.a", f"{tag}.b",
                          f"{tag}.ev{k-2}" if k >= 2 else
                          (f"{prev_tag}.ev{6+k}" if prev_tag else None)),
                   sig=f"{tag}.mm{k}")
            dst = WSTG if slot is None else slot
            p.step("v", lambda e, k=k, we=we, dst=dst: e.scalar_tensor_tensor(
                dst[:, k * 1024:(k + 1) * 1024], we, 2.0,
                WRAW[:, k * 1024:(k + 1) * 1024], ALU.mult, ALU.add),
                waits=(f"{tag}.mm{k}", f"{tag}.w",
                       (f"{prev_tag}.st{k}" if prev_tag and slot is None else None)),
                sig=f"{tag}.ev{k}")
            dma("y", s_weff[pi, k * 128:(k + 1) * 128, :], dst[:, k * 1024:(k + 1) * 1024],
                waits=(f"{tag}.ev{k}",), sig=f"{tag}.st{k}")

    if lvl >= -2:
        build_weff(0, aq_d, bq_d, wq_d, "weq", None, slot=W[0])
    if lvl >= -1:
        build_weff(1, ak_d, bk_d, wk_d, "wek", "weq", slot=W[1])
        build_weff(2, av_d, bv_d, wv_d, "wev", "wek", slot=W[2])
        build_weff(3, ap_d, bpm_d, wp_d, "wp", "wev")

    prev = "wp"
    for pi, (fd, tag) in enumerate(((fr_d, "fr"), (fk_d, "fk"), (fv_d, "fv")) if lvl >= 0 else ()):
        if prev == "wp":
            ww = (f"{prev}.ev7",) + tuple(f"wp.st{k}" for k in range(NC8))
        else:
            ww = tuple(f"{prev}.fst{k}" for k in range(NC8))
        dma("g", WRAW[:, :], rows_ap(fd), waits=ww, sig=f"{tag}.c")
        for k in range(NC8):
            dma("y", s_ffn[pi, k * 128:(k + 1) * 128, :], WRAW[:, k * 1024:(k + 1) * 1024],
                waits=(f"{tag}.c",), sig=f"{tag}.fst{k}")
        prev = tag

    for s in range(NS):
        if lvl <= 0 or (s > 0 and lvl < 9):
            break
        tg = f"s{s}"
        pg = f"s{s-1}"

        if s == 0:
            p.alias(f"{tg}.vones", "pr.vones")
        else:
            p.step("v", lambda e: e.memset(VT4[:, :, :, 64:65], 1.0),
                   waits=(f"{pg}.d.q3.vo3", f"{pg}.b.h15.pv_last"),
                   sig=f"{tg}.vones", hazard=True)
        if s == 0:
            p.alias(f"{tg}.lw.q", "weq.ev7")
            p.alias(f"{tg}.lw.k", "wek.ev7")
            p.alias(f"{tg}.lw.v", "wev.ev7")
        else:
            dma("y", W[0][:, :], rows_ap(s_weff[0]),
                waits=(f"{pg}.d.q3.rmm3",) + tuple(f"weq.st{k}" for k in range(NC8)),
                sig=f"{tg}.lw.q")
            dma("y", W[1][:, :], rows_ap(s_weff[1]),
                waits=(f"{pg}.d.q3.kmm7",) + tuple(f"wek.st{k}" for k in range(NC8)),
                sig=f"{tg}.lw.k")
            dma("y", W[2][:, :], rows_ap(s_weff[2]),
                waits=(f"{pg}.c.q3.pj3",) + tuple(f"wev.st{k}" for k in range(NC8)),
                sig=f"{tg}.lw.v")


        for th in range(2):
            t0 = th * 1024
            for k in range(NC8):
                if s == 0 and th == 0:
                    continue
                elif th == 0:
                    ww = (f"{pg}.d.q3.ln2aff.i3",) if k == 0 else ()
                else:
                    ww = (f"{tg}.a.v.mm7",) if k == 0 else ()
                dma("g", XT[:, k * 1024:(k + 1) * 1024],
                    xT_d[s, k * 128:(k + 1) * 128, t0:t0 + 1024],
                    waits=ww, sig=f"{tg}.lx{th}.{k}")

            for (buf, wsl, lw, pn) in ((QT, W[0], f"{tg}.lw.q", "q"), (KT, W[1], f"{tg}.lw.k", "k")):
                for m in range(NC8):
                    r = (m + (0 if pn == "q" else 8)) % 4
                    PS = (PSA, PSB)[r // 2][:, (r % 2) * 1024:(r % 2) * 1024 + 1024]
                    pidx = m - 4
                    if pidx >= 0:
                        pev = f"{tg}.a.{pn}.ev{th}.{pidx}"
                    elif pn == "k":
                        pev = f"{tg}.a.q.ev{th}.{pidx+8}"
                    elif th > 0:
                        pev = f"{tg}.a.v.ev{(th-1)*8 + 4 + m}"
                    elif s == 0:
                        pev = "wp.ev7"
                    else:
                        pev = f"{pg}.d.q3.g.i3"
                    extra = []
                    if m == 0 and th == 0 and s > 0:
                        if pn == "q":
                            extra += [f"{pg}.d.q3.s2.i3", f"{pg}.d.q3.sq2.i3"]
                        else:
                            extra += [f"{pg}.d.q3.out3"]

                    def emit(e, m=m, PS=PS, wsl=wsl, t0=t0):
                        inst = None
                        for k in range(NC8):
                            for nn in range(2):
                                inst = e.matmul(
                                    PS[:, nn * 512:(nn + 1) * 512],
                                    wsl[:, k * 1024 + m * 128:k * 1024 + m * 128 + 128],
                                    XT[:, k * 1024 + nn * 512:k * 1024 + (nn + 1) * 512],
                                    start=(k == 0), stop=(k == NC8 - 1))
                        return inst
                    p.step("t", emit,
                           waits=(lw, pev) + tuple(extra) +
                                 tuple(f"{tg}.lx{th}.{k}" for k in range(NC8)),
                           sig=f"{tg}.a.{pn}.mm{th}.{m}")
                    p.step("s", lambda e, m=m, PS=PS, buf=buf, t0=t0: e.copy(
                        buf[:, m * T + t0:m * T + t0 + 1024], PS),
                        waits=(f"{tg}.a.{pn}.mm{th}.{m}",), sig=f"{tg}.a.{pn}.ev{th}.{m}")

            for jj in range(8):
                j = th * 8 + jj
                r = jj % 4
                PS = (PSA, PSB)[r // 2][:, (r % 2) * 1024:(r % 2) * 1024 + 1024]
                pidx = jj - 4
                pev = (f"{tg}.a.v.ev{j-4}" if pidx >= 0 else f"{tg}.a.k.ev{th}.{pidx+8}")

                def emitv(e, j=j, jj=jj, PS=PS):
                    inst = None
                    for k in range(NC8):
                        for nn in range(2):
                            inst = e.matmul(
                                PS[:, nn * 512:(nn + 1) * 512],
                                XT[:, k * 1024 + jj * 128:k * 1024 + jj * 128 + 128],
                                W[2][:, k * 1024 + nn * 512:k * 1024 + (nn + 1) * 512],
                                start=(k == 0), stop=(k == NC8 - 1))
                    return inst
                p.step("t", emitv, waits=(f"{tg}.lw.v", pev), sig=f"{tg}.a.v.mm{j}")
                p.step("v", lambda e, j=j, PS=PS: e.tensor_copy(
                    VT4[:, j, :, 0:64], PS.rearrange("x (h d) -> x h d", h=16)),
                    waits=(f"{tg}.a.v.mm{j}", f"{tg}.vones",
                           f"{pg}.b.h15.pv_last" if s > 0 and j == 0 else None),
                    sig=f"{tg}.a.v.ev{j}")

        if lvl == 1:
            break
        sq_prev = {0: None, 1: None}
        prev_red = None
        sq_i = 0
        for (srcbuf, dstn, nt, evp) in ((QT, QN2, "nq", "q"), (KT, KN2, "nk", "k")):
            for m in range(NC8):
                par = sq_i % 2
                sq_i += 1
                sqb = PT if par == 0 else STG
                extra = None
                if sq_i == 1:
                    extra = f"{pg}.b.h15.rbe1" if s > 0 else None
                elif sq_i == 2:
                    extra = f"{pg}.b.h15.shm1" if s > 0 else None
                p.step("s", lambda e, m=m, srcbuf=srcbuf, sqb=sqb: e.activation(
                    sqb[:, 0:T], srcbuf[:, m * T:(m + 1) * T], AF.Square),
                    waits=(f"{tg}.a.{evp}.ev1.{m}", extra, sq_prev[par]),
                    sig=f"{tg}.{nt}.sq{m}")

                def emitn(e, m=m, sqb=sqb):
                    inst = None
                    for nn in range(4):
                        inst = e.matmul(PSB[0:33, nn * 512:(nn + 1) * 512],
                                        CON2[:, :], sqb[:, nn * 512:(nn + 1) * 512],
                                        start=True, stop=True)
                    return inst
                p.step("t", emitn,
                       waits=(f"{tg}.{nt}.sq{m}", "pr.con2", prev_red,
                              f"{tg}.a.v.ev{NT-1}" if (m == 0 and nt == "nq") else None,
                              f"{tg}.a.v.ev{NT-2}" if (m == 0 and nt == "nq") else None),
                       sig=f"{tg}.{nt}.mm{m}")
                sq_prev[par] = f"{tg}.{nt}.mm{m}"
                p.step("v", lambda e, m=m, dstn=dstn: e.reduce_max(
                    dstn[0:33, m:m + 1], PSB[0:33, :], axis=AX.X),
                    waits=(f"{tg}.{nt}.mm{m}",), sig=f"{tg}.{nt}.red{m}", hazard=True)
                prev_red = f"{tg}.{nt}.red{m}"
        p.step("v", lambda e: e.tensor_add(QN2[0:33, 0:8], QN2[0:33, 0:8], KN2[0:33, 0:8]),
               waits=tuple(f"{tg}.nq.red{m}" for m in range(NC8)) +
                     tuple(f"{tg}.nk.red{m}" for m in range(NC8)), sig=f"{tg}.nb.add")

        def emitbb(e):
            inst = None
            for h in range(H):
                row = (h % 2) * 32
                inst = e.matmul(PSA[:, h:h + 1], CONE[row:row + 1, 0:128],
                                QN2[row:row + 1, h // 2:h // 2 + 1], start=True, stop=True)
            return inst
        p.step("t", emitbb, waits=(f"{tg}.nb.add", f"{tg}.a.v.ev{NT-1}",
                                   f"{tg}.a.v.ev{NT-2}", "pr.cone"),
               sig=f"{tg}.nb.mm")
        p.step("v", lambda e: e.tensor_scalar(BIASV[:, :], PSA[:, 0:16], -0.0625,
                                              None, ALU.mult),
               waits=(f"{tg}.nb.mm",), sig=f"{tg}.nb.bias")

        dma("y", W[0][:, :], rows_ap(s_ffn[0]),
            waits=tuple(f"fr.fst{k}" for k in range(NC8)) +
                  tuple(f"weq.st{k}" for k in range(NC8)) +
                  (f"{tg}.a.q.mm1.7", f"{tg}.nb.mm"), sig=f"{tg}.lw.fr")
        dma("y", W[1][:, :], rows_ap(s_ffn[1]),
            waits=tuple(f"fk.fst{k}" for k in range(NC8)) +
                  tuple(f"wek.st{k}" for k in range(NC8)) +
                  (f"{tg}.a.k.mm1.7",), sig=f"{tg}.lw.fk")
        dma("y", W[2][:, :], rows_ap(s_weff[3]),
            waits=tuple(f"wp.st{k}" for k in range(NC8)) +
                  tuple(f"wev.st{k}" for k in range(NC8)) + (f"{tg}.a.v.mm15",),
            sig=f"{tg}.lw.p0")

        if lvl == 2:
            break
        for h in range(1 if lvl == 3 else (2 if lvl == 35 else H)):
            hof = (h % 2) * 64
            ch = h // 2
            ht = f"{tg}.b.h{h}"
            jq = []
            for j in range(NT):
                qlen = T - j * 128
                subs_j = []
                off = 0
                while off < qlen:
                    n = min(512, qlen - off)
                    subs_j.append((j, j * 128 + off, n))
                    off += n
                jq.extend(subs_j[1:] + subs_j[:1])
            def segments(q0, n):
                a = q0
                out = []
                while a < q0 + n:
                    b = min(q0 + n, (a // 512 + 1) * 512)
                    out.append((a, b))
                    a = b
                return out
            pvf, pvl = {}, {}
            for (j, q0, n) in jq:
                for (a, b) in segments(q0, n):
                    g = a // 512
                    pvf.setdefault(g, (j, q0, a))
                    pvl[g] = (j, q0, a)
            prev_exp = {0: None, 1: None, 2: None, 3: None}
            subs = list(jq)
            NSUB = len(subs)
            mask_of = {}

            def emit_st_step(si):
                j, q0, n = subs[si - 1]
                pp = (si - 1) % 4
                stp = PSA[:, pp * 512:pp * 512 + 512]
                stw = [prev_exp[pp]]
                if serial and h > 0 and si == 1:
                    stw += [f"{tg}.b.h{h-1}.ev0", f"{tg}.b.h{h-1}.ev1",
                            f"{tg}.b.h{h-1}.done"]
                if h == 0 and si <= 4:
                    stw.append(f"{tg}.nb.bias")
                if h > 0 and si <= 4:
                    half = 0 if pp < 2 else 1
                    if (h - 1) % 2 == 1:
                        stw.append(f"{tg}.b.h{h-1}.sme{half}")
                    else:
                        stw.append(f"{tg}.b.h{h-1}.rbe{half}")

                def emit_st(e, j=j, q0=q0, n=n, stp=stp, hof=hof, ch=ch):
                    return e.matmul(
                        stp[:, 0:n],
                        KT[hof:hof + 64, ch * T + j * 128:ch * T + j * 128 + 128],
                        QT[hof:hof + 64, ch * T + q0:ch * T + q0 + n],
                        start=True, stop=True)
                p.step("t", emit_st, waits=tuple(stw), sig=f"{ht}.st.{j}.{q0}")

            def emit_exp_step(si):
                j, q0, n = subs[si - 1]
                pp = (si - 1) % 4
                stp = PSA[:, pp * 512:pp * 512 + 512]
                expw = [f"{ht}.st.{j}.{q0}"]
                if h == 0 and si == 1:
                    expw.append(f"{tg}.nb.bias")
                if h > 0 and si == 1:
                    expw.append(f"{tg}.b.h{h-1}.ev1")
                for back in (1, 2, 3):
                    if si > back:
                        jp, qp, np_ = subs[si - 1 - back]
                        if q0 < qp + np_ and qp < q0 + n:
                            expw.append(f"{ht}.pv.{jp}.{qp}")
                p.step("s", lambda e, q0=q0, n=n, stp=stp, h=h: e.activation(
                    PT[:, q0:q0 + n], stp[:, 0:n], AF.Exp,
                    bias=BIASV[:, h:h + 1], scale=SCALE),
                    waits=tuple(expw), sig=f"{ht}.exp.{j}.{q0}")
                prev_exp[pp] = f"{ht}.exp.{j}.{q0}"
                if q0 == j * 128:
                    p.step("v", lambda e, j=j: e.tensor_mul(
                        PT[:, j * 128:j * 128 + 128], PT[:, j * 128:j * 128 + 128],
                        CMSK[:, :]),
                        waits=(f"{ht}.exp.{j}.{q0}", "pr.cmsk"), sig=f"{ht}.msk.{j}")
                    mask_of[si] = f"{ht}.msk.{j}"

            def emit_pv_step(si):
                j, q0, n = subs[si - 1]

                def emit_pv(e, j=j, q0=q0, n=n, h=h):
                    inst = None
                    for (a, b) in segments(q0, n):
                        g = a // 512
                        inst = e.matmul(
                            PSB[0:65, a:b],
                            VT[:, j * 1040 + h * 65:j * 1040 + h * 65 + 65],
                            PT[:, a:b],
                            start=(pvf[g] == (j, q0, a)), stop=(pvl[g] == (j, q0, a)))
                    return inst
                pvw = [f"{ht}.exp.{j}.{q0}", mask_of.get(si)]
                if si == 1:
                    pvw.append(f"{tg}.a.v.ev{NT-1}")
                    if h > 0:
                        pvw.append(f"{tg}.b.h{h-1}.ev1")
                    else:
                        pvw.append(f"{tg}.nk.red7")
                p.step("t", emit_pv, waits=tuple(pvw), sig=f"{ht}.pv.{j}.{q0}")
                for (a, b) in segments(q0, n):
                    g = a // 512
                    if pvl[g] == (j, q0, a):
                        p.step("v", lambda e, g=g: e.reciprocal(
                            RECF[64:65, 0:512], PSB[64:65, g * 512:(g + 1) * 512]),
                            waits=(f"{ht}.pv.{j}.{q0}",), sig=f"{ht}.rc{g}",
                            hazard=True)
                        p.step("v", lambda e, g=g: e.tensor_copy(
                            PT[64:65, g * 512:(g + 1) * 512], RECF[64:65, 0:512]),
                            waits=(), sig=f"{ht}.rcb{g}")

            for si in range(1, NSUB + 1):
                emit_st_step(si)
                if si >= 4:
                    emit_pv_step(si - 3)
                if si >= 3:
                    emit_exp_step(si - 2)
            emit_pv_step(NSUB - 2)
            emit_exp_step(NSUB - 1)
            emit_pv_step(NSUB - 1)
            emit_exp_step(NSUB)
            emit_pv_step(NSUB)
            jN, qN, _ = subs[NSUB - 1]
            last_pv = f"{ht}.pv.{jN}.{qN}"
            p.alias(f"{ht}.pv_last", last_pv)

            for hh in range(2):
                def emit_rb(e, hh=hh):
                    inst = None
                    for nn in range(2):
                        c0 = hh * 1024 + nn * 512
                        inst = e.matmul(PSA[0:64, c0:c0 + 512],
                                        CONE[64:65, 0:64], PT[64:65, c0:c0 + 512],
                                        start=True, stop=True)
                    return inst
                p.step("t", emit_rb,
                       waits=(f"{ht}.rcb{2*hh}", f"{ht}.rcb{2*hh+1}",
                              prev_exp[2 * hh], prev_exp[2 * hh + 1]),
                       sig=f"{ht}.rb{hh}")
                p.step("v", lambda e, hh=hh: e.tensor_copy(
                    PT[0:64, hh * 1024:(hh + 1) * 1024],
                    PSA[0:64, hh * 1024:(hh + 1) * 1024]),
                    waits=(f"{ht}.rb{hh}",), sig=f"{ht}.rbe{hh}")
                dst = OT if hof == 0 else STG
                dc0 = (ch * T if hof == 0 else 0) + hh * 1024
                evw = [f"{ht}.rbe{hh}"]
                if s == 0 and h == 0 and hh == 0:
                    evw += [f"fv.fst{k}" for k in range(NC8)]
                    evw += [f"wp.st{k}" for k in range(NC8)]
                if hof == 64 and h > 1:
                    evw.append(f"{tg}.b.h{h-2}.shm{hh}")
                p.step("v", lambda e, hh=hh, dst=dst, dc0=dc0: e.tensor_mul(
                    dst[0:64, dc0:dc0 + 1024],
                    PSB[0:64, hh * 1024:(hh + 1) * 1024],
                    PT[0:64, hh * 1024:(hh + 1) * 1024]),
                    waits=tuple(evw), sig=f"{ht}.ev{hh}")
            p.alias(f"{ht}.rbe", f"{ht}.rbe1")
            p.alias(f"{ht}.ev", f"{ht}.ev1")
            if hof == 0:
                p.alias(f"{ht}.done", f"{ht}.ev1")
            else:
                for hh in range(2):
                    def emit_sh(e, hh=hh):
                        inst = None
                        for nn in range(2):
                            c0 = hh * 1024 + nn * 512
                            inst = e.matmul(PSA[64:128, c0:c0 + 512],
                                            CIDB[0:64, 0:64], STG[0:64, c0:c0 + 512],
                                            start=True, stop=True)
                        return inst
                    p.step("t", emit_sh,
                           waits=(f"{ht}.ev{hh}", "pr.cidb", f"{ht}.rbe{hh}"),
                           sig=f"{ht}.shm{hh}")
                    p.step("v", lambda e, hh=hh, ch=ch: e.tensor_copy(
                        OT[64:128, ch * T + hh * 1024:ch * T + (hh + 1) * 1024],
                        PSA[64:128, hh * 1024:(hh + 1) * 1024]),
                        waits=(f"{ht}.shm{hh}",), sig=f"{ht}.sme{hh}")
                p.alias(f"{ht}.shm", f"{ht}.shm1")
                p.alias(f"{ht}.sme", f"{ht}.sme1")
                p.alias(f"{ht}.done", f"{ht}.sme1")

        if lvl in (3, 35, 4):
            break
        for q in range(NQ if lvl != 5 else 1):
            qt = f"{tg}.c.q{q}"
            dt_ = f"{tg}.d.q{q}"
            if q == 0:
                dma("y", WV[:, :], rows_ap(s_ffn[2]),
                    waits=tuple(f"fv.fst{k}" for k in range(NC8)) +
                          (f"{tg}.b.h15.pv_last",), sig=f"{tg}.lw.fv")
            for i4 in range(4):
                i = q * 4 + i4
                pp = i4 % 2
                xin_prev = (f"{qt}.s1.i{i4-2}" if i4 >= 2 else
                            (f"{tg}.c.q{q-1}.s1.i{i4+2}" if q >= 1 else
                             (f"{pg}.c.q3.s1.i{i4+2}" if s > 0 else None)))
                kt_free = f"{tg}.b.h15.pv_last" if (q == 0 and i4 < 2) else None
                dma("y", XIN[:, pp * 1024:(pp + 1) * 1024], x_d[s, i * 128:(i + 1) * 128, :],
                    waits=(xin_prev, kt_free), sig=f"{qt}.xin{i4}")
                PJ = PSA[:, (i4 % 2) * 1024:(i4 % 2) * 1024 + 1024]
                prev_use = (f"{qt}.s1.i{i4-2}" if i4 >= 2 else
                            (f"{tg}.c.q{q-1}.s1.i{i4+2}" if q >= 1 else None))

                def emit_pj(e, i=i, PJ=PJ):
                    for k in range(NC8):
                        for nn in range(2):
                            e.matmul(PJ[:, nn * 512:(nn + 1) * 512],
                                     OT[:, k * T + i * 128:k * T + i * 128 + 128],
                                     W[2][:, k * 1024 + nn * 512:k * 1024 + (nn + 1) * 512],
                                     start=(k == 0), stop=False)
                    inst = None
                    for nn in range(2):
                        inst = e.matmul(PJ[:, nn * 512:(nn + 1) * 512],
                                        CONE[0:1, 0:128], BPB[0:1, nn * 512:(nn + 1) * 512],
                                        start=False, stop=True)
                    return inst
                pjw = [f"{tg}.lw.p0", "pr.bpb", prev_use]
                if q == 0 and i4 == 0:
                    pjw += [f"{tg}.b.h15.sme1", f"{tg}.b.h14.ev1"]
                p.step("t", emit_pj, waits=tuple(pjw), sig=f"{qt}.pj{i4}")
                p.step("v", lambda e, i4=i4, PJ=PJ: e.scalar_tensor_tensor(
                    LNIN[:, i4 * 1024:(i4 + 1) * 1024], PJ, 1.0, XIN[:, (i4 % 2) * 1024:(i4 % 2 + 1) * 1024],
                    ALU.mult, ALU.add, accum_out=STAT[:, i4:i4 + 1]),
                    waits=(f"{qt}.pj{i4}", f"{qt}.xin{i4}",
                           f"{tg}.d.q{q-1}.sq2.i{i4}" if q >= 1 else
                           (f"{pg}.d.q3.sq2.i{i4}" if s > 0 else None),
                           f"{tg}.lx1.7" if q == 0 and i4 == 0 else None),
                    sig=f"{qt}.s1.i{i4}", hazard=True)
                p.step("s", lambda e, i4=i4: e.activation(
                    XRQ[:, 0:1024], LNIN[:, i4 * 1024:(i4 + 1) * 1024], AF.Square,
                    accum_out=STAT[:, 4 + i4:5 + i4]),
                    waits=(f"{qt}.s1.i{i4}",
                           f"{tg}.d.q{q-1}.rmm3" if q >= 1 and i4 == 0 else None),
                    sig=f"{qt}.sq.i{i4}")
            p.step("v", lambda e: e.tensor_scalar(STAT[:, 16:20], STAT[:, 0:4],
                                                  1.0 / C, None, ALU.mult),
                   waits=(f"{qt}.s1.i3",), sig=f"{qt}.mean", hazard=True)
            p.step("v", lambda e: e.tensor_mul(STAT[:, 20:24], STAT[:, 16:20],
                                               STAT[:, 16:20]), waits=(f"{qt}.mean",), sig=f"{qt}.m2", hazard=True)
            p.step("v", lambda e: e.scalar_tensor_tensor(
                STAT[:, 24:28], STAT[:, 4:8], 1.0 / C, STAT[:, 20:24],
                ALU.mult, ALU.subtract),
                waits=(f"{qt}.sq.i3", f"{qt}.m2"), sig=f"{qt}.var", hazard=True)
            p.step("s", lambda e: e.activation(STAT[:, 28:32], STAT[:, 24:28],
                                               AF.Sqrt, bias=EPSV[:, 0:1]),
                   waits=(f"{qt}.var", "pr.eps"), sig=f"{qt}.lnv")
            p.step("v", lambda e: e.reciprocal(STAT[:, 8:12], STAT[:, 28:32]),
                   waits=(f"{qt}.lnv",), sig=f"{qt}.rstd", hazard=True)
            p.step("v", lambda e: e.tensor_mul(STAT[:, 12:16], STAT[:, 16:20],
                                               STAT[:, 8:12]),
                   waits=(f"{qt}.rstd",), sig=f"{qt}.mr", hazard=True)
            for i4 in range(4):
                p.step("v", lambda e, i4=i4: e.tensor_scalar(
                    X1Q[:, i4 * 1024:(i4 + 1) * 1024], LNIN[:, i4 * 1024:(i4 + 1) * 1024],
                    STAT[:, 8 + i4:9 + i4], STAT[:, 12 + i4:13 + i4],
                    ALU.mult, ALU.subtract),
                    waits=(f"{qt}.mr",
                           f"{tg}.d.q{q-1}.s2.i{i4}" if q >= 1 else
                           (f"{pg}.d.q3.s2.i{i4}" if s > 0 else None)),
                    sig=f"{qt}.x1.i{i4}", hazard=True)
            if q == 0:
                p.step("v", lambda e: e.memset(X1T3[:, :, 0:1], 0.0),
                       waits=(), sig=f"{qt}.pad", hazard=True)
            else:
                p.step("v", lambda e: e.tensor_copy(X1T3[:, :, 0:1], BNDRY[:, 0:8]),
                       waits=(f"{tg}.c.q{q-1}.bnd",), sig=f"{qt}.pad", hazard=True)
            for k in range(NC8):
                ppk = k % 4

                def emit_tp(e, k=k, ppk=ppk, i0=0, i1=4):
                    inst = None
                    for i4 in range(i0, i1):
                        inst = e.transpose(
                            PSB[:, ppk * 512 + i4 * 128:ppk * 512 + (i4 + 1) * 128],
                            X1Q[:, i4 * 1024 + k * 128:i4 * 1024 + k * 128 + 128],
                            CIDF[:, :])
                    return inst
                tpw = ["pr.cid", f"{qt}.x1.i1",
                       f"{qt}.tpe.k{k-4}" if k >= 4 else None]
                if k <= 3:
                    if q == 0 and s == 0:
                        tpw.append(None)
                    else:
                        tpw.append(f"{tg}.c.q{q-1}.tpe.k{4+k}" if q >= 1 else
                                   f"{pg}.c.q3.tpe.k{4+k}")
                    if q == 0:
                        tpw.append(f"{tg}.b.h15.ev1")
                from functools import partial
                p.step("t", partial(emit_tp, i0=0, i1=2),
                       waits=tuple(w for w in tpw if w), sig=f"{qt}.tpa.k{k}")
                p.step("t", partial(emit_tp, i0=2, i1=4),
                       waits=(f"{qt}.x1.i3",), sig=f"{qt}.tp.k{k}")
                p.step("s", lambda e, k=k, ppk=ppk: e.copy(
                    X1T[:, k * 513 + 1:k * 513 + 513],
                    PSB[:, ppk * 512:(ppk + 1) * 512]),
                    waits=(f"{qt}.tp.k{k}", f"{qt}.pad",
                           f"{tg}.d.q{q-1}.mixk{k}" if q >= 1 else None),
                    sig=f"{qt}.tpe.k{k}")
            for k in range(NC8):
                p.step("v", lambda e, k=k: e.tensor_sub(
                    MIXT[:, :], X1T[:, k * 513 + 1:k * 513 + 513],
                    X1T[:, k * 513:k * 513 + 512]),
                    waits=(f"{qt}.tpe.k{k}",), sig=f"{dt_}.mt{k}", hazard=True)
                p.step("v", lambda e, k=k: e.scalar_tensor_tensor(
                    XRQ[:, k * 512:(k + 1) * 512], MIXT[:, :],
                    MUR[:, k:k + 1], X1T[:, k * 513:k * 513 + 512],
                    ALU.mult, ALU.add),
                    waits=("pr.mur", f"{dt_}.mt{k}",
                           f"{qt}.sq.i3" if k == 0 else None,
                           f"{tg}.d.q{q-1}.rmm3" if q >= 1 and k == 0 else
                           (f"{pg}.d.q3.rmm3" if s > 0 and q == 0 and k == 0 else None)),
                    sig=f"{dt_}.mixr{k}")
                p.step("v", lambda e, k=k: e.scalar_tensor_tensor(
                    XKQ[:, k * 512:(k + 1) * 512], MIXT[:, :],
                    MUK[:, k:k + 1], X1T[:, k * 513:k * 513 + 512],
                    ALU.mult, ALU.add),
                    waits=("pr.muk",
                           f"{tg}.d.q{q-1}.kmm7" if q >= 1 and k == 0 else
                           (f"{pg}.d.q3.kmm7" if s > 0 and q == 0 and k == 0 else None)),
                    sig=f"{dt_}.mixk{k}")
            p.step("v", lambda e: e.tensor_copy(BNDRY[:, 0:8], X1T3[:, :, 512:513]),
                   waits=(f"{qt}.tpe.k7",), sig=f"{qt}.bnd", hazard=True)
            for i4 in range(4):
                RP = PSA[:, (i4 % 2) * 1024:(i4 % 2) * 1024 + 1024]
                prev_use = f"{dt_}.sig{i4-2}" if i4 >= 2 else f"{qt}.s1.i{i4+2}"

                def emit_r(e, i4=i4, RP=RP, k0=0, k1=NC8):
                    inst = None
                    for k in range(k0, k1):
                        for nn in range(2):
                            inst = e.matmul(
                                RP[:, nn * 512:(nn + 1) * 512],
                                XRQ[:, k * 512 + i4 * 128:k * 512 + i4 * 128 + 128],
                                W[0][:, k * 1024 + nn * 512:k * 1024 + (nn + 1) * 512],
                                start=(k == 0), stop=(k == NC8 - 1))
                    return inst
                if i4 == 0:
                    from functools import partial
                    p.step("t", partial(emit_r, k0=0, k1=4),
                           waits=(f"{tg}.lw.fr", prev_use) +
                           tuple(f"{dt_}.mixr{k}" for k in range(4)),
                           sig=f"{dt_}.rmmA")
                    p.step("t", partial(emit_r, k0=4, k1=NC8),
                           waits=tuple(f"{dt_}.mixr{k}" for k in range(4, NC8)),
                           sig=f"{dt_}.rmm{i4}")
                else:
                    p.step("t", emit_r, waits=(f"{tg}.lw.fr", prev_use) +
                           tuple(f"{dt_}.mixr{k}" for k in range(NC8)),
                           sig=f"{dt_}.rmm{i4}")
                p.step("s", lambda e, i4=i4, RP=RP: e.activation(
                    THQ[:, i4 * 1024:(i4 + 1) * 1024], RP, AF.Sigmoid),
                    waits=(f"{dt_}.rmm{i4}",
                           f"{tg}.b.h15.sme1" if q == 0 and i4 == 0 else None,
                           f"{tg}.d.q{q-1}.g.i{i4}" if q >= 1 else
                           (f"{pg}.d.q3.g.i{i4}" if s > 0 else None)),
                    sig=f"{dt_}.sig{i4}")
            for m in range(NC8):
                KP = PSB[:, (m % 2) * 512:(m % 2) * 512 + 512]
                prev_use = (f"{dt_}.krelu{m-2}" if m >= 2 else f"{qt}.tpe.k{4+m}")

                def emit_k(e, m=m, KP=KP):
                    inst = None
                    for k in range(NC8):
                        inst = e.matmul(
                            KP, W[1][:, k * 1024 + m * 128:k * 1024 + m * 128 + 128],
                            XKQ[:, k * 512:(k + 1) * 512],
                            start=(k == 0), stop=(k == NC8 - 1))
                    return inst
                p.step("t", emit_k, waits=(f"{tg}.lw.fk", prev_use) +
                       tuple(f"{dt_}.mixk{k}" for k in range(NC8)),
                       sig=f"{dt_}.kmm{m}")
                p.step("s", lambda e, m=m, KP=KP: e.activation(
                    KAQ[:, m * 512:(m + 1) * 512], KP, AF.Relu),
                    waits=(f"{dt_}.kmm{m}",
                           f"{tg}.d.q{q-1}.vo3" if q >= 1 and m == 0 else
                           (f"{pg}.d.q3.vo3" if s > 0 and q == 0 and m == 0 else None)),
                    sig=f"{dt_}.krelu{m}")
                p.step("s", lambda e, m=m: e.activation(
                    KAQ[:, m * 512:(m + 1) * 512], KAQ[:, m * 512:(m + 1) * 512],
                    AF.Square),
                    waits=(f"{dt_}.krelu{m}",), sig=f"{dt_}.ksq{m}")
            for i4 in range(4):
                VP = PSA[:, (i4 % 2) * 1024:(i4 % 2) * 1024 + 1024]
                prev_use = f"{dt_}.g.i{i4-2}" if i4 >= 2 else f"{dt_}.sig{i4+2}"

                def emit_vo(e, i4=i4, VP=VP):
                    inst = None
                    for k in range(NC8):
                        for nn in range(2):
                            inst = e.matmul(
                                VP[:, nn * 512:(nn + 1) * 512],
                                KAQ[:, k * 512 + i4 * 128:k * 512 + i4 * 128 + 128],
                                WV[:, k * 1024 + nn * 512:k * 1024 + (nn + 1) * 512],
                                start=(k == 0), stop=(k == NC8 - 1))
                    return inst
                p.step("t", emit_vo, waits=(f"{tg}.lw.fv", prev_use) +
                       tuple(f"{dt_}.ksq{k}" for k in range(NC8)),
                       sig=f"{dt_}.vo{i4}")
                p.step("v", lambda e, i4=i4, VP=VP: e.tensor_mul(
                    THQ[:, i4 * 1024:(i4 + 1) * 1024], THQ[:, i4 * 1024:(i4 + 1) * 1024],
                    VP), waits=(f"{dt_}.vo{i4}",), sig=f"{dt_}.g.i{i4}", hazard=True)
                p.step("v", lambda e, i4=i4: e.scalar_tensor_tensor(
                    LNIN[:, i4 * 1024:(i4 + 1) * 1024], THQ[:, i4 * 1024:(i4 + 1) * 1024],
                    1.0, X1Q[:, i4 * 1024:(i4 + 1) * 1024], ALU.mult, ALU.add,
                    accum_out=STAT[:, 32 + i4:33 + i4]),
                    waits=(f"{dt_}.g.i{i4}",), sig=f"{dt_}.s2.i{i4}", hazard=True)
                p.step("s", lambda e, i4=i4: e.activation(
                    XRQ[:, 0:1024], LNIN[:, i4 * 1024:(i4 + 1) * 1024], AF.Square,
                    accum_out=STAT[:, 36 + i4:37 + i4]),
                    waits=(f"{dt_}.s2.i{i4}", f"{dt_}.rmm3" if i4 == 0 else None),
                    sig=f"{dt_}.sq2.i{i4}")
            p.step("v", lambda e: e.tensor_scalar(STAT[:, 40:44], STAT[:, 32:36],
                                                  1.0 / C, None, ALU.mult),
                   waits=(f"{dt_}.s2.i3",), sig=f"{dt_}.mean2", hazard=True)
            p.step("v", lambda e: e.tensor_mul(STAT[:, 44:48], STAT[:, 40:44],
                                               STAT[:, 40:44]), waits=(f"{dt_}.mean2",), sig=f"{dt_}.m22", hazard=True)
            p.step("v", lambda e: e.scalar_tensor_tensor(
                STAT[:, 48:52], STAT[:, 36:40], 1.0 / C, STAT[:, 44:48],
                ALU.mult, ALU.subtract),
                waits=(f"{dt_}.sq2.i3", f"{dt_}.m22"), sig=f"{dt_}.var2", hazard=True)
            p.step("s", lambda e: e.activation(STAT[:, 52:56], STAT[:, 48:52],
                                               AF.Sqrt, bias=EPSV[:, 0:1]),
                   waits=(f"{dt_}.var2", "pr.eps"), sig=f"{dt_}.lnv2")
            p.step("v", lambda e: e.reciprocal(STAT[:, 56:60], STAT[:, 52:56]),
                   waits=(f"{dt_}.lnv2",), sig=f"{dt_}.rstd2", hazard=True)
            p.step("v", lambda e: e.tensor_mul(STAT[:, 60:64], STAT[:, 40:44],
                                               STAT[:, 56:60]),
                   waits=(f"{dt_}.rstd2",), sig=f"{dt_}.mr2", hazard=True)
            for i4 in range(4):
                i = q * 4 + i4
                pp = i4 % 2
                prev_dma = (f"{dt_}.out{i4-2}" if i4 >= 2 else
                            (f"{tg}.d.q{q-1}.out{i4+2}" if q >= 1 else
                             (f"{pg}.d.q3.out{i4+2}" if s > 0 else None)))
                p.step("v", lambda e, i4=i4, pp=pp: e.tensor_scalar(
                    OUTQ[:, pp * 1024:(pp + 1) * 1024], LNIN[:, i4 * 1024:(i4 + 1) * 1024],
                    STAT[:, 56 + i4:57 + i4], STAT[:, 60 + i4:61 + i4],
                    ALU.mult, ALU.subtract),
                    waits=(f"{dt_}.mr2", prev_dma), sig=f"{dt_}.ln2aff.i{i4}")
                dma("y", y_d[s, i * 128:(i + 1) * 128, :], OUTQ[:, pp * 1024:(pp + 1) * 1024],
                    waits=(f"{dt_}.ln2aff.i{i4}",), sig=f"{dt_}.out{i4}")

    DSQ = int(os.environ.get('MK_DBG_SEQ', '0'))
    if "dbg_qt" in dbg_d and f"s{DSQ}.a.q.ev1.7" in p.events:
        dma("g", dbg_d["dbg_qt"], QT[:, :], waits=(f"s{DSQ}.a.q.ev1.7",), sig="dbg.qt")
    if "dbg_kt" in dbg_d and f"s{DSQ}.a.k.ev1.7" in p.events:
        dma("g", dbg_d["dbg_kt"], KT[:, :], waits=(f"s{DSQ}.a.k.ev1.7",), sig="dbg.kt")
    if "dbg_vt" in dbg_d and f"s{DSQ}.a.v.ev{NT-1}" in p.events:
        dma("g", dbg_d["dbg_vt"], VT[:, :], waits=(f"s{DSQ}.a.v.ev{NT-1}",), sig="dbg.vt")
    if "dbg_ot" in dbg_d:
        lastd = [f"s{DSQ}.b.h{hh_}.done" for hh_ in range(H) if f"s{DSQ}.b.h{hh_}.done" in p.events]
        lastv = [f"s{DSQ}.b.h{hh_}.ev1" for hh_ in range(H) if f"s{DSQ}.b.h{hh_}.ev1" in p.events]
        if lastd:
            dma("g", dbg_d["dbg_ot"], OT[:, :],
                waits=(lastd[-1], lastv[-1]), sig="dbg.ot")
    if "dbg_x1" in dbg_d:
        for q in range(NQ):
            for i4 in range(4):
                i = q * 4 + i4
                if f"s{DSQ}.c.q{q}.x1.i{i4}" not in p.events:
                    continue
                dma("g", dbg_d["dbg_x1"][i * 128:(i + 1) * 128, :],
                    X1Q[:, i4 * 1024:(i4 + 1) * 1024],
                    waits=(f"s{DSQ}.c.q{q}.x1.i{i4}",), sig=f"dbg.x1.{i}")
    if "dbg_s1" in dbg_d and f"s{DSQ}.c.q0.sq.i3" in p.events:
        dma("g", dbg_d["dbg_s1"], LNIN[:, :], waits=(f"s{DSQ}.c.q0.sq.i3",), sig="dbg.s1")
    if "dbg_stat" in dbg_d and f"s{DSQ}.c.q0.mr" in p.events:
        dma("g", dbg_d["dbg_stat"], STAT[:, :], waits=(f"s{DSQ}.c.q0.mr",), sig="dbg.stat")
    if "dbg_x1t" in dbg_d and f"s{DSQ}.c.q0.bnd" in p.events:
        dma("g", dbg_d["dbg_x1t"], X1T[:, :], waits=(f"s{DSQ}.c.q0.bnd",), sig="dbg.x1t")
    if "dbg_qn" in dbg_d and f"s{DSQ}.nb.add" in p.events:
        dma("g", dbg_d["dbg_qn"], QN2[:, :], waits=(f"s{DSQ}.nb.add",), sig="dbg.qn")
        dma("g", dbg_d["dbg_kn"], KN2[:, :], waits=(f"s{DSQ}.nb.add",), sig="dbg.kn")
    if "dbg_bias" in dbg_d and f"s{DSQ}.nb.bias" in p.events:
        dma("y", dbg_d["dbg_bias"], BIASV[:, :], waits=(f"s{DSQ}.nb.bias",), sig="dbg.bias")

    import contextlib
    stack = contextlib.ExitStack()
    sems = {name: stack.enter_context(nc.semaphore(f"sem_{name}"))
            for name in ("t", "v", "s")}
    for ln in p.lane_names():
        sems[ln] = stack.enter_context(nc.semaphore(f"sem_{ln}"))
    finals = dict(p.lane_count)

    def final_wait(e):
        for ln, v in finals.items():
            if v:
                e.wait_ge(sems[ln], v)
        return None
    p.script["y"].append(([], final_wait, False, None))
    with nc.Block() as block:
        p.materialize(block, sems)
    stack.close()
    return nc, sorted(dbg_d.keys())


def _prep_core(inp, core):
    b = [2 * core, 2 * core + 1]
    x = np.asarray(inp["L0_attn_x"])
    f32 = np.float32
    bf = ml_dtypes.bfloat16
    d = {}
    d["xT"] = np.ascontiguousarray(x[b].transpose(0, 2, 1)).astype(f32)
    d["x_tok"] = np.ascontiguousarray(x[b]).astype(f32)
    for nm, key in (("wq_cat", "L0_attn_Wq"), ("wk_cat", "L0_attn_Wk"),
                    ("wv_cat", "L0_attn_Wv")):
        Wm = np.asarray(inp[key])
        d[nm] = np.ascontiguousarray(Wm.transpose(1, 0, 2).reshape(C, C)).astype(f32)
    for nm, key in (("aq_t", "L0_attn_Aq"), ("ak_t", "L0_attn_Ak"), ("av_t", "L0_attn_Av")):
        A = np.asarray(inp[key])
        d[nm] = np.ascontiguousarray(A.transpose(2, 0, 1).reshape(R, H * C)).astype(f32)
    for nm, key in (("bq_r", "L0_attn_Bq"), ("bk_r", "L0_attn_Bk"), ("bv_r", "L0_attn_Bv")):
        B = np.asarray(inp[key])
        d[nm] = np.ascontiguousarray(B.transpose(1, 0, 2).reshape(R, H * HD)).astype(f32)
    d["wp"] = np.ascontiguousarray(np.asarray(inp["L0_attn_Wp"])).astype(f32)
    d["ap_t"] = np.ascontiguousarray(np.asarray(inp["L0_attn_Ap"]).T).astype(f32)
    d["bp_m"] = np.ascontiguousarray(np.asarray(inp["L0_attn_Bp"])).astype(f32)
    d["bp_b"] = np.ascontiguousarray(np.asarray(inp["L0_attn_bp"]).reshape(1, C)).astype(f32)
    d["fr_t"] = np.ascontiguousarray(np.asarray(inp["L0_ffn_W_r"]).T).astype(f32)
    d["fk_t"] = np.ascontiguousarray(np.asarray(inp["L0_ffn_W_k"]).T).astype(f32)
    d["fv_t"] = np.ascontiguousarray(np.asarray(inp["L0_ffn_W_v"]).T).astype(f32)
    d["mu_r_c"] = np.ascontiguousarray(np.asarray(inp["L0_ffn_mu_r"]).reshape(8, 128).T).astype(f32)
    d["mu_k_c"] = np.ascontiguousarray(np.asarray(inp["L0_ffn_mu_k"]).reshape(8, 128).T).astype(f32)
    d["c_idf"] = np.eye(128, dtype=f32)
    d["c_idb"] = np.eye(128, dtype=bf)
    sg, qg = np.meshgrid(np.arange(128), np.arange(128), indexing="ij")
    d["c_mask"] = (qg >= sg).astype(bf)
    d["c_ones"] = np.ones((128, 128), dtype=bf)
    c2 = np.zeros((128, 33), dtype=bf)
    c2[0:64, 0] = 1
    c2[64:128, 32] = 1
    d["c_ones2"] = c2
    return d


_CACHE = {}


def kernel(**inputs):
    dbg = inputs.pop("_dbg", ())
    stop = inputs.pop("_stop", "full")
    key = (tuple(sorted(dbg)), stop)
    if key not in _CACHE:
        _CACHE[key] = build_program(dbg, stop)
    nc, dbg_names = _CACHE[key]
    in_maps = [_prep_core(inputs, c) for c in range(8)]
    res = run_bass_kernel_spmd(nc, in_maps, core_ids=list(range(8)))
    out = np.empty((16, T, C), dtype=np.float32)
    for c in range(8):
        out[2 * c:2 * c + 2] = res.results[c]["y"]
    if dbg:
        return out, {n: res.results[0][n] for n in dbg_names}
    return out


# version 12 (final)
# speedup vs baseline: 1.0187x; 1.0037x over previous
import os
import numpy as np
import ml_dtypes
import concourse.bass as bass
import concourse.mybir as mybir
from concourse.bass_utils import run_bass_kernel_spmd

F32 = mybir.dt.float32
BF16 = mybir.dt.bfloat16
AF = mybir.ActivationFunctionType
ALU = mybir.AluOpType
AX = mybir.AxisListType

H, HD, T, C, R = 16, 64, 2048, 1024, 8
NS = 2
NC8 = 8
NT = 16
NQ = 4
SCALE = 0.125
EPS = 1e-5


class Plan:

    ENGINES = ("t", "v", "s", "y", "g")
    NLANES = {"y": 8, "g": 4}

    def __init__(self):
        self.script = {e: [] for e in self.ENGINES}
        self.count = {e: 0 for e in ("t", "v", "s")}
        self.lane_count = {}
        self.lane_rr = {"y": 0, "g": 0}
        self.events = {}
        self.hazard = set()

    def lane_names(self):
        return [f"{e}{i}" for e in ("y", "g") for i in range(self.NLANES[e])]

    def step(self, eng, emit, waits=(), sig=None, hazard=False):
        w = []
        for name in waits:
            if name is None:
                continue
            ev = self.events[name]
            if ev[0] != eng or ev[0] not in ("t", "v", "s") or name in self.hazard:
                w.append(ev)
        if hazard and sig is not None:
            self.hazard.add(sig)
        lane = None
        if eng in ("y", "g"):
            lane = f"{eng}{self.lane_rr[eng] % self.NLANES[eng]}"
            self.lane_rr[eng] += 1
            prev = self.lane_count.get(lane, 0)
            if prev:
                w.append((lane, prev))
            self.lane_count[lane] = prev + 16
            if sig is not None:
                assert sig not in self.events, sig
                self.events[sig] = (lane, prev + 16)
        elif sig is not None:
            self.count[eng] += 1
            assert sig not in self.events, sig
            self.events[sig] = (eng, self.count[eng])
        self.script[eng].append((w, emit, sig is not None or lane is not None, lane))

    def alias(self, new, old):
        self.events[new] = self.events[old]

    def materialize(self, block, sems):
        def mk(ename):
            hiwater = {}

            def body(eng):
                for waits, emit, has_sig, lane in self.script[ename]:
                    for (pe, val) in waits:
                        if hiwater.get(pe, 0) >= val:
                            continue
                        hiwater[pe] = val
                        eng.wait_ge(sems[pe], val)
                    inst = emit(eng)
                    if lane is not None:
                        inst.then_inc(sems[lane], 16)
                    elif has_sig:
                        inst.then_inc(sems[ename], 1)
            return body

        for ename in self.ENGINES:
            getattr(block, {"t": "tensor", "v": "vector", "s": "scalar",
                            "y": "sync", "g": "gpsimd"}[ename])(mk(ename))


def build_program(dbg=(), stop="full"):
    serial = stop.endswith("S")
    stop = stop.rstrip("S")
    lvl = {"pro0": -3, "pro1": -2, "pro2": -1, "pro": 0, "a": 1, "norm": 2,
           "b1": 3, "b2": 35, "b": 4, "cd1": 5, "full": 9}[stop]
    nc = bass.Bass("TRN2", target_bir_lowering=False,
                   detect_race_conditions=False)

    def din(name, shape, dt=F32):
        return nc.dram_tensor(name, shape, dt, kind="ExternalInput").ap()

    xT_d = din("xT", [NS, C, T])
    x_d = din("x_tok", [NS, T, C])
    wq_d, wk_d, wv_d = din("wq_cat", [C, C]), din("wk_cat", [C, C]), din("wv_cat", [C, C])
    aq_d, ak_d, av_d = din("aq_t", [R, H * C]), din("ak_t", [R, H * C]), din("av_t", [R, H * C])
    bq_d, bk_d, bv_d = din("bq_r", [R, C]), din("bk_r", [R, C]), din("bv_r", [R, C])
    wp_d = din("wp", [C, C])
    ap_d = din("ap_t", [R, C])
    bpm_d = din("bp_m", [R, C])
    bpb_d = din("bp_b", [1, C])
    fr_d, fk_d, fv_d = din("fr_t", [C, C]), din("fk_t", [C, C]), din("fv_t", [C, C])
    mur_d, muk_d = din("mu_r_c", [128, 8]), din("mu_k_c", [128, 8])
    cidf_d = din("c_idf", [128, 128])
    cidb_d = din("c_idb", [128, 128], BF16)
    cmask_d = din("c_mask", [128, 128], BF16)
    cones_d = din("c_ones", [128, 128], BF16)
    cones2_d = din("c_ones2", [128, 33], BF16)

    y_d = nc.dram_tensor("y", [NS, T, C], F32, kind="ExternalOutput").ap()

    dbg_d = {}
    for name, shape in (("dbg_qt", [128, NC8 * T]), ("dbg_kt", [128, NC8 * T]),
                        ("dbg_vt", [128, 16 * 1040]), ("dbg_ot", [128, NC8 * T]),
                        ("dbg_x1", [T, C]), ("dbg_bias", [128, 16]),
                        ("dbg_s1", [128, 4096]), ("dbg_stat", [128, 64]),
                        ("dbg_qn", [128, 16]), ("dbg_kn", [128, 16]),
                        ("dbg_x1t", [128, NC8 * 513])):
        if name in dbg:
            dbg_d[name] = nc.dram_tensor(name, shape, F32, kind="ExternalOutput").ap()

    s_weff = nc.dram_tensor("s_wef", [4, C, C], BF16, kind="Internal").ap()
    s_ffn = nc.dram_tensor("s_ffn", [3, C, C], BF16, kind="Internal").ap()

    base, bend = nc.bump_sbuf(211136)
    assert bend <= 229376, (base, bend)

    def at(name, shape, dt, off, live):
        off = off - 16384 + base
        nbytes = int(np.prod(shape)) // shape[0] * (2 if dt == BF16 else 4)
        assert off + nbytes <= bend, (name, off + nbytes, bend)
        return nc.alloc_sbuf_tensor_at(name, shape, dt, offset=off)

    XT = at("XT", [128, NC8 * 1024], BF16, 16384, "A")
    QT = at("QT", [128, NC8 * T], BF16, 32768, "AB")
    KT = at("KT", [128, NC8 * T], BF16, 65536, "AB")
    VT = at("VT", [128, 16 * 1040], BF16, 98304, "AB")
    WV = at("WV", [128, 8192], BF16, 98304, "CD")
    XKQ = at("XKQ", [128, 4096], BF16, 114688, "CD")
    OT = at("OT", [128, NC8 * T], BF16, 131584, "BC")
    PT = at("PT", [128, T], BF16, 164352, "B")
    STG = at("STG", [128, T], BF16, 168448, "B")
    RECF = at("RECF", [128, 512], F32, 168448, "B")
    LNR = at("LNR", [128, 2048], F32, 16384, "B")
    W0 = at("W0", [128, 8192], BF16, 172544, "W")
    W1 = at("W1", [128, 8192], BF16, 188928, "W")
    W2 = at("W2", [128, 8192], BF16, 205312, "W")
    LNIN = at("LNIN", [128, NQ * 1024], F32, 16384, "CD")
    X1Q = at("X1Q", [128, NQ * 1024], F32, 32768, "CD")
    XRQ = at("XRQ", [128, NC8 * 512], BF16, 49152, "CD")
    X1T = at("X1T", [128, NC8 * 513], BF16, 57344, "CD")
    KAQ = at("KAQ", [128, NC8 * 512], BF16, 65568, "CD")
    XIN = at("XIN", [128, 2048], F32, 73760, "CD")
    OUTQ = at("OUTQ", [128, 2048], F32, 81952, "CD")
    THQ = at("THQ", [128, NQ * 1024], BF16, 164352, "CD")
    AT_BF = at("AT_BF", [R, H * C], BF16, 32768, "P")
    BT_BF = at("BT_BF", [R, C], BF16, 65536, "P")
    WRAW = at("WRAW", [128, 8192], BF16, 131584, "P")
    WSTG = at("WSTG", [128, 8192], BF16, 147968, "P")
    o = 221696
    def small(name, shape, dt):
        nonlocal o
        h = at(name, shape, dt, o, "X")
        o += int(np.prod(shape)) // shape[0] * (2 if dt == BF16 else 4)
        o = (o + 31) & ~31
        return h
    CIDF = small("CIDF", [128, 128], F32)
    CIDB = small("CIDB", [128, 128], BF16)
    CMSK = small("CMSK", [128, 128], BF16)
    CONE = small("CONE", [128, 128], BF16)
    CON2 = small("CON2", [128, 33], BF16)
    BPB = small("BPB", [1, 1024], BF16)
    MUR = small("MUR", [128, 8], F32)
    MUK = small("MUK", [128, 8], F32)
    OMR = small("OMR", [128, 8], F32)
    OMK = small("OMK", [128, 8], F32)
    QN2 = small("QN2", [128, 16], BF16)
    KN2 = small("KN2", [128, 16], BF16)
    BIASV = small("BIASV", [128, 16], F32)
    MIXT = small("MIXT", [128, 512], BF16)
    STAT = small("STAT", [128, 64], F32)
    BNDRY = small("BNDRY", [128, 8], BF16)
    EPSV = small("EPSV", [128, 1], F32)
    assert o <= 227520, o

    W = [W0, W1, W2]
    PSA = nc.alloc_psum_tensor("PSA", [128, 2048], F32)
    PSB = nc.alloc_psum_tensor("PSB", [128, 2048], F32)

    VT4 = VT[:, :].rearrange("p (j h e) -> p j h e", j=16, h=16)
    X1T3 = X1T[:, :].rearrange("p (k x) -> p k x", k=8)

    p = Plan()

    def dma(eng_key, dst, src, waits=(), sig=None):
        p.step(eng_key, lambda e, d=dst, s=src: e.dma_start(d, s), waits, sig)

    def rows_ap(d2):
        return d2.rearrange("(k x) c -> x k c", k=NC8)

    dma("y", CIDF[:, :], cidf_d, sig="pr.cid")
    dma("y", CIDB[:, :], cidb_d, sig="pr.cidb")
    dma("y", CMSK[:, :], cmask_d, sig="pr.cmsk")
    dma("y", CONE[:, :], cones_d, sig="pr.cone")
    dma("y", CON2[:, :], cones2_d, sig="pr.con2")
    dma("y", MUR[:, :], mur_d, sig="pr.mur")
    dma("y", MUK[:, :], muk_d, sig="pr.muk")
    dma("g", BPB[:, :], bpb_d, sig="pr.bpb")
    p.step("s", lambda e: e.activation(OMR[:, :], MUR[:, :], AF.Copy, bias=1.0, scale=-1.0),
           waits=("pr.mur",), sig="pr.omr")
    p.step("s", lambda e: e.activation(OMK[:, :], MUK[:, :], AF.Copy, bias=1.0, scale=-1.0),
           waits=("pr.muk",), sig="pr.omk")
    p.step("v", lambda e: e.memset(VT4[:, :, :, 64:65], 1.0), waits=(), sig="pr.vones")
    p.step("v", lambda e: e.memset(EPSV[:, :], EPS), waits=(), sig="pr.eps")
    for k in range(NC8):
        dma("g", XT[:, k * 1024:(k + 1) * 1024], xT_d[0, k * 128:(k + 1) * 128, 0:1024],
            waits=(), sig=f"s0.lx0.{k}")

    def build_weff(pi, a_d, b_d, w_d, tag, prev_tag, slot=None):
        aw = (f"{prev_tag}.mm7",) if prev_tag else ()
        ww = (f"{prev_tag}.ev7",) if prev_tag else ()
        dma("g", AT_BF[:, :] if tag != "wp" else AT_BF[0:R, 0:C], a_d, waits=aw, sig=f"{tag}.a")
        dma("g", BT_BF[:, :], b_d, waits=aw, sig=f"{tag}.b")
        dma("g", WRAW[:, :], rows_ap(w_d), waits=ww, sig=f"{tag}.w")
        for k in range(NC8):
            we = PSA[:, (k % 2) * 1024:(k % 2) * 1024 + 1024]

            def emit_mm(e, k=k, we=we):
                inst = None
                if tag != "wp":
                    for h in range(H):
                        inst = e.matmul(we[:, h * 64:h * 64 + 64],
                                        AT_BF[0:R, h * C + k * 128:h * C + k * 128 + 128],
                                        BT_BF[0:R, h * 64:h * 64 + 64],
                                        start=True, stop=True)
                else:
                    for nn in range(2):
                        inst = e.matmul(we[:, nn * 512:(nn + 1) * 512],
                                        AT_BF[0:R, k * 128:k * 128 + 128],
                                        BT_BF[0:R, nn * 512:(nn + 1) * 512],
                                        start=True, stop=True)
                return inst
            p.step("t", emit_mm,
                   waits=(f"{tag}.a", f"{tag}.b",
                          f"{tag}.ev{k-2}" if k >= 2 else
                          (f"{prev_tag}.ev{6+k}" if prev_tag else None)),
                   sig=f"{tag}.mm{k}")
            dst = WSTG if slot is None else slot
            p.step("v", lambda e, k=k, we=we, dst=dst: e.scalar_tensor_tensor(
                dst[:, k * 1024:(k + 1) * 1024], we, 2.0,
                WRAW[:, k * 1024:(k + 1) * 1024], ALU.mult, ALU.add),
                waits=(f"{tag}.mm{k}", f"{tag}.w",
                       (f"{prev_tag}.st{k}" if prev_tag and slot is None else None)),
                sig=f"{tag}.ev{k}")
            dma("y", s_weff[pi, k * 128:(k + 1) * 128, :], dst[:, k * 1024:(k + 1) * 1024],
                waits=(f"{tag}.ev{k}",), sig=f"{tag}.st{k}")

    if lvl >= -2:
        build_weff(0, aq_d, bq_d, wq_d, "weq", None, slot=W[0])
    if lvl >= -1:
        build_weff(1, ak_d, bk_d, wk_d, "wek", "weq", slot=W[1])
        build_weff(2, av_d, bv_d, wv_d, "wev", "wek", slot=W[2])
        build_weff(3, ap_d, bpm_d, wp_d, "wp", "wev")

    prev = "wp"
    for pi, (fd, tag) in enumerate(((fr_d, "fr"), (fk_d, "fk"), (fv_d, "fv")) if lvl >= 0 else ()):
        if prev == "wp":
            ww = (f"{prev}.ev7",) + tuple(f"wp.st{k}" for k in range(NC8))
        else:
            ww = tuple(f"{prev}.fst{k}" for k in range(NC8))
        dma("g", WRAW[:, :], rows_ap(fd), waits=ww, sig=f"{tag}.c")
        for k in range(NC8):
            dma("y", s_ffn[pi, k * 128:(k + 1) * 128, :], WRAW[:, k * 1024:(k + 1) * 1024],
                waits=(f"{tag}.c",), sig=f"{tag}.fst{k}")
        prev = tag

    for s in range(NS):
        if lvl <= 0 or (s > 0 and lvl < 9):
            break
        tg = f"s{s}"
        pg = f"s{s-1}"

        if s == 0:
            p.alias(f"{tg}.vones", "pr.vones")
        else:
            p.step("v", lambda e: e.memset(VT4[:, :, :, 64:65], 1.0),
                   waits=(f"{pg}.d.q3.vo3", f"{pg}.b.h15.pv_last"),
                   sig=f"{tg}.vones", hazard=True)
        if s == 0:
            p.alias(f"{tg}.lw.q", "weq.ev7")
            p.alias(f"{tg}.lw.k", "wek.ev7")
            p.alias(f"{tg}.lw.v", "wev.ev7")
        else:
            dma("y", W[0][:, :], rows_ap(s_weff[0]),
                waits=(f"{pg}.d.q3.rmm3",) + tuple(f"weq.st{k}" for k in range(NC8)),
                sig=f"{tg}.lw.q")
            dma("y", W[1][:, :], rows_ap(s_weff[1]),
                waits=(f"{pg}.d.q3.kmm7",) + tuple(f"wek.st{k}" for k in range(NC8)),
                sig=f"{tg}.lw.k")
            dma("y", W[2][:, :], rows_ap(s_weff[2]),
                waits=(f"{pg}.c.q3.pj3",) + tuple(f"wev.st{k}" for k in range(NC8)),
                sig=f"{tg}.lw.v")


        for th in range(2):
            t0 = th * 1024
            for k in range(NC8):
                if s == 0 and th == 0:
                    continue
                elif th == 0:
                    ww = (f"{pg}.d.q3.ln2aff.i3",) if k == 0 else ()
                else:
                    ww = (f"{tg}.a.v.mm7",) if k == 0 else ()
                dma("g", XT[:, k * 1024:(k + 1) * 1024],
                    xT_d[s, k * 128:(k + 1) * 128, t0:t0 + 1024],
                    waits=ww, sig=f"{tg}.lx{th}.{k}")

            for (buf, wsl, lw, pn) in ((QT, W[0], f"{tg}.lw.q", "q"), (KT, W[1], f"{tg}.lw.k", "k")):
                for m in range(NC8):
                    r = (m + (0 if pn == "q" else 8)) % 4
                    PS = (PSA, PSB)[r // 2][:, (r % 2) * 1024:(r % 2) * 1024 + 1024]
                    pidx = m - 4
                    if pidx >= 0:
                        pev = f"{tg}.a.{pn}.ev{th}.{pidx}"
                    elif pn == "k":
                        pev = f"{tg}.a.q.ev{th}.{pidx+8}"
                    elif th > 0:
                        pev = f"{tg}.a.v.ev{(th-1)*8 + 4 + m}"
                    elif s == 0:
                        pev = "wp.ev7"
                    else:
                        pev = f"{pg}.d.q3.g.i3"
                    extra = []
                    if m == 0 and th == 0 and s > 0:
                        if pn == "q":
                            extra += [f"{pg}.d.q3.s2.i3", f"{pg}.d.q3.sq2.i3"]
                        else:
                            extra += [f"{pg}.d.q3.out3"]

                    def emit(e, m=m, PS=PS, wsl=wsl, t0=t0):
                        inst = None
                        for k in range(NC8):
                            for nn in range(2):
                                inst = e.matmul(
                                    PS[:, nn * 512:(nn + 1) * 512],
                                    wsl[:, k * 1024 + m * 128:k * 1024 + m * 128 + 128],
                                    XT[:, k * 1024 + nn * 512:k * 1024 + (nn + 1) * 512],
                                    start=(k == 0), stop=(k == NC8 - 1))
                        return inst
                    p.step("t", emit,
                           waits=(lw, pev) + tuple(extra) +
                                 tuple(f"{tg}.lx{th}.{k}" for k in range(NC8)),
                           sig=f"{tg}.a.{pn}.mm{th}.{m}")
                    p.step("s", lambda e, m=m, PS=PS, buf=buf, t0=t0: e.copy(
                        buf[:, m * T + t0:m * T + t0 + 1024], PS),
                        waits=(f"{tg}.a.{pn}.mm{th}.{m}",), sig=f"{tg}.a.{pn}.ev{th}.{m}")

            for jj in range(8):
                j = th * 8 + jj
                r = jj % 4
                PS = (PSA, PSB)[r // 2][:, (r % 2) * 1024:(r % 2) * 1024 + 1024]
                pidx = jj - 4
                pev = (f"{tg}.a.v.ev{j-4}" if pidx >= 0 else f"{tg}.a.k.ev{th}.{pidx+8}")

                def emitv(e, j=j, jj=jj, PS=PS):
                    inst = None
                    for k in range(NC8):
                        for nn in range(2):
                            inst = e.matmul(
                                PS[:, nn * 512:(nn + 1) * 512],
                                XT[:, k * 1024 + jj * 128:k * 1024 + jj * 128 + 128],
                                W[2][:, k * 1024 + nn * 512:k * 1024 + (nn + 1) * 512],
                                start=(k == 0), stop=(k == NC8 - 1))
                    return inst
                p.step("t", emitv, waits=(f"{tg}.lw.v", pev), sig=f"{tg}.a.v.mm{j}")
                p.step("v", lambda e, j=j, PS=PS: e.tensor_copy(
                    VT4[:, j, :, 0:64], PS.rearrange("x (h d) -> x h d", h=16)),
                    waits=(f"{tg}.a.v.mm{j}", f"{tg}.vones",
                           f"{pg}.b.h15.pv_last" if s > 0 and j == 0 else None),
                    sig=f"{tg}.a.v.ev{j}")

        if lvl == 1:
            break
        sq_prev = {0: None, 1: None}
        prev_red = None
        sq_i = 0
        for (srcbuf, dstn, nt, evp) in ((QT, QN2, "nq", "q"), (KT, KN2, "nk", "k")):
            for m in range(NC8):
                par = sq_i % 2
                sq_i += 1
                sqb = PT if par == 0 else STG
                extra = None
                if sq_i == 1:
                    extra = f"{pg}.b.h15.rbe1" if s > 0 else None
                elif sq_i == 2:
                    extra = f"{pg}.b.h15.shm1" if s > 0 else None
                p.step("s", lambda e, m=m, srcbuf=srcbuf, sqb=sqb: e.activation(
                    sqb[:, 0:T], srcbuf[:, m * T:(m + 1) * T], AF.Square),
                    waits=(f"{tg}.a.{evp}.ev1.{m}", extra, sq_prev[par]),
                    sig=f"{tg}.{nt}.sq{m}")

                def emitn(e, m=m, sqb=sqb):
                    inst = None
                    for nn in range(4):
                        inst = e.matmul(PSB[0:33, nn * 512:(nn + 1) * 512],
                                        CON2[:, :], sqb[:, nn * 512:(nn + 1) * 512],
                                        start=True, stop=True)
                    return inst
                p.step("t", emitn,
                       waits=(f"{tg}.{nt}.sq{m}", "pr.con2", prev_red,
                              f"{tg}.a.v.ev{NT-1}" if (m == 0 and nt == "nq") else None,
                              f"{tg}.a.v.ev{NT-2}" if (m == 0 and nt == "nq") else None),
                       sig=f"{tg}.{nt}.mm{m}")
                sq_prev[par] = f"{tg}.{nt}.mm{m}"
                p.step("v", lambda e, m=m, dstn=dstn: e.reduce_max(
                    dstn[0:33, m:m + 1], PSB[0:33, :], axis=AX.X),
                    waits=(f"{tg}.{nt}.mm{m}",), sig=f"{tg}.{nt}.red{m}", hazard=True)
                prev_red = f"{tg}.{nt}.red{m}"
        p.step("v", lambda e: e.tensor_add(QN2[0:33, 0:8], QN2[0:33, 0:8], KN2[0:33, 0:8]),
               waits=tuple(f"{tg}.nq.red{m}" for m in range(NC8)) +
                     tuple(f"{tg}.nk.red{m}" for m in range(NC8)), sig=f"{tg}.nb.add")

        def emitbb(e):
            inst = None
            for h in range(H):
                row = (h % 2) * 32
                inst = e.matmul(PSA[:, h:h + 1], CONE[row:row + 1, 0:128],
                                QN2[row:row + 1, h // 2:h // 2 + 1], start=True, stop=True)
            return inst
        p.step("t", emitbb, waits=(f"{tg}.nb.add", f"{tg}.a.v.ev{NT-1}",
                                   f"{tg}.a.v.ev{NT-2}", "pr.cone"),
               sig=f"{tg}.nb.mm")
        p.step("v", lambda e: e.tensor_scalar(BIASV[:, :], PSA[:, 0:16], -0.0625,
                                              None, ALU.mult),
               waits=(f"{tg}.nb.mm",), sig=f"{tg}.nb.bias")

        dma("y", W[0][:, :], rows_ap(s_ffn[0]),
            waits=tuple(f"fr.fst{k}" for k in range(NC8)) +
                  tuple(f"weq.st{k}" for k in range(NC8)) +
                  (f"{tg}.a.q.mm1.7", f"{tg}.nb.mm"), sig=f"{tg}.lw.fr")
        dma("y", W[1][:, :], rows_ap(s_ffn[1]),
            waits=tuple(f"fk.fst{k}" for k in range(NC8)) +
                  tuple(f"wek.st{k}" for k in range(NC8)) +
                  (f"{tg}.a.k.mm1.7",), sig=f"{tg}.lw.fk")
        dma("y", W[2][:, :], rows_ap(s_weff[3]),
            waits=tuple(f"wp.st{k}" for k in range(NC8)) +
                  tuple(f"wev.st{k}" for k in range(NC8)) + (f"{tg}.a.v.mm15",),
            sig=f"{tg}.lw.p0")

        if lvl == 2:
            break
        for h in range(1 if lvl == 3 else (2 if lvl == 35 else H)):
            hof = (h % 2) * 64
            ch = h // 2
            ht = f"{tg}.b.h{h}"
            jq = []
            for j in range(NT):
                qlen = T - j * 128
                subs_j = []
                off = 0
                while off < qlen:
                    n = min(512, qlen - off)
                    subs_j.append((j, j * 128 + off, n))
                    off += n
                jq.extend(subs_j[1:] + subs_j[:1])
            def segments(q0, n):
                a = q0
                out = []
                while a < q0 + n:
                    b = min(q0 + n, (a // 512 + 1) * 512)
                    out.append((a, b))
                    a = b
                return out
            pvf, pvl = {}, {}
            for (j, q0, n) in jq:
                for (a, b) in segments(q0, n):
                    g = a // 512
                    pvf.setdefault(g, (j, q0, a))
                    pvl[g] = (j, q0, a)
            prev_exp = {0: None, 1: None, 2: None, 3: None}
            subs = list(jq)
            NSUB = len(subs)
            mask_of = {}

            def emit_st_step(si):
                j, q0, n = subs[si - 1]
                pp = (si - 1) % 4
                stp = PSA[:, pp * 512:pp * 512 + 512]
                stw = [prev_exp[pp]]
                if serial and h > 0 and si == 1:
                    stw += [f"{tg}.b.h{h-1}.ev0", f"{tg}.b.h{h-1}.ev1",
                            f"{tg}.b.h{h-1}.done"]
                if h == 0 and si <= 4:
                    stw.append(f"{tg}.nb.bias")
                if h > 0 and si <= 4:
                    half = 0 if pp < 2 else 1
                    if (h - 1) % 2 == 1:
                        stw.append(f"{tg}.b.h{h-1}.sme{half}")
                    else:
                        stw.append(f"{tg}.b.h{h-1}.rbe{half}")

                def emit_st(e, j=j, q0=q0, n=n, stp=stp, hof=hof, ch=ch):
                    return e.matmul(
                        stp[:, 0:n],
                        KT[hof:hof + 64, ch * T + j * 128:ch * T + j * 128 + 128],
                        QT[hof:hof + 64, ch * T + q0:ch * T + q0 + n],
                        start=True, stop=True)
                p.step("t", emit_st, waits=tuple(stw), sig=f"{ht}.st.{j}.{q0}")

            def emit_exp_step(si):
                j, q0, n = subs[si - 1]
                pp = (si - 1) % 4
                stp = PSA[:, pp * 512:pp * 512 + 512]
                expw = [f"{ht}.st.{j}.{q0}"]
                if h == 0 and si == 1:
                    expw.append(f"{tg}.nb.bias")
                if h > 0 and si == 1:
                    expw.append(f"{tg}.b.h{h-1}.ev1")
                for back in (1, 2, 3):
                    if si > back:
                        jp, qp, np_ = subs[si - 1 - back]
                        if q0 < qp + np_ and qp < q0 + n:
                            expw.append(f"{ht}.pv.{jp}.{qp}")
                p.step("s", lambda e, q0=q0, n=n, stp=stp, h=h: e.activation(
                    PT[:, q0:q0 + n], stp[:, 0:n], AF.Exp,
                    bias=BIASV[:, h:h + 1], scale=SCALE),
                    waits=tuple(expw), sig=f"{ht}.exp.{j}.{q0}")
                prev_exp[pp] = f"{ht}.exp.{j}.{q0}"
                if q0 == j * 128:
                    p.step("v", lambda e, j=j: e.tensor_mul(
                        PT[:, j * 128:j * 128 + 128], PT[:, j * 128:j * 128 + 128],
                        CMSK[:, :]),
                        waits=(f"{ht}.exp.{j}.{q0}", "pr.cmsk"), sig=f"{ht}.msk.{j}")
                    mask_of[si] = f"{ht}.msk.{j}"

            def emit_pv_step(si):
                j, q0, n = subs[si - 1]

                def emit_pv(e, j=j, q0=q0, n=n, h=h):
                    inst = None
                    for (a, b) in segments(q0, n):
                        g = a // 512
                        inst = e.matmul(
                            PSB[0:65, a:b],
                            VT[:, j * 1040 + h * 65:j * 1040 + h * 65 + 65],
                            PT[:, a:b],
                            start=(pvf[g] == (j, q0, a)), stop=(pvl[g] == (j, q0, a)))
                    return inst
                pvw = [f"{ht}.exp.{j}.{q0}", mask_of.get(si)]
                if si == 1:
                    pvw.append(f"{tg}.a.v.ev{NT-1}")
                    if h > 0:
                        pvw.append(f"{tg}.b.h{h-1}.ev1")
                    else:
                        pvw.append(f"{tg}.nk.red7")
                p.step("t", emit_pv, waits=tuple(pvw), sig=f"{ht}.pv.{j}.{q0}")
                for (a, b) in segments(q0, n):
                    g = a // 512
                    if pvl[g] == (j, q0, a):
                        p.step("v", lambda e, g=g: e.reciprocal(
                            RECF[64:65, 0:512], PSB[64:65, g * 512:(g + 1) * 512]),
                            waits=(f"{ht}.pv.{j}.{q0}",), sig=f"{ht}.rc{g}",
                            hazard=True)
                        p.step("v", lambda e, g=g: e.tensor_copy(
                            PT[64:65, g * 512:(g + 1) * 512], RECF[64:65, 0:512]),
                            waits=(), sig=f"{ht}.rcb{g}")

            for si in range(1, NSUB + 1):
                emit_st_step(si)
                if si >= 4:
                    emit_pv_step(si - 3)
                if si >= 3:
                    emit_exp_step(si - 2)
            emit_pv_step(NSUB - 2)
            emit_exp_step(NSUB - 1)
            emit_pv_step(NSUB - 1)
            emit_exp_step(NSUB)
            emit_pv_step(NSUB)
            jN, qN, _ = subs[NSUB - 1]
            last_pv = f"{ht}.pv.{jN}.{qN}"
            p.alias(f"{ht}.pv_last", last_pv)

            for hh in range(2):
                def emit_rb(e, hh=hh):
                    inst = None
                    for nn in range(2):
                        c0 = hh * 1024 + nn * 512
                        inst = e.matmul(PSA[0:64, c0:c0 + 512],
                                        CONE[64:65, 0:64], PT[64:65, c0:c0 + 512],
                                        start=True, stop=True)
                    return inst
                p.step("t", emit_rb,
                       waits=(f"{ht}.rcb{2*hh}", f"{ht}.rcb{2*hh+1}",
                              prev_exp[2 * hh], prev_exp[2 * hh + 1]),
                       sig=f"{ht}.rb{hh}")
                p.step("v", lambda e, hh=hh: e.tensor_copy(
                    PT[0:64, hh * 1024:(hh + 1) * 1024],
                    PSA[0:64, hh * 1024:(hh + 1) * 1024]),
                    waits=(f"{ht}.rb{hh}",), sig=f"{ht}.rbe{hh}")
                dst = OT if hof == 0 else STG
                dc0 = (ch * T if hof == 0 else 0) + hh * 1024
                evw = [f"{ht}.rbe{hh}"]
                if s == 0 and h == 0 and hh == 0:
                    evw += [f"fv.fst{k}" for k in range(NC8)]
                    evw += [f"wp.st{k}" for k in range(NC8)]
                if hof == 64 and h > 1:
                    evw.append(f"{tg}.b.h{h-2}.shm{hh}")
                p.step("v", lambda e, hh=hh, dst=dst, dc0=dc0: e.tensor_mul(
                    dst[0:64, dc0:dc0 + 1024],
                    PSB[0:64, hh * 1024:(hh + 1) * 1024],
                    PT[0:64, hh * 1024:(hh + 1) * 1024]),
                    waits=tuple(evw), sig=f"{ht}.ev{hh}")
            p.alias(f"{ht}.rbe", f"{ht}.rbe1")
            p.alias(f"{ht}.ev", f"{ht}.ev1")
            if hof == 0:
                p.alias(f"{ht}.done", f"{ht}.ev1")
            else:
                for hh in range(2):
                    def emit_sh(e, hh=hh):
                        inst = None
                        for nn in range(2):
                            c0 = hh * 1024 + nn * 512
                            inst = e.matmul(PSA[64:128, c0:c0 + 512],
                                            CIDB[0:64, 0:64], STG[0:64, c0:c0 + 512],
                                            start=True, stop=True)
                        return inst
                    p.step("t", emit_sh,
                           waits=(f"{ht}.ev{hh}", "pr.cidb", f"{ht}.rbe{hh}"),
                           sig=f"{ht}.shm{hh}")
                    p.step("v", lambda e, hh=hh, ch=ch: e.tensor_copy(
                        OT[64:128, ch * T + hh * 1024:ch * T + (hh + 1) * 1024],
                        PSA[64:128, hh * 1024:(hh + 1) * 1024]),
                        waits=(f"{ht}.shm{hh}",), sig=f"{ht}.sme{hh}")
                p.alias(f"{ht}.shm", f"{ht}.shm1")
                p.alias(f"{ht}.sme", f"{ht}.sme1")
                p.alias(f"{ht}.done", f"{ht}.sme1")

        if lvl in (3, 35, 4):
            break
        for q in range(NQ if lvl != 5 else 1):
            qt = f"{tg}.c.q{q}"
            dt_ = f"{tg}.d.q{q}"
            if q == 0:
                dma("y", WV[:, :], rows_ap(s_ffn[2]),
                    waits=tuple(f"fv.fst{k}" for k in range(NC8)) +
                          (f"{tg}.b.h15.pv_last",), sig=f"{tg}.lw.fv")
            for i4 in range(4):
                i = q * 4 + i4
                pp = i4 % 2
                xin_prev = (f"{qt}.s1.i{i4-2}" if i4 >= 2 else
                            (f"{tg}.c.q{q-1}.s1.i{i4+2}" if q >= 1 else
                             (f"{pg}.c.q3.s1.i{i4+2}" if s > 0 else None)))
                kt_free = f"{tg}.b.h15.pv_last" if (q == 0 and i4 < 2) else None
                dma("y", XIN[:, pp * 1024:(pp + 1) * 1024], x_d[s, i * 128:(i + 1) * 128, :],
                    waits=(xin_prev, kt_free), sig=f"{qt}.xin{i4}")
                PJ = PSA[:, (i4 % 2) * 1024:(i4 % 2) * 1024 + 1024]
                prev_use = (f"{qt}.s1.i{i4-2}" if i4 >= 2 else
                            (f"{tg}.c.q{q-1}.s1.i{i4+2}" if q >= 1 else None))

                def emit_pj(e, i=i, PJ=PJ):
                    for k in range(NC8):
                        for nn in range(2):
                            e.matmul(PJ[:, nn * 512:(nn + 1) * 512],
                                     OT[:, k * T + i * 128:k * T + i * 128 + 128],
                                     W[2][:, k * 1024 + nn * 512:k * 1024 + (nn + 1) * 512],
                                     start=(k == 0), stop=False)
                    inst = None
                    for nn in range(2):
                        inst = e.matmul(PJ[:, nn * 512:(nn + 1) * 512],
                                        CONE[0:1, 0:128], BPB[0:1, nn * 512:(nn + 1) * 512],
                                        start=False, stop=True)
                    return inst
                pjw = [f"{tg}.lw.p0", "pr.bpb", prev_use]
                if q == 0 and i4 == 0:
                    pjw += [f"{tg}.b.h15.sme1", f"{tg}.b.h14.ev1"]
                p.step("t", emit_pj, waits=tuple(pjw), sig=f"{qt}.pj{i4}")
                p.step("v", lambda e, i4=i4, PJ=PJ: e.scalar_tensor_tensor(
                    LNIN[:, i4 * 1024:(i4 + 1) * 1024], PJ, 1.0, XIN[:, (i4 % 2) * 1024:(i4 % 2 + 1) * 1024],
                    ALU.mult, ALU.add, accum_out=STAT[:, i4:i4 + 1]),
                    waits=(f"{qt}.pj{i4}", f"{qt}.xin{i4}",
                           f"{tg}.d.q{q-1}.sq2.i{i4}" if q >= 1 else
                           (f"{pg}.d.q3.sq2.i{i4}" if s > 0 else None),
                           f"{tg}.lx1.7" if q == 0 and i4 == 0 else None),
                    sig=f"{qt}.s1.i{i4}", hazard=True)
                p.step("s", lambda e, i4=i4: e.activation(
                    XRQ[:, 0:1024], LNIN[:, i4 * 1024:(i4 + 1) * 1024], AF.Square,
                    accum_out=STAT[:, 4 + i4:5 + i4]),
                    waits=(f"{qt}.s1.i{i4}",
                           f"{tg}.d.q{q-1}.rmm3" if q >= 1 and i4 == 0 else None),
                    sig=f"{qt}.sq.i{i4}")
            p.step("v", lambda e: e.tensor_scalar(STAT[:, 16:20], STAT[:, 0:4],
                                                  1.0 / C, None, ALU.mult),
                   waits=(f"{qt}.s1.i3",), sig=f"{qt}.mean", hazard=True)
            p.step("v", lambda e: e.tensor_mul(STAT[:, 20:24], STAT[:, 16:20],
                                               STAT[:, 16:20]), waits=(f"{qt}.mean",), sig=f"{qt}.m2", hazard=True)
            p.step("v", lambda e: e.scalar_tensor_tensor(
                STAT[:, 24:28], STAT[:, 4:8], 1.0 / C, STAT[:, 20:24],
                ALU.mult, ALU.subtract),
                waits=(f"{qt}.sq.i3", f"{qt}.m2"), sig=f"{qt}.var", hazard=True)
            p.step("s", lambda e: e.activation(STAT[:, 28:32], STAT[:, 24:28],
                                               AF.Sqrt, bias=EPSV[:, 0:1]),
                   waits=(f"{qt}.var", "pr.eps"), sig=f"{qt}.lnv")
            p.step("v", lambda e: e.reciprocal(STAT[:, 8:12], STAT[:, 28:32]),
                   waits=(f"{qt}.lnv",), sig=f"{qt}.rstd", hazard=True)
            p.step("v", lambda e: e.tensor_mul(STAT[:, 12:16], STAT[:, 16:20],
                                               STAT[:, 8:12]),
                   waits=(f"{qt}.rstd",), sig=f"{qt}.mr", hazard=True)
            for i4 in range(4):
                p.step("v", lambda e, i4=i4: e.tensor_scalar(
                    X1Q[:, i4 * 1024:(i4 + 1) * 1024], LNIN[:, i4 * 1024:(i4 + 1) * 1024],
                    STAT[:, 8 + i4:9 + i4], STAT[:, 12 + i4:13 + i4],
                    ALU.mult, ALU.subtract),
                    waits=(f"{qt}.mr",
                           f"{tg}.d.q{q-1}.s2.i{i4}" if q >= 1 else
                           (f"{pg}.d.q3.s2.i{i4}" if s > 0 else None)),
                    sig=f"{qt}.x1.i{i4}", hazard=True)
            if q == 0:
                p.step("v", lambda e: e.memset(X1T3[:, :, 0:1], 0.0),
                       waits=(), sig=f"{qt}.pad")
            else:
                p.step("v", lambda e: e.tensor_copy(X1T3[:, :, 0:1], BNDRY[:, 0:8]),
                       waits=(f"{tg}.c.q{q-1}.bnd",), sig=f"{qt}.pad")
            for k in range(NC8):
                ppk = k % 4

                def emit_tp(e, k=k, ppk=ppk, i0=0, i1=4):
                    inst = None
                    for i4 in range(i0, i1):
                        inst = e.transpose(
                            PSB[:, ppk * 512 + i4 * 128:ppk * 512 + (i4 + 1) * 128],
                            X1Q[:, i4 * 1024 + k * 128:i4 * 1024 + k * 128 + 128],
                            CIDF[:, :])
                    return inst
                tpw = ["pr.cid", f"{qt}.x1.i1",
                       f"{qt}.tpe.k{k-4}" if k >= 4 else None]
                if k <= 3:
                    if q == 0 and s == 0:
                        tpw.append(None)
                    else:
                        tpw.append(f"{tg}.c.q{q-1}.tpe.k{4+k}" if q >= 1 else
                                   f"{pg}.c.q3.tpe.k{4+k}")
                    if q == 0:
                        tpw.append(f"{tg}.b.h15.ev1")
                from functools import partial
                p.step("t", partial(emit_tp, i0=0, i1=2),
                       waits=tuple(w for w in tpw if w), sig=f"{qt}.tpa.k{k}")
                p.step("t", partial(emit_tp, i0=2, i1=4),
                       waits=(f"{qt}.x1.i3",), sig=f"{qt}.tp.k{k}")
                p.step("s", lambda e, k=k, ppk=ppk: e.copy(
                    X1T[:, k * 513 + 1:k * 513 + 513],
                    PSB[:, ppk * 512:(ppk + 1) * 512]),
                    waits=(f"{qt}.tp.k{k}", f"{qt}.pad",
                           f"{tg}.d.q{q-1}.mixk{k}" if q >= 1 else None),
                    sig=f"{qt}.tpe.k{k}")
            for k in range(NC8):
                p.step("v", lambda e, k=k: e.tensor_sub(
                    MIXT[:, :], X1T[:, k * 513 + 1:k * 513 + 513],
                    X1T[:, k * 513:k * 513 + 512]),
                    waits=(f"{qt}.tpe.k{k}",), sig=f"{dt_}.mt{k}")
                p.step("v", lambda e, k=k: e.scalar_tensor_tensor(
                    XRQ[:, k * 512:(k + 1) * 512], MIXT[:, :],
                    MUR[:, k:k + 1], X1T[:, k * 513:k * 513 + 512],
                    ALU.mult, ALU.add),
                    waits=("pr.mur", f"{dt_}.mt{k}",
                           f"{qt}.sq.i3" if k == 0 else None,
                           f"{tg}.d.q{q-1}.rmm3" if q >= 1 and k == 0 else
                           (f"{pg}.d.q3.rmm3" if s > 0 and q == 0 and k == 0 else None)),
                    sig=f"{dt_}.mixr{k}")
                p.step("v", lambda e, k=k: e.scalar_tensor_tensor(
                    XKQ[:, k * 512:(k + 1) * 512], MIXT[:, :],
                    MUK[:, k:k + 1], X1T[:, k * 513:k * 513 + 512],
                    ALU.mult, ALU.add),
                    waits=("pr.muk",
                           f"{tg}.d.q{q-1}.kmm7" if q >= 1 and k == 0 else
                           (f"{pg}.d.q3.kmm7" if s > 0 and q == 0 and k == 0 else None)),
                    sig=f"{dt_}.mixk{k}")
            p.step("v", lambda e: e.tensor_copy(BNDRY[:, 0:8], X1T3[:, :, 512:513]),
                   waits=(f"{qt}.tpe.k7",), sig=f"{qt}.bnd")
            for i4 in range(4):
                RP = PSA[:, (i4 % 2) * 1024:(i4 % 2) * 1024 + 1024]
                prev_use = f"{dt_}.sig{i4-2}" if i4 >= 2 else f"{qt}.s1.i{i4+2}"

                def emit_r(e, i4=i4, RP=RP, k0=0, k1=NC8):
                    inst = None
                    for k in range(k0, k1):
                        for nn in range(2):
                            inst = e.matmul(
                                RP[:, nn * 512:(nn + 1) * 512],
                                XRQ[:, k * 512 + i4 * 128:k * 512 + i4 * 128 + 128],
                                W[0][:, k * 1024 + nn * 512:k * 1024 + (nn + 1) * 512],
                                start=(k == 0), stop=(k == NC8 - 1))
                    return inst
                if i4 == 0:
                    from functools import partial
                    p.step("t", partial(emit_r, k0=0, k1=4),
                           waits=(f"{tg}.lw.fr", prev_use) +
                           tuple(f"{dt_}.mixr{k}" for k in range(4)),
                           sig=f"{dt_}.rmmA")
                    p.step("t", partial(emit_r, k0=4, k1=NC8),
                           waits=tuple(f"{dt_}.mixr{k}" for k in range(4, NC8)),
                           sig=f"{dt_}.rmm{i4}")
                else:
                    p.step("t", emit_r, waits=(f"{tg}.lw.fr", prev_use) +
                           tuple(f"{dt_}.mixr{k}" for k in range(NC8)),
                           sig=f"{dt_}.rmm{i4}")
                p.step("s", lambda e, i4=i4, RP=RP: e.activation(
                    THQ[:, i4 * 1024:(i4 + 1) * 1024], RP, AF.Sigmoid),
                    waits=(f"{dt_}.rmm{i4}",
                           f"{tg}.b.h15.sme1" if q == 0 and i4 == 0 else None,
                           f"{tg}.d.q{q-1}.g.i{i4}" if q >= 1 else
                           (f"{pg}.d.q3.g.i{i4}" if s > 0 else None)),
                    sig=f"{dt_}.sig{i4}")
            for m in range(NC8):
                KP = PSB[:, (m % 2) * 512:(m % 2) * 512 + 512]
                prev_use = (f"{dt_}.krelu{m-2}" if m >= 2 else f"{qt}.tpe.k{4+m}")

                def emit_k(e, m=m, KP=KP):
                    inst = None
                    for k in range(NC8):
                        inst = e.matmul(
                            KP, W[1][:, k * 1024 + m * 128:k * 1024 + m * 128 + 128],
                            XKQ[:, k * 512:(k + 1) * 512],
                            start=(k == 0), stop=(k == NC8 - 1))
                    return inst
                p.step("t", emit_k, waits=(f"{tg}.lw.fk", prev_use) +
                       tuple(f"{dt_}.mixk{k}" for k in range(NC8)),
                       sig=f"{dt_}.kmm{m}")
                p.step("s", lambda e, m=m, KP=KP: e.activation(
                    KAQ[:, m * 512:(m + 1) * 512], KP, AF.Relu),
                    waits=(f"{dt_}.kmm{m}",
                           f"{tg}.d.q{q-1}.vo3" if q >= 1 and m == 0 else
                           (f"{pg}.d.q3.vo3" if s > 0 and q == 0 and m == 0 else None)),
                    sig=f"{dt_}.krelu{m}")
                p.step("s", lambda e, m=m: e.activation(
                    KAQ[:, m * 512:(m + 1) * 512], KAQ[:, m * 512:(m + 1) * 512],
                    AF.Square),
                    waits=(f"{dt_}.krelu{m}",), sig=f"{dt_}.ksq{m}")
            for i4 in range(4):
                VP = PSA[:, (i4 % 2) * 1024:(i4 % 2) * 1024 + 1024]
                prev_use = f"{dt_}.g.i{i4-2}" if i4 >= 2 else f"{dt_}.sig{i4+2}"

                def emit_vo(e, i4=i4, VP=VP):
                    inst = None
                    for k in range(NC8):
                        for nn in range(2):
                            inst = e.matmul(
                                VP[:, nn * 512:(nn + 1) * 512],
                                KAQ[:, k * 512 + i4 * 128:k * 512 + i4 * 128 + 128],
                                WV[:, k * 1024 + nn * 512:k * 1024 + (nn + 1) * 512],
                                start=(k == 0), stop=(k == NC8 - 1))
                    return inst
                p.step("t", emit_vo, waits=(f"{tg}.lw.fv", prev_use) +
                       tuple(f"{dt_}.ksq{k}" for k in range(NC8)),
                       sig=f"{dt_}.vo{i4}")
                p.step("v", lambda e, i4=i4, VP=VP: e.tensor_mul(
                    THQ[:, i4 * 1024:(i4 + 1) * 1024], THQ[:, i4 * 1024:(i4 + 1) * 1024],
                    VP), waits=(f"{dt_}.vo{i4}",), sig=f"{dt_}.g.i{i4}", hazard=True)
                p.step("v", lambda e, i4=i4: e.scalar_tensor_tensor(
                    LNIN[:, i4 * 1024:(i4 + 1) * 1024], THQ[:, i4 * 1024:(i4 + 1) * 1024],
                    1.0, X1Q[:, i4 * 1024:(i4 + 1) * 1024], ALU.mult, ALU.add,
                    accum_out=STAT[:, 32 + i4:33 + i4]),
                    waits=(f"{dt_}.g.i{i4}",), sig=f"{dt_}.s2.i{i4}", hazard=True)
                p.step("s", lambda e, i4=i4: e.activation(
                    XRQ[:, 0:1024], LNIN[:, i4 * 1024:(i4 + 1) * 1024], AF.Square,
                    accum_out=STAT[:, 36 + i4:37 + i4]),
                    waits=(f"{dt_}.s2.i{i4}", f"{dt_}.rmm3" if i4 == 0 else None),
                    sig=f"{dt_}.sq2.i{i4}")
            p.step("v", lambda e: e.tensor_scalar(STAT[:, 40:44], STAT[:, 32:36],
                                                  1.0 / C, None, ALU.mult),
                   waits=(f"{dt_}.s2.i3",), sig=f"{dt_}.mean2", hazard=True)
            p.step("v", lambda e: e.tensor_mul(STAT[:, 44:48], STAT[:, 40:44],
                                               STAT[:, 40:44]), waits=(f"{dt_}.mean2",), sig=f"{dt_}.m22", hazard=True)
            p.step("v", lambda e: e.scalar_tensor_tensor(
                STAT[:, 48:52], STAT[:, 36:40], 1.0 / C, STAT[:, 44:48],
                ALU.mult, ALU.subtract),
                waits=(f"{dt_}.sq2.i3", f"{dt_}.m22"), sig=f"{dt_}.var2", hazard=True)
            p.step("s", lambda e: e.activation(STAT[:, 52:56], STAT[:, 48:52],
                                               AF.Sqrt, bias=EPSV[:, 0:1]),
                   waits=(f"{dt_}.var2", "pr.eps"), sig=f"{dt_}.lnv2")
            p.step("v", lambda e: e.reciprocal(STAT[:, 56:60], STAT[:, 52:56]),
                   waits=(f"{dt_}.lnv2",), sig=f"{dt_}.rstd2", hazard=True)
            p.step("v", lambda e: e.tensor_mul(STAT[:, 60:64], STAT[:, 40:44],
                                               STAT[:, 56:60]),
                   waits=(f"{dt_}.rstd2",), sig=f"{dt_}.mr2", hazard=True)
            for i4 in range(4):
                i = q * 4 + i4
                pp = i4 % 2
                prev_dma = (f"{dt_}.out{i4-2}" if i4 >= 2 else
                            (f"{tg}.d.q{q-1}.out{i4+2}" if q >= 1 else
                             (f"{pg}.d.q3.out{i4+2}" if s > 0 else None)))
                p.step("v", lambda e, i4=i4, pp=pp: e.tensor_scalar(
                    OUTQ[:, pp * 1024:(pp + 1) * 1024], LNIN[:, i4 * 1024:(i4 + 1) * 1024],
                    STAT[:, 56 + i4:57 + i4], STAT[:, 60 + i4:61 + i4],
                    ALU.mult, ALU.subtract),
                    waits=(f"{dt_}.mr2", prev_dma), sig=f"{dt_}.ln2aff.i{i4}")
                dma("y", y_d[s, i * 128:(i + 1) * 128, :], OUTQ[:, pp * 1024:(pp + 1) * 1024],
                    waits=(f"{dt_}.ln2aff.i{i4}",), sig=f"{dt_}.out{i4}")

    DSQ = int(os.environ.get('MK_DBG_SEQ', '0'))
    if "dbg_qt" in dbg_d and f"s{DSQ}.a.q.ev1.7" in p.events:
        dma("g", dbg_d["dbg_qt"], QT[:, :], waits=(f"s{DSQ}.a.q.ev1.7",), sig="dbg.qt")
    if "dbg_kt" in dbg_d and f"s{DSQ}.a.k.ev1.7" in p.events:
        dma("g", dbg_d["dbg_kt"], KT[:, :], waits=(f"s{DSQ}.a.k.ev1.7",), sig="dbg.kt")
    if "dbg_vt" in dbg_d and f"s{DSQ}.a.v.ev{NT-1}" in p.events:
        dma("g", dbg_d["dbg_vt"], VT[:, :], waits=(f"s{DSQ}.a.v.ev{NT-1}",), sig="dbg.vt")
    if "dbg_ot" in dbg_d:
        lastd = [f"s{DSQ}.b.h{hh_}.done" for hh_ in range(H) if f"s{DSQ}.b.h{hh_}.done" in p.events]
        lastv = [f"s{DSQ}.b.h{hh_}.ev1" for hh_ in range(H) if f"s{DSQ}.b.h{hh_}.ev1" in p.events]
        if lastd:
            dma("g", dbg_d["dbg_ot"], OT[:, :],
                waits=(lastd[-1], lastv[-1]), sig="dbg.ot")
    if "dbg_x1" in dbg_d:
        for q in range(NQ):
            for i4 in range(4):
                i = q * 4 + i4
                if f"s{DSQ}.c.q{q}.x1.i{i4}" not in p.events:
                    continue
                dma("g", dbg_d["dbg_x1"][i * 128:(i + 1) * 128, :],
                    X1Q[:, i4 * 1024:(i4 + 1) * 1024],
                    waits=(f"s{DSQ}.c.q{q}.x1.i{i4}",), sig=f"dbg.x1.{i}")
    if "dbg_s1" in dbg_d and f"s{DSQ}.c.q0.sq.i3" in p.events:
        dma("g", dbg_d["dbg_s1"], LNIN[:, :], waits=(f"s{DSQ}.c.q0.sq.i3",), sig="dbg.s1")
    if "dbg_stat" in dbg_d and f"s{DSQ}.c.q0.mr" in p.events:
        dma("g", dbg_d["dbg_stat"], STAT[:, :], waits=(f"s{DSQ}.c.q0.mr",), sig="dbg.stat")
    if "dbg_x1t" in dbg_d and f"s{DSQ}.c.q0.bnd" in p.events:
        dma("g", dbg_d["dbg_x1t"], X1T[:, :], waits=(f"s{DSQ}.c.q0.bnd",), sig="dbg.x1t")
    if "dbg_qn" in dbg_d and f"s{DSQ}.nb.add" in p.events:
        dma("g", dbg_d["dbg_qn"], QN2[:, :], waits=(f"s{DSQ}.nb.add",), sig="dbg.qn")
        dma("g", dbg_d["dbg_kn"], KN2[:, :], waits=(f"s{DSQ}.nb.add",), sig="dbg.kn")
    if "dbg_bias" in dbg_d and f"s{DSQ}.nb.bias" in p.events:
        dma("y", dbg_d["dbg_bias"], BIASV[:, :], waits=(f"s{DSQ}.nb.bias",), sig="dbg.bias")

    import contextlib
    stack = contextlib.ExitStack()
    sems = {name: stack.enter_context(nc.semaphore(f"sem_{name}"))
            for name in ("t", "v", "s")}
    for ln in p.lane_names():
        sems[ln] = stack.enter_context(nc.semaphore(f"sem_{ln}"))
    finals = dict(p.lane_count)

    def final_wait(e):
        for ln, v in finals.items():
            if v:
                e.wait_ge(sems[ln], v)
        return None
    p.script["y"].append(([], final_wait, False, None))
    with nc.Block() as block:
        p.materialize(block, sems)
    stack.close()
    return nc, sorted(dbg_d.keys())


def _prep_core(inp, core):
    b = [2 * core, 2 * core + 1]
    x = np.asarray(inp["L0_attn_x"])
    f32 = np.float32
    bf = ml_dtypes.bfloat16
    d = {}
    d["xT"] = np.ascontiguousarray(x[b].transpose(0, 2, 1)).astype(f32)
    d["x_tok"] = np.ascontiguousarray(x[b]).astype(f32)
    for nm, key in (("wq_cat", "L0_attn_Wq"), ("wk_cat", "L0_attn_Wk"),
                    ("wv_cat", "L0_attn_Wv")):
        Wm = np.asarray(inp[key])
        d[nm] = np.ascontiguousarray(Wm.transpose(1, 0, 2).reshape(C, C)).astype(f32)
    for nm, key in (("aq_t", "L0_attn_Aq"), ("ak_t", "L0_attn_Ak"), ("av_t", "L0_attn_Av")):
        A = np.asarray(inp[key])
        d[nm] = np.ascontiguousarray(A.transpose(2, 0, 1).reshape(R, H * C)).astype(f32)
    for nm, key in (("bq_r", "L0_attn_Bq"), ("bk_r", "L0_attn_Bk"), ("bv_r", "L0_attn_Bv")):
        B = np.asarray(inp[key])
        d[nm] = np.ascontiguousarray(B.transpose(1, 0, 2).reshape(R, H * HD)).astype(f32)
    d["wp"] = np.ascontiguousarray(np.asarray(inp["L0_attn_Wp"])).astype(f32)
    d["ap_t"] = np.ascontiguousarray(np.asarray(inp["L0_attn_Ap"]).T).astype(f32)
    d["bp_m"] = np.ascontiguousarray(np.asarray(inp["L0_attn_Bp"])).astype(f32)
    d["bp_b"] = np.ascontiguousarray(np.asarray(inp["L0_attn_bp"]).reshape(1, C)).astype(f32)
    d["fr_t"] = np.ascontiguousarray(np.asarray(inp["L0_ffn_W_r"]).T).astype(f32)
    d["fk_t"] = np.ascontiguousarray(np.asarray(inp["L0_ffn_W_k"]).T).astype(f32)
    d["fv_t"] = np.ascontiguousarray(np.asarray(inp["L0_ffn_W_v"]).T).astype(f32)
    d["mu_r_c"] = np.ascontiguousarray(np.asarray(inp["L0_ffn_mu_r"]).reshape(8, 128).T).astype(f32)
    d["mu_k_c"] = np.ascontiguousarray(np.asarray(inp["L0_ffn_mu_k"]).reshape(8, 128).T).astype(f32)
    d["c_idf"] = np.eye(128, dtype=f32)
    d["c_idb"] = np.eye(128, dtype=bf)
    sg, qg = np.meshgrid(np.arange(128), np.arange(128), indexing="ij")
    d["c_mask"] = (qg >= sg).astype(bf)
    d["c_ones"] = np.ones((128, 128), dtype=bf)
    c2 = np.zeros((128, 33), dtype=bf)
    c2[0:64, 0] = 1
    c2[64:128, 32] = 1
    d["c_ones2"] = c2
    return d


_CACHE = {}


def kernel(**inputs):
    dbg = inputs.pop("_dbg", ())
    stop = inputs.pop("_stop", "full")
    key = (tuple(sorted(dbg)), stop)
    if key not in _CACHE:
        _CACHE[key] = build_program(dbg, stop)
    nc, dbg_names = _CACHE[key]
    in_maps = [_prep_core(inputs, c) for c in range(8)]
    res = run_bass_kernel_spmd(nc, in_maps, core_ids=list(range(8)))
    out = np.empty((16, T, C), dtype=np.float32)
    for c in range(8):
        out[2 * c:2 * c + 2] = res.results[c]["y"]
    if dbg:
        return out, {n: res.results[0][n] for n in dbg_names}
    return out
